# version 01
# baseline (speedup 1.0000x reference)
import numpy as np
import ml_dtypes
import contextlib
import concourse.bass as bass
import concourse.mybir as mybir
from concourse.bass_utils import run_bass_kernel_spmd

F32 = mybir.dt.float32
BF16 = mybir.dt.bfloat16
ALU = mybir.AluOpType
ACT = mybir.ActivationFunctionType
AX = mybir.AxisListType

C = 2048
CCN = 16
H = 5632
HHN = 44
WIN = 128
T_OWN = 2048
T_KV = T_OWN + WIN
NBLK = T_KV // 128
EPS_LN = 1e-5
EPS_Q = 1e-8
NW = 5632.0 * 2048.0

SC = [(0, 1024), (1024, 1152)]
SCW = 1152
KV_TT = [(0, 512), (512, 512), (1024, 512), (1536, 512), (2048, 128)]
R_TT = [(WIN, 512), (WIN + 512, 512), (WIN + 1024, 512), (WIN + 1536, 512)]
TT5 = 512
NTT5 = T_OWN // TT5
NTB5 = TT5 // 128
DT5 = 512
NDT5 = C // DT5


def build_nc():
    import os as _os
    _norace = _os.environ.get("ATHENA_SIM_NORACE", "") == "1"
    nc = bass.Bass("TRN2", target_bir_lowering=False,
                   detect_race_conditions=not _norace)

    x_d = nc.dram_tensor("x", [T_KV, C], F32, kind="ExternalInput").ap()
    wrvk_d = nc.dram_tensor("wrvk", [CCN, 128, 3 * CCN * 128], F32, kind="ExternalInput").ap()
    w12_d = nc.dram_tensor("w12", [HHN, 2, 128, C], F32, kind="ExternalInput").ap()
    w3t_d = nc.dram_tensor("w3t", [NDT5, HHN, 128, DT5], F32, kind="ExternalInput").ap()
    wout_d = nc.dram_tensor("wout", [NDT5, 128, CCN * DT5], F32, kind="ExternalInput").ap()
    td_d = nc.dram_tensor("td", [128, CCN], F32, kind="ExternalInput").ap()
    tf_d = nc.dram_tensor("tf", [128, CCN], F32, kind="ExternalInput").ap()
    mask_d = nc.dram_tensor("mask", [128, 1], F32, kind="ExternalInput").ap()
    eye_d = nc.dram_tensor("eye", [128, 128], BF16, kind="ExternalInput").ap()
    out_d = nc.dram_tensor("out", [T_OWN, C], F32, kind="ExternalOutput").ap()
    ysp_d = nc.dram_tensor("ysp", [128, CCN * T_OWN], BF16, kind="Internal").ap()
    hidsp_d = nc.dram_tensor("hidsp", [128, HHN * T_OWN], BF16, kind="Internal").ap()
    q3sp_d = nc.dram_tensor("q3sp", [128, NDT5 * HHN * DT5], BF16, kind="Internal").ap()

    es = contextlib.ExitStack()

    SB_BASE = (nc._sbuf_addr_for_side("left") + 63) // 64 * 64
    SB_TOP = nc._sbuf_addr_for_side("right")
    off = [SB_BASE]

    def sb(name, shape, dtype, at=None):
        nbytes = int(np.prod(shape[1:])) * (2 if dtype == BF16 else 4)
        if at is None:
            at = off[0]
            off[0] = at + ((nbytes + 63) // 64) * 64
        return nc.alloc_sbuf_tensor_at(name, shape, dtype, offset=at), at

    def sba(name, shape, dtype, at=None):
        return sb(name, shape, dtype, at)[0]

    consts = sba("consts", [128, 16], F32)
    dch = sba("dch", [128, CCN], F32)
    euch = sba("euch", [128, CCN], F32)
    tdt = sba("tdt", [128, CCN], F32)
    etd = sba("etd", [128, CCN], F32)
    tft = sba("tft", [128, CCN], F32)
    eye_t = sba("eye", [128, 128], BF16)
    sum_s = sba("sum_s", [128, NBLK], F32)
    sq_s = sba("sq_s", [128, NBLK], F32)
    mean_s = sba("mean_s", [128, NBLK], F32)
    m2_s = sba("m2_s", [128, NBLK], F32)
    var_s = sba("var_s", [128, NBLK], F32)
    std_s = sba("std_s", [128, NBLK], F32)
    rstd_s = sba("rstd_s", [128, NBLK], F32)
    gacc = sba("gacc", [128, 144], F32)
    gred = sba("gred", [128, 8], F32)
    g1 = sba("g1", [1, 8], F32)
    canary = sba("canary", [128, 8], F32)
    ones_t = sba("ones_t", [128, 128], F32)
    sgn3 = sba("sgn3", [128, DT5], BF16)
    q3st = sba("q3st", [128, 2 * DT5], BF16)
    w3qb = sba("w3qb", [128, 2 * DT5], F32)
    tanh4 = sba("tanh4", [128, 2 * 512], F32)
    hidum, hid_at = sb("hidum", [128, C], F32)
    hidb = sba("hidb", [128, 2 * C], BF16, at=hid_at)
    gdum = hidum
    xsh = sba("xsh", [128, 2 * C], F32)
    big0 = sba("big0", [128, CCN * T_KV], BF16)
    union_at = off[0]
    wkb = sba("wkb", [128, 2 * 3 * CCN * 128], BF16)
    ekt, ekt_at = sb("ekt", [128, 2 * SCW], F32)
    vt = sba("vt", [128, 2 * SCW], F32)
    pbuf = sba("pbuf", [128, 2 * (SCW + 1)], F32)
    qbuf = sba("qbuf", [128, 2 * (SCW + 1)], F32)
    dte = sba("dte", [128, SCW], F32)
    ybf = sba("ybf", [128, 2 * SCW], BF16)
    trt = sba("trt", [128, T_OWN], F32)
    sgn = sba("sgn", [128, C], BF16)
    u_t = sba("u_t", [128, C], BF16)
    thr = sba("thr", [128, 8], F32)
    gapt = sba("gapt", [128, 512], F32)
    end_at = off[0]
    qb = sba("qb", [128, 4 * C], BF16, at=ekt_at)
    xntok = sba("xntok", [128, 2 * C], BF16, at=ekt_at)
    qdt = sba("qdt", [128, (HHN + CCN) * DT5], BF16, at=union_at)
    xtok5 = sba("xtok5", [128, 2 * C], F32, at=union_at + (HHN + CCN) * DT5 * 2)
    outsb = sba("outsb", [128, 4 * 512], F32, at=union_at + (HHN + CCN) * DT5 * 2 + 2 * C * 4)
    assert union_at + (HHN + CCN) * DT5 * 2 + 2 * C * 4 + 4 * 512 * 4 <= end_at + 64, "P5 region"
    assert end_at <= SB_TOP, f"sbuf overflow: {end_at} > {SB_TOP}"

    NMB = 6
    mmb = [es.enter_context(nc.psum_tensor(f"mmb{i}", [128, 512], F32)) for i in range(NMB)]
    ps_tr = [es.enter_context(nc.psum_tensor("ps_tr0", [128, 1024], BF16)),
             es.enter_context(nc.psum_tensor("ps_tr1", [128, 1024], BF16))]

    sem_names = ["sx", "sxA", "sxB", "swgA", "swgB", "sw5", "sws", "sscA", "sscB",
                 "smm", "str", "sevv", "sevs", "sv", "ss",
                 "sgy0", "sgy1", "sgh0", "sgh1", "sgq0", "sgq1",
                 "spx0", "spx1", "spx2", "spx3",
                 "sst0", "sst1", "sst2", "sst3", "sp5", "sth"]
    sems = {n: es.enter_context(nc.semaphore(n)) for n in sem_names}
    cnt = {n: 0 for n in sem_names}
    OPS = {e: [] for e in ["sync", "scalar", "vector", "tensor", "gpsimd"]}

    def emit(eng, fn):
        OPS[eng].append(fn)

    def op_inc(eng, name, by, fn):
        cnt[name] += by
        v = cnt[name]
        sem = sems[name]
        emit(eng, lambda e, fn=fn, sem=sem, by=by: fn(e).then_inc(sem, by))
        return v

    def wait(eng, name, v):
        if v <= 0:
            return
        sem = sems[name]
        emit(eng, lambda e, sem=sem, v=v: e.wait_ge(sem, v))

    def vgap():
        emit("vector", lambda v: v.memset(gapt[:, 0:256], 0.0))

    def sgap():
        emit("scalar", lambda s: s.memzero(gapt[:, 256:512]))

    bank_free = [("sevv", 0)] * NMB
    bank_i = [0]

    def next_bank():
        b = bank_i[0] % NMB
        bank_i[0] += 1
        return b

    ST = {}

    def p0():
        op_inc("sync", "sx", 16, lambda e: e.dma_start(out=tdt[:], in_=td_d[:]))
        op_inc("sync", "sx", 16, lambda e: e.dma_start(out=tft[:], in_=tf_d[:]))
        op_inc("sync", "sx", 16, lambda e: e.dma_start(out=consts[:, 4:5], in_=mask_d[:]))
        op_inc("sync", "sx", 16, lambda e: e.dma_start(out=eye_t[:], in_=eye_d[:]))
        emit("vector", lambda v: v.memset(consts[:, 0:1], EPS_LN))
        emit("vector", lambda v: v.memset(consts[:, 1:2], 0.5))
        emit("vector", lambda v: v.memset(consts[:, 2:3], -1.0))
        emit("vector", lambda v: v.memset(consts[:, 3:4], 0.0))
        emit("vector", lambda v: v.memset(consts[:, 5:6], 0.25))
        emit("vector", lambda v: v.memset(pbuf[:, 0:1], 0.0))
        emit("vector", lambda v: v.memset(qbuf[:, 0:1], 0.0))
        emit("vector", lambda v: v.memset(gacc[:], 0.0))
        vv = op_inc("vector", "sv", 1, lambda v: v.memset(ones_t[:], 1.0))
        wait("scalar", "sx", 64)
        wait("scalar", "sv", vv)
        emit("scalar", lambda s: s.activation(out=etd[:], in_=tdt[:], func=ACT.Exp))
        op_inc("scalar", "ss", 1, lambda s: s.activation(out=euch[:], in_=tft[:], func=ACT.Exp))
        sgap()
        op_inc("scalar", "ss", 1, lambda s: s.activation(out=dch[:], in_=etd[:], func=ACT.Exp,
                                                         scale=consts[:, 2:3]))

    xn_done = {}
    sq_done = {}
    tr_last = {}
    tr_evac = {}
    trc = [0]

    def emit_xn(b):
        par = b % 2
        if b >= 2:
            wait("vector", "str", tr_last[b - 2])
        xv = op_inc("vector", "sevv", 1,
                    lambda v, b=b, par=par: v.tensor_scalar(
                        out=xntok[:, par * C:(par + 1) * C], in0=xsh[:, par * C:(par + 1) * C],
                        scalar1=mean_s[:, b:b + 1], scalar2=rstd_s[:, b:b + 1],
                        op0=ALU.subtract, op1=ALU.mult))
        xn_done[b] = xv
        wait("tensor", "sevv", xv)
        for g4 in range(CCN // 4):
            g = trc[0]
            bank = g % 2
            if g >= 2:
                wait("tensor", "sevv", tr_evac[g - 2])
            tv = 0
            for i in range(4):
                cc = g4 * 4 + i
                inst = [None]
                tv = op_inc("tensor", "str", 1,
                            lambda t, par=par, cc=cc, i=i, bank=bank: t.transpose(
                                ps_tr[bank][:, i * 256:(i + 1) * 256][:, 0:128]
                                if False else
                                ps_tr[bank][:, i * 128 + 0: i * 128 + 128],
                                xntok[:, par * C + cc * 128: par * C + (cc + 1) * 128],
                                eye_t[:]))
            wait("vector", "str", tv)
            ev = 0
            for i in range(4):
                cc = g4 * 4 + i
                ev = op_inc("vector", "sevv", 1,
                            lambda v, b=b, cc=cc, i=i, bank=bank: v.tensor_copy(
                                out=big0[:, cc * T_KV + b * 128: cc * T_KV + (b + 1) * 128],
                                in_=ps_tr[bank][:, i * 128: i * 128 + 128]))
            tr_evac[g] = ev
            trc[0] += 1
        tr_last[b] = cnt["str"]

    def p1():
        for b in range(NBLK):
            par = b % 2
            if b >= 2:
                wait("sync", "sevv", xn_done[b - 2])
                wait("sync", "sevs", sq_done[b - 2])
            sxp = "sxA" if par == 0 else "sxB"
            lv = op_inc("sync", sxp, 16,
                        lambda e, b=b, par=par: e.dma_start(
                            out=xsh[:, par * C:(par + 1) * C], in_=x_d[b * 128:(b + 1) * 128, :]))
            wait("vector", sxp, lv)
            rv = op_inc("vector", "sevv", 1,
                        lambda v, b=b, par=par: v.reduce_sum(
                            out=sum_s[:, b:b + 1], in_=xsh[:, par * C:(par + 1) * C], axis=AX.X))
            wait("scalar", sxp, lv)
            sqv = op_inc("scalar", "sevs", 1,
                         lambda s, b=b, par=par: s.activation(
                             out=gdum[:], in_=xsh[:, par * C:(par + 1) * C],
                             func=ACT.Square, accum_out=sq_s[:, b:b + 1]))
            sq_done[b] = sqv
            wait("scalar", "sevv", rv)
            emit("scalar", lambda s, b=b: s.mul(out=mean_s[:, b:b + 1], in_=sum_s[:, b:b + 1],
                                                mul=1.0 / C))
            sgap()
            mv = op_inc("scalar", "sevs", 1,
                        lambda s, b=b: s.activation(out=m2_s[:, b:b + 1], in_=mean_s[:, b:b + 1],
                                                    func=ACT.Square))
            wait("vector", "sevs", mv)
            vv = op_inc("vector", "sevv", 1,
                        lambda v, b=b: v.scalar_tensor_tensor(
                            out=var_s[:, b:b + 1], in0=sq_s[:, b:b + 1], scalar=1.0 / C,
                            in1=m2_s[:, b:b + 1], op0=ALU.mult, op1=ALU.subtract))
            wait("scalar", "sevv", vv)
            sv_ = op_inc("scalar", "sevs", 1,
                         lambda s, b=b: s.activation(out=std_s[:, b:b + 1], in_=var_s[:, b:b + 1],
                                                     func=ACT.Sqrt, bias=consts[:, 0:1]))
            wait("vector", "sevs", sv_)
            emit("vector", lambda v, b=b: v.reciprocal(out=rstd_s[:, b:b + 1],
                                                       in_=std_s[:, b:b + 1]))
            if b > 0:
                emit_xn(b - 1)
        emit("vector", lambda v: v.memset(consts[:, 9:10], 0.0))
        emit_xn(NBLK - 1)

    gunits = []

    def make_gamma_units():
        w3flat = w3t_d.rearrange("a b p f -> (a b) p f")
        for blk in range(NDT5 * HHN // 4):
            srcs = [w3flat[4 * blk + i, :, :] for i in range(4)]
            gunits.append((srcs, DT5, 96 + blk))
        for hh in range(HHN):
            for m in range(2):
                gunits.append(([w12_d[hh, m, :, :]], C, m * 48 + hh))

    gstate = {"load": 0, "use": 0, "loadv": {}}

    def pump_gamma(n):
        for _ in range(n):
            for _ld in range(2):
                if gstate["load"] < len(gunits) and gstate["load"] - gstate["use"] < 2:
                    i = gstate["load"]
                    srcs, ncols, col = gunits[i]
                    par = i % 2
                    if i == 0:
                        wait("scalar", "sevv", xn_done[NBLK - 1])
                    lv = 0
                    for si, s_ap in enumerate(srcs):
                        lv = op_inc("scalar", "sscA" if par == 0 else "sscB", 16,
                                    lambda e, s_ap=s_ap, ncols=ncols, par=par, si=si:
                                    e.dma_start(out=xsh[:, par * C + si * ncols:
                                                        par * C + (si + 1) * ncols], in_=s_ap))
                    gstate["loadv"][i] = lv
                    gstate["load"] += 1
            if gstate["use"] < gstate["load"]:
                i = gstate["use"]
                srcs, ncols, col = gunits[i]
                tot = ncols * len(srcs)
                par = i % 2
                wait("scalar", "sscA" if par == 0 else "sscB", gstate["loadv"][i])
                op_inc("scalar", "ss", 1,
                       lambda s, tot=tot, col=col, par=par: s.activation(
                           out=gdum[:, 0:tot], in_=xsh[:, par * C: par * C + tot],
                           func=ACT.Abs, accum_out=gacc[:, col:col + 1]))
                gstate["use"] += 1

    def gamma_finish():
        pump_gamma(len(gunits))
        ST["gamma_ss"] = cnt["ss"]
        wait("vector", "ss", cnt["ss"])
        rv = 0
        for m in range(3):
            rv = op_inc("vector", "sevv", 1,
                        lambda v, m=m: v.reduce_sum(out=gred[:, m:m + 1],
                                                    in_=gacc[:, m * 48:(m + 1) * 48], axis=AX.X))
        wait("tensor", "sevv", rv)
        b1 = next_bank()
        wait("tensor", *bank_free[b1])
        mv = op_inc("tensor", "smm", 1,
                    lambda t, b1=b1: t.matmul(mmb[b1][0:1, 0:3], ones_t[:, 0:1], gred[:, 0:3],
                                              start=True, stop=True))
        wait("vector", "smm", mv)
        cv = op_inc("vector", "sevv", 1, lambda v, b1=b1: v.tensor_copy(
            out=g1[0:1, 0:3], in_=mmb[b1][0:1, 0:3]))
        bank_free[b1] = ("sevv", cv)
        wait("tensor", "sevv", cv)
        b2 = next_bank()
        wait("tensor", *bank_free[b2])
        mv2 = op_inc("tensor", "smm", 1,
                     lambda t, b2=b2: t.matmul(mmb[b2][:, 4:7], ones_t[0:1, :], g1[0:1, 0:3],
                                               start=True, stop=True))
        wait("vector", "smm", mv2)
        tv_ = op_inc("vector", "sth", 1,
                     lambda v, b2=b2: v.tensor_scalar(out=thr[:, 0:3], in0=mmb[b2][:, 4:7],
                                                      scalar1=0.5 / NW, scalar2=0.5 * EPS_Q,
                                                      op0=ALU.mult, op1=ALU.add))
        bank_free[b2] = ("sth", tv_)
        vgap()
        vgap()
        ST["thr_done"] = op_inc("vector", "sth", 1,
                                lambda v: v.tensor_scalar(out=thr[:, 3:6], in0=thr[:, 0:3],
                                                          scalar1=-1.0, scalar2=None,
                                                          op0=ALU.mult))


    last_q = {"v": 0}

    def quantize_sgn(src_ap, dst_ap, m, sg_ap):
        wait("scalar", "sevv", last_q["v"])
        op_inc("scalar", "sevs", 1,
               lambda s, src_ap=src_ap, sg_ap=sg_ap, m=m: s.activation(
                   out=sg_ap, in_=src_ap, func=ACT.Sign, bias=thr[:, 3 + m:4 + m]))
        a2 = op_inc("scalar", "sevs", 1,
                    lambda s, src_ap=src_ap, dst_ap=dst_ap, m=m: s.activation(
                        out=dst_ap, in_=src_ap, func=ACT.Sign, bias=thr[:, m:m + 1]))
        wait("vector", "sevs", a2)
        qv = op_inc("vector", "sevv", 1,
                    lambda v, dst_ap=dst_ap, sg_ap=sg_ap: v.tensor_add(
                        out=dst_ap, in0=dst_ap, in1=sg_ap))
        last_q["v"] = qv
        return qv

    def quantize_vec(src_ap, dst_ap, m):
        op_inc("vector", "sevv", 1,
               lambda v, src_ap=src_ap, m=m: v.tensor_scalar(
                   out=u_t[:, 0:src_ap.shape[1]], in0=src_ap, scalar1=thr[:, 3 + m:4 + m],
                   scalar2=None, op0=ALU.is_le))
        vgap()
        return op_inc("vector", "sevv", 1,
                      lambda v, src_ap=src_ap, dst_ap=dst_ap, m=m: v.scalar_tensor_tensor(
                          out=dst_ap, in0=src_ap, scalar=thr[:, m:m + 1],
                          in1=u_t[:, 0:src_ap.shape[1]], op0=ALU.is_ge, op1=ALU.subtract))

    q3units = []

    def make_q3_units():
        for dt in range(NDT5):
            for hh in range(HHN):
                q3units.append((dt, hh))

    qstate = {"load": 0, "use": 0, "loadv": {}, "spw": {}}

    def pump_q3(n):
        for _ in range(n):
            for _ld in range(2):
                if qstate["load"] < len(q3units) and qstate["load"] - qstate["use"] < 2:
                    i = qstate["load"]
                    dt, hh = q3units[i]
                    par = i % 2
                    if i == 0:
                        wait("scalar", "sth", ST["thr_done"])
                    if i >= 2:
                        wait("scalar", "sgq0" if i % 2 == 0 else "sgq1", qstate["spw"][i - 2])
                    qstate["loadv"][i] = op_inc("scalar", "sscA" if par == 0 else "sscB", 16,
                                                lambda e, dt=dt, hh=hh, par=par: e.dma_start(
                                                    out=w3qb[:, par * DT5:(par + 1) * DT5],
                                                    in_=w3t_d[dt, hh, :, :]))
                    qstate["load"] += 1
            if qstate["use"] < qstate["load"]:
                i = qstate["use"]
                dt, hh = q3units[i]
                par = i % 2
                wait("scalar", "sscA" if par == 0 else "sscB", qstate["loadv"][i])
                qv = quantize_sgn(w3qb[:, par * DT5:(par + 1) * DT5],
                                  q3st[:, par * DT5:(par + 1) * DT5], 2, sgn3[:])
                wait("gpsimd", "sevv", qv)
                qstate["spw"][i] = op_inc("gpsimd", "sgq0" if i % 2 == 0 else "sgq1", 16,
                                          lambda e, dt=dt, hh=hh, par=par: e.dma_start(
                                              out=q3sp_d[:, (dt * HHN + hh) * DT5:
                                                         (dt * HHN + hh + 1) * DT5],
                                              in_=q3st[:, par * DT5:(par + 1) * DT5]))
                qstate["use"] += 1

    cons = {}
    ysp_done = {}

    def p3():
        import os as _os
        NCC = int(_os.environ.get("ATHENA_DBG_CC", str(CCN)))
        SKIP_SCAN = _os.environ.get("ATHENA_DBG_NOSCAN", "") == "1"
        NWKB = 3 * CCN * 128
        wk_used = {}
        for cc in range(NCC):
            par = cc % 2
            if cc >= 2:
                wait("gpsimd", "smm", wk_used[cc - 2])
            swp = "swgA" if par == 0 else "swgB"
            lw = op_inc("gpsimd", swp, 16,
                        lambda e, cc=cc, par=par: e.dma_start(
                            out=wkb[:, par * NWKB:(par + 1) * NWKB], in_=wrvk_d[cc, :, :]))
            wait("tensor", swp, lw)

            for j, tts in ((0, KV_TT), (1, KV_TT), (2, R_TT)):
                for (t0, tl) in tts:
                    b = next_bank()
                    wait("tensor", *bank_free[b])
                    for c in range(CCN - 1):
                        lhs = wkb[:, par * NWKB + (j * CCN + c) * 128:
                                  par * NWKB + (j * CCN + c) * 128 + 128]
                        rhs = big0[:, c * T_KV + t0: c * T_KV + t0 + tl]
                        emit("tensor", lambda t, lhs=lhs, rhs=rhs, b=b, c=c, tl=tl:
                             t.matmul(mmb[b][:, 0:tl], lhs, rhs,
                                      start=(c == 0), stop=False))
                    c = CCN - 1
                    lhs = wkb[:, par * NWKB + (j * CCN + c) * 128:
                              par * NWKB + (j * CCN + c) * 128 + 128]
                    rhs = big0[:, c * T_KV + t0: c * T_KV + t0 + tl]
                    gv = op_inc("tensor", "smm", 1,
                                lambda t, lhs=lhs, rhs=rhs, b=b, tl=tl:
                                t.matmul(mmb[b][:, 0:tl], lhs, rhs, start=False, stop=True))
                    wk_used[cc] = gv
                    ch = 0 if t0 < SC[1][0] else 1
                    c0 = SC[ch][0]
                    if j == 0:
                        dst = ekt[:, ch * SCW + (t0 - c0): ch * SCW + (t0 - c0) + tl]
                        wait("scalar", "smm", gv)
                        if cc >= 1:
                            wait("scalar", "sevv", cons[cc - 1])
                        ev = op_inc("scalar", "sevs", 1,
                                    lambda s, dst=dst, b=b, tl=tl: s.activation(
                                        out=dst, in_=mmb[b][:, 0:tl], func=ACT.Exp))
                        bank_free[b] = ("sevs", ev)
                    elif j == 1:
                        dst = vt[:, ch * SCW + (t0 - c0): ch * SCW + (t0 - c0) + tl]
                        wait("vector", "smm", gv)
                        if cc >= 1:
                            wait("vector", "sevv", cons[cc - 1])
                        ev = op_inc("vector", "sevv", 1,
                                    lambda v, dst=dst, b=b, tl=tl: v.tensor_copy(
                                        out=dst, in_=mmb[b][:, 0:tl]))
                        bank_free[b] = ("sevv", ev)
                    else:
                        o0 = t0 - WIN
                        wait("scalar", "smm", gv)
                        if cc >= 1:
                            wait("scalar", "sevv", cons[cc - 1])
                        ev = op_inc("scalar", "sevs", 1,
                                    lambda s, o0=o0, b=b, tl=tl: s.activation(
                                        out=trt[:, o0:o0 + tl], in_=mmb[b][:, 0:tl],
                                        func=ACT.Tanh, scale=consts[:, 1:2]))
                        bank_free[b] = ("sevs", ev)

            wait("vector", "sevs", cnt["sevs"])
            wait("vector", "sevv", cnt["sevv"])
            if SKIP_SCAN:
                cons[cc] = op_inc("vector", "sevv", 1, lambda v: v.memset(consts[:, 8:9], 0.0))
                for _ch in range(2):
                    k_ = cc * 2 + _ch
                    ysp_done[k_] = cnt["sgy0"] if k_ % 2 == 0 else cnt["sgy1"]
                if _os.environ.get("ATHENA_DBG_NOGAMMA", "") != "1":
                    pump_gamma(9)
                continue
            emit("vector", lambda v: v.tensor_scalar(out=ekt[:, 0:WIN], in0=ekt[:, 0:WIN],
                                                     scalar1=consts[:, 4:5], scalar2=None,
                                                     op0=ALU.mult))
            vgap()
            for ch, (c0, cl) in enumerate(SC):
                ek_c = ekt[:, ch * SCW: ch * SCW + cl]
                v_c = vt[:, ch * SCW: ch * SCW + cl]
                pb = pbuf[:, ch * (SCW + 1): ch * (SCW + 1) + SCW + 1]
                qbv = qbuf[:, ch * (SCW + 1): ch * (SCW + 1) + SCW + 1]
                d_bc = dch[:, cc:cc + 1].broadcast_to([128, cl])
                oo, ol = (WIN, 1024 - WIN) if ch == 0 else (0, cl)
                own_off = 0 if ch == 0 else 1024 - WIN
                emit("vector", lambda v, qbv=qbv, d_bc=d_bc, ek_c=ek_c, cl=cl:
                     v.tensor_tensor_scan(out=qbv[:, 1:1 + cl], data0=d_bc, data1=ek_c,
                                          initial=qbv[:, 0:1], op0=ALU.mult, op1=ALU.add))
                emit("vector", lambda v, own_off=own_off, ol=ol: v.tensor_scalar(
                    out=trt[:, own_off:own_off + ol], in0=trt[:, own_off:own_off + ol],
                    scalar1=0.5, scalar2=0.5, op0=ALU.mult, op1=ALU.add))
                emit("vector", lambda v, ek_c=ek_c, qbv=qbv, cc=cc, oo=oo, ol=ol:
                     v.scalar_tensor_tensor(out=dte[:, 0:ol], in0=ek_c[:, oo:oo + ol],
                                            scalar=euch[:, cc:cc + 1], in1=qbv[:, oo:oo + ol],
                                            op0=ALU.mult, op1=ALU.add))
                if ch + 1 < len(SC):
                    emit("vector", lambda v, qbv=qbv, cl=cl, ch=ch: v.tensor_copy(
                        out=qbuf[:, (ch + 1) * (SCW + 1):(ch + 1) * (SCW + 1) + 1],
                        in_=qbv[:, cl:cl + 1]))
                emit("vector", lambda v, ek_c=ek_c, v_c=v_c: v.tensor_mul(
                    out=ek_c, in0=ek_c, in1=v_c))
                vgap()
                emit("vector", lambda v, pb=pb, d_bc=d_bc, ek_c=ek_c, cl=cl:
                     v.tensor_tensor_scan(out=pb[:, 1:1 + cl], data0=d_bc, data1=ek_c,
                                          initial=pb[:, 0:1], op0=ALU.mult, op1=ALU.add))
                emit("vector", lambda v, ol=ol: v.reciprocal(
                    out=dte[:, 0:ol], in_=dte[:, 0:ol]))
                emit("vector", lambda v, v_c=v_c, ek_c=ek_c, pb=pb, cc=cc, oo=oo, ol=ol:
                     v.scalar_tensor_tensor(out=v_c[:, 0:ol], in0=ek_c[:, oo:oo + ol],
                                            scalar=euch[:, cc:cc + 1], in1=pb[:, oo:oo + ol],
                                            op0=ALU.mult, op1=ALU.add))
                if ch + 1 < len(SC):
                    emit("vector", lambda v, pb=pb, cl=cl, ch=ch: v.tensor_copy(
                        out=pbuf[:, (ch + 1) * (SCW + 1):(ch + 1) * (SCW + 1) + 1],
                        in_=pb[:, cl:cl + 1]))
                vgap()
                emit("vector", lambda v, qbv=qbv, v_c=v_c, ol=ol: v.tensor_mul(
                    out=qbv[:, 1:1 + ol], in0=v_c[:, 0:ol], in1=dte[:, 0:ol]))
                k = cc * 2 + ch
                ypar = k % 2
                if k >= 2:
                    wait("vector", "sgy0" if ypar == 0 else "sgy1", ysp_done[k - 2])
                vgap()
                yv = op_inc("vector", "sevv", 1,
                            lambda v, own_off=own_off, ol=ol, qbv=qbv, ypar=ypar:
                            v.tensor_mul(out=ybf[:, ypar * SCW: ypar * SCW + ol],
                                         in0=trt[:, own_off:own_off + ol],
                                         in1=qbv[:, 1:1 + ol]))
                wait("gpsimd", "sevv", yv)
                ysp_done[k] = op_inc("gpsimd", "sgy0" if ypar == 0 else "sgy1", 16,
                                     lambda e, cc=cc, own_off=own_off, ol=ol, ypar=ypar:
                                     e.dma_start(
                                         out=ysp_d[:, cc * T_OWN + own_off:
                                                   cc * T_OWN + own_off + ol],
                                         in_=ybf[:, ypar * SCW: ypar * SCW + ol]))
            cons[cc] = op_inc("vector", "sevv", 1, lambda v: v.memset(consts[:, 8:9], 0.0))
            if _os.environ.get("ATHENA_DBG_NOGAMMA", "") != "1":
                pump_gamma(9)

    hid_ev = {}
    hid_spill = {}
    mm_after_hh = {}

    def p4():
        wait("sync", "ss", ST["gamma_ss"])
        q_ready = {}
        for hh in range(HHN):
            par = hh % 2
            if hh >= 1:
                wait("sync", "sevv", q_ready[hh - 1])
            l1 = op_inc("sync", "sws", 16,
                        lambda e, hh=hh: e.dma_start(out=xsh[:, 0:C], in_=w12_d[hh, 0, :, :]))
            l2 = op_inc("sync", "sws", 16,
                        lambda e, hh=hh: e.dma_start(out=xsh[:, C:2 * C], in_=w12_d[hh, 1, :, :]))
            if hh >= 2:
                wait("scalar", "smm", mm_after_hh[hh - 2])
                wait("vector", "smm", mm_after_hh[hh - 2])
            wait("scalar", "sws", l2)
            if hh == 0:
                wait("scalar", "sth", ST["thr_done"])
            quantize_sgn(xsh[:, 0:C], qb[:, (par * 2) * C:(par * 2 + 1) * C], 0, sgn[:])
            wait("vector", "sws", l2)
            qv = quantize_vec(xsh[:, C:2 * C], qb[:, (par * 2 + 1) * C:(par * 2 + 2) * C], 1)
            q_ready[hh] = qv
            wait("tensor", "sevv", qv)
            for ti, (t0, tl) in enumerate(R_TT):
                b1 = next_bank()
                wait("tensor", *bank_free[b1])
                for c in range(CCN - 1):
                    lhs = qb[:, (par * 2) * C + c * 128:(par * 2) * C + (c + 1) * 128]
                    rhs = big0[:, c * T_KV + t0: c * T_KV + t0 + tl]
                    emit("tensor", lambda t, lhs=lhs, rhs=rhs, b1=b1, c=c, tl=tl:
                         t.matmul(mmb[b1][:, 0:tl], lhs, rhs, start=(c == 0), stop=False))
                c = CCN - 1
                lhs = qb[:, (par * 2) * C + c * 128:(par * 2) * C + (c + 1) * 128]
                rhs = big0[:, c * T_KV + t0: c * T_KV + t0 + tl]
                g1v = op_inc("tensor", "smm", 1,
                             lambda t, lhs=lhs, rhs=rhs, b1=b1, tl=tl:
                             t.matmul(mmb[b1][:, 0:tl], lhs, rhs, start=False, stop=True))
                b2 = next_bank()
                wait("tensor", *bank_free[b2])
                for c in range(CCN - 1):
                    lhs = qb[:, (par * 2 + 1) * C + c * 128:(par * 2 + 1) * C + (c + 1) * 128]
                    rhs = big0[:, c * T_KV + t0: c * T_KV + t0 + tl]
                    emit("tensor", lambda t, lhs=lhs, rhs=rhs, b2=b2, c=c, tl=tl:
                         t.matmul(mmb[b2][:, 0:tl], lhs, rhs, start=(c == 0), stop=False))
                c = CCN - 1
                lhs = qb[:, (par * 2 + 1) * C + c * 128:(par * 2 + 1) * C + (c + 1) * 128]
                rhs = big0[:, c * T_KV + t0: c * T_KV + t0 + tl]
                g2v = op_inc("tensor", "smm", 1,
                             lambda t, lhs=lhs, rhs=rhs, b2=b2, tl=tl:
                             t.matmul(mmb[b2][:, 0:tl], lhs, rhs, start=False, stop=True))
                mm_after_hh[hh] = g2v
                k = hh * 4 + ti
                tpar = ti % 2
                wait("scalar", "smm", g1v)
                if k >= 2:
                    wait("scalar", "sevv", hid_ev[k - 2])
                e1 = op_inc("scalar", "sevs", 1,
                            lambda s, b1=b1, tpar=tpar, tl=tl: s.activation(
                                out=tanh4[:, tpar * 512: tpar * 512 + tl],
                                in_=mmb[b1][:, 0:tl], func=ACT.Tanh, scale=consts[:, 5:6]))
                bank_free[b1] = ("sevs", e1)
                o0 = t0 - WIN
                hpar = ti % 2
                wait("vector", "sevs", e1)
                wait("vector", "smm", g2v)
                if k >= 2:
                    wait("vector", "sgh0" if k % 2 == 0 else "sgh1", hid_spill[k - 2])
                emit("vector", lambda v, tpar=tpar, hpar=hpar, tl=tl: v.tensor_scalar(
                    out=hidb[:, hpar * C: hpar * C + tl],
                    in0=tanh4[:, tpar * 512: tpar * 512 + tl],
                    scalar1=0.25, scalar2=0.25, op0=ALU.mult, op1=ALU.add))
                vgap()
                e3 = op_inc("vector", "sevv", 1,
                            lambda v, b2=b2, hpar=hpar, tl=tl: v.tensor_mul(
                                out=hidb[:, hpar * C: hpar * C + tl],
                                in0=hidb[:, hpar * C: hpar * C + tl], in1=mmb[b2][:, 0:tl]))
                hid_ev[k] = e3
                bank_free[b2] = ("sevv", e3)
                wait("gpsimd", "sevv", e3)
                hid_spill[k] = op_inc("gpsimd", "sgh0" if k % 2 == 0 else "sgh1", 16,
                                      lambda e, hh=hh, o0=o0, tl=tl, hpar=hpar: e.dma_start(
                                          out=hidsp_d[:, hh * T_OWN + o0:
                                                      hh * T_OWN + o0 + tl],
                                          in_=hidb[:, hpar * C: hpar * C + tl]))
            pump_q3(4)
        pump_q3(len(q3units))
        ST["hid_sg"] = (cnt["sgh0"], cnt["sgh1"])
        ST["q3_sg"] = (cnt["sgq0"], cnt["sgq1"])
        ST["y_sg"] = (cnt["sgy0"], cnt["sgy1"])
        ST["p4_mm"] = cnt["smm"]

    def p5():
        hoff = 0
        yoff = HHN * TT5
        mm_tti = {}
        mm_dt = {}
        xused = {}
        outst = {}
        slot_free = {}
        eidx = [0]
        for tti in range(NTT5):
            wait("sync", "sgh0", ST["hid_sg"][0])
            wait("sync", "sgh1", ST["hid_sg"][1])
            wait("sync", "sgq0", ST["q3_sg"][0])
            wait("sync", "sgq1", ST["q3_sg"][1])
            wait("sync", "smm", mm_tti.get(tti - 1, ST["p4_mm"]))
            hv = 0
            for g0 in range(0, HHN, 11):
                src = hidsp_d.rearrange("p (a f) -> p a f", a=HHN)[
                    :, g0:g0 + 11, tti * TT5:(tti + 1) * TT5]
                dst = big0[:, hoff + g0 * TT5: hoff + (g0 + 11) * TT5].rearrange(
                    "p (a f) -> p a f", a=11)
                hv = op_inc("sync", "sp5", 16,
                            lambda e, src=src, dst=dst: e.dma_start(out=dst, in_=src))
            wait("sync", "sgy0", ST["y_sg"][0])
            wait("sync", "sgy1", ST["y_sg"][1])
            ysrc = ysp_d.rearrange("p (a f) -> p a f", a=CCN)[:, :, tti * TT5:(tti + 1) * TT5]
            ydst = big0[:, yoff: yoff + CCN * TT5].rearrange("p (a f) -> p a f", a=CCN)
            yv = op_inc("sync", "sp5", 16,
                        lambda e, ysrc=ysrc, ydst=ydst: e.dma_start(out=ydst, in_=ysrc))
            for dt in range(NDT5):
                prev = mm_dt.get((tti, dt - 1) if dt > 0 else (tti - 1, NDT5 - 1), 0)
                wait("sync", "smm", prev)
                wait("gpsimd", "smm", prev)
                ql = op_inc("sync", "sp5", 16,
                            lambda e, dt=dt: e.dma_start(
                                out=qdt[:, 0:HHN * DT5],
                                in_=q3sp_d[:, dt * HHN * DT5:(dt + 1) * HHN * DT5]))
                wl = op_inc("gpsimd", "sw5", 16,
                            lambda e, dt=dt: e.dma_start(
                                out=qdt[:, HHN * DT5:(HHN + CCN) * DT5], in_=wout_d[dt, :, :]))
                for tb in range(NTB5):
                    tglob = tti * NTB5 + tb
                    xslot = eidx[0] % 4
                    if eidx[0] >= 4:
                        wait("sync", "sevv", xused[eidx[0] - 4])
                    xlv = op_inc("sync", f"spx{xslot}", 16,
                                 lambda e, tglob=tglob, dt=dt, xslot=xslot: e.dma_start(
                                     out=xtok5[:, xslot * 512:(xslot + 1) * 512],
                                     in_=x_d[WIN + tglob * 128: WIN + (tglob + 1) * 128,
                                             dt * DT5:(dt + 1) * DT5]))
                    ST[("xl", eidx[0])] = xlv
                    b = next_bank()
                    wait("tensor", *bank_free[b])
                    wait("tensor", "sp5", cnt["sp5"])
                    wait("tensor", "sw5", wl)
                    for hh in range(HHN):
                        lhs = big0[:, hoff + hh * TT5 + tb * 128:
                                   hoff + hh * TT5 + (tb + 1) * 128]
                        rhs = qdt[:, hh * DT5:(hh + 1) * DT5]
                        emit("tensor", lambda t, lhs=lhs, rhs=rhs, b=b, first=(hh == 0):
                             t.matmul(mmb[b][:], lhs, rhs, start=first, stop=False))
                    for cc in range(CCN - 1):
                        lhs = big0[:, yoff + cc * TT5 + tb * 128:
                                   yoff + cc * TT5 + (tb + 1) * 128]
                        rhs = qdt[:, (HHN + cc) * DT5:(HHN + cc + 1) * DT5]
                        emit("tensor", lambda t, lhs=lhs, rhs=rhs, b=b:
                             t.matmul(mmb[b][:], lhs, rhs, start=False, stop=False))
                    cc = CCN - 1
                    lhs = big0[:, yoff + cc * TT5 + tb * 128: yoff + cc * TT5 + (tb + 1) * 128]
                    rhs = qdt[:, (HHN + cc) * DT5:(HHN + cc + 1) * DT5]
                    gv = op_inc("tensor", "smm", 1,
                                lambda t, lhs=lhs, rhs=rhs, b=b:
                                t.matmul(mmb[b][:], lhs, rhs, start=False, stop=True))
                    mm_dt[(tti, dt)] = gv
                    mm_tti[tti] = gv
                    slot = eidx[0] % 4
                    wait("vector", "smm", gv)
                    wait("vector", f"spx{slot}", ST[("xl", eidx[0])])
                    if eidx[0] >= 4:
                        wait("vector", f"sst{slot}", slot_free[eidx[0] - 4])
                    ev = op_inc("vector", "sevv", 1,
                                lambda v, b=b, slot=slot, xslot=xslot: v.tensor_add(
                                    out=outsb[:, slot * 512:(slot + 1) * 512],
                                    in0=mmb[b][:],
                                    in1=xtok5[:, xslot * 512:(xslot + 1) * 512]))
                    bank_free[b] = ("sevv", ev)
                    xused[eidx[0]] = ev
                    wait("gpsimd", "sevv", ev)
                    sv = op_inc("gpsimd", f"sst{slot}", 16,
                                lambda e, tglob=tglob, dt=dt, slot=slot: e.dma_start(
                                    out=out_d[tglob * 128:(tglob + 1) * 128,
                                              dt * DT5:(dt + 1) * DT5],
                                    in_=outsb[:, slot * 512:(slot + 1) * 512]))
                    slot_free[eidx[0]] = sv
                    outst[tglob] = sv
                    eidx[0] += 1
        for s_ in range(4):
            wait("gpsimd", f"sst{s_}", cnt[f"sst{s_}"])

    import os
    DBG = int(os.environ.get("ATHENA_DBG_PHASES", "9"))
    make_gamma_units()
    make_q3_units()
    p0()
    if DBG >= 1:
        p1()
    if DBG >= 2:
        p3()
    if DBG >= 3:
        gamma_finish()
    if DBG >= 4:
        p4()
    if DBG >= 5:
        p5()
    if DBG < 5:
        for n in ("sevv", "sevs", "smm", "str", "ss", "sgy0", "sgy1",
                  "sgh0", "sgh1", "sgq0", "sgq1"):
            wait("gpsimd", n, cnt[n])

    with nc.Block() as block:
        @block.sync
        def _(e):
            for f in OPS["sync"]:
                f(e)

        @block.scalar
        def _(e):
            for f in OPS["scalar"]:
                f(e)

        @block.vector
        def _(e):
            for f in OPS["vector"]:
                f(e)

        @block.tensor
        def _(e):
            for f in OPS["tensor"]:
                f(e)

        @block.gpsimd
        def _(e):
            for f in OPS["gpsimd"]:
                f(e)

    es.close()
    return nc


def _prep_inputs(L0_attn_x, L0_attn_W_rvk, L0_attn_W_out, L0_attn_time_decay,
                 L0_attn_time_first, L0_ffn_w1, L0_ffn_w2, L0_ffn_w3):
    x = np.ascontiguousarray(np.asarray(L0_attn_x, dtype=np.float32))
    W_rvk = np.asarray(L0_attn_W_rvk, dtype=np.float32)
    W_out = np.asarray(L0_attn_W_out, dtype=np.float32)
    td = np.asarray(L0_attn_time_decay, dtype=np.float32)
    tf = np.asarray(L0_attn_time_first, dtype=np.float32)
    w1 = np.asarray(L0_ffn_w1, dtype=np.float32)
    w2 = np.asarray(L0_ffn_w2, dtype=np.float32)
    w3 = np.asarray(L0_ffn_w3, dtype=np.float32)

    WT = W_rvk.T
    wrvk = np.empty((CCN, 128, 3, CCN, 128), np.float32)
    for j, base in ((0, 2 * C), (1, C), (2, 0)):
        blk = WT[:, base:base + C].reshape(CCN, 128, CCN, 128)
        wrvk[:, :, j, :, :] = blk.transpose(2, 1, 0, 3)
    wrvk = np.ascontiguousarray(wrvk.reshape(CCN, 128, 3 * CCN * 128))

    w12 = np.empty((HHN, 2, 128, C), np.float32)
    for m, w in ((0, w1), (1, w2)):
        t = w.reshape(HHN, 128, CCN, 128)
        w12[:, m, :, :] = t.transpose(0, 3, 2, 1).reshape(HHN, 128, C)
    w12 = np.ascontiguousarray(w12)

    w3t = np.ascontiguousarray(w3.reshape(NDT5, DT5, HHN, 128).transpose(0, 2, 3, 1))

    wo = W_out.reshape(NDT5, DT5, CCN, 128).transpose(0, 3, 2, 1)
    wout = np.ascontiguousarray(wo.reshape(NDT5, 128, CCN * DT5))

    tdt = np.ascontiguousarray(td.reshape(CCN, 128).T)
    tft = np.ascontiguousarray(tf.reshape(CCN, 128).T)
    eye = np.eye(128, dtype=ml_dtypes.bfloat16)

    in_maps = []
    for core in range(8):
        b, h = core // 2, core % 2
        if h == 0:
            xs = np.concatenate([x[b, 0:WIN], x[b, 0:T_OWN]], axis=0)
            mask = np.zeros((128, 1), np.float32)
        else:
            xs = x[b, T_OWN - WIN: 2 * T_OWN]
            mask = np.ones((128, 1), np.float32)
        in_maps.append({
            "x": np.ascontiguousarray(xs),
            "wrvk": wrvk, "w12": w12, "w3t": w3t, "wout": wout,
            "td": tdt, "tf": tft, "mask": mask, "eye": eye,
        })
    return in_maps


def kernel(L0_attn_x, L0_attn_W_rvk, L0_attn_W_out, L0_attn_time_decay,
           L0_attn_time_first, L0_ffn_w1, L0_ffn_w2, L0_ffn_w3):
    in_maps = _prep_inputs(L0_attn_x, L0_attn_W_rvk, L0_attn_W_out,
                           L0_attn_time_decay, L0_attn_time_first,
                           L0_ffn_w1, L0_ffn_w2, L0_ffn_w3)
    nc = build_nc()
    res = run_bass_kernel_spmd(nc, in_maps, core_ids=list(range(8)))
    B, T = 4, 4096
    out = np.empty((B, T, C), np.float32)
    for core in range(8):
        b, h = core // 2, core % 2
        out[b, h * T_OWN:(h + 1) * T_OWN, :] = res.results[core]["out"]
    return out


# version 02
# speedup vs baseline: 1.1485x; 1.1485x over previous
import numpy as np
import ml_dtypes
import contextlib
import concourse.bass as bass
import concourse.mybir as mybir
from concourse.bass_utils import run_bass_kernel_spmd

F32 = mybir.dt.float32
BF16 = mybir.dt.bfloat16
ALU = mybir.AluOpType
ACT = mybir.ActivationFunctionType
AX = mybir.AxisListType

C = 2048
CCN = 16
H = 5632
HHN = 44
WIN = 128
T_OWN = 2048
T_KV = T_OWN + WIN
NBLK = T_KV // 128
EPS_LN = 1e-5
EPS_Q = 1e-8
NW = 5632.0 * 2048.0

SC = [(0, 1024), (1024, 1152)]
SCW = 1152
KV_TT = [(0, 512), (512, 512), (1024, 512), (1536, 512), (2048, 128)]
R_TT = [(WIN, 512), (WIN + 512, 512), (WIN + 1024, 512), (WIN + 1536, 512)]
TT5 = 512
NTT5 = T_OWN // TT5
NTB5 = TT5 // 128
DT5 = 512
NDT5 = C // DT5


def build_nc():
    import os as _os
    _norace = _os.environ.get("ATHENA_SIM_NORACE", "") == "1"
    nc = bass.Bass("TRN2", target_bir_lowering=False,
                   detect_race_conditions=not _norace)

    x_d = nc.dram_tensor("x", [T_KV, C], F32, kind="ExternalInput").ap()
    wrvk_d = nc.dram_tensor("wrvk", [CCN, 128, 3 * CCN * 128], F32, kind="ExternalInput").ap()
    w12_d = nc.dram_tensor("w12", [HHN, 2, 128, C], F32, kind="ExternalInput").ap()
    w3t_d = nc.dram_tensor("w3t", [NDT5, HHN, 128, DT5], F32, kind="ExternalInput").ap()
    wout_d = nc.dram_tensor("wout", [NDT5, 128, CCN * DT5], F32, kind="ExternalInput").ap()
    td_d = nc.dram_tensor("td", [128, CCN], F32, kind="ExternalInput").ap()
    tf_d = nc.dram_tensor("tf", [128, CCN], F32, kind="ExternalInput").ap()
    mask_d = nc.dram_tensor("mask", [128, 1], F32, kind="ExternalInput").ap()
    eye_d = nc.dram_tensor("eye", [128, 128], BF16, kind="ExternalInput").ap()
    out_d = nc.dram_tensor("out", [T_OWN, C], F32, kind="ExternalOutput").ap()
    ysp_d = nc.dram_tensor("ysp", [128, CCN * T_OWN], BF16, kind="Internal").ap()
    hidsp_d = nc.dram_tensor("hidsp", [128, HHN * T_OWN], BF16, kind="Internal").ap()
    q3sp_d = nc.dram_tensor("q3sp", [128, NDT5 * HHN * DT5], BF16, kind="Internal").ap()

    es = contextlib.ExitStack()

    SB_BASE = (nc._sbuf_addr_for_side("left") + 63) // 64 * 64
    SB_TOP = nc._sbuf_addr_for_side("right")
    off = [SB_BASE]

    def sb(name, shape, dtype, at=None):
        nbytes = int(np.prod(shape[1:])) * (2 if dtype == BF16 else 4)
        if at is None:
            at = off[0]
            off[0] = at + ((nbytes + 63) // 64) * 64
        return nc.alloc_sbuf_tensor_at(name, shape, dtype, offset=at), at

    def sba(name, shape, dtype, at=None):
        return sb(name, shape, dtype, at)[0]

    consts = sba("consts", [128, 16], F32)
    dch = sba("dch", [128, CCN], F32)
    euch = sba("euch", [128, CCN], F32)
    tdt = sba("tdt", [128, CCN], F32)
    etd = sba("etd", [128, CCN], F32)
    tft = sba("tft", [128, CCN], F32)
    eye_t = sba("eye", [128, 128], BF16)
    sum_s = sba("sum_s", [128, NBLK], F32)
    sq_s = sba("sq_s", [128, NBLK], F32)
    mean_s = sba("mean_s", [128, NBLK], F32)
    m2_s = sba("m2_s", [128, NBLK], F32)
    var_s = sba("var_s", [128, NBLK], F32)
    std_s = sba("std_s", [128, NBLK], F32)
    rstd_s = sba("rstd_s", [128, NBLK], F32)
    gacc = sba("gacc", [128, 144], F32)
    gred = sba("gred", [128, 8], F32)
    g1 = sba("g1", [1, 8], F32)
    canary = sba("canary", [128, 8], F32)
    ones_t = sba("ones_t", [128, 128], F32)
    sgn3, sgn3_at = sb("sgn3", [128, DT5], BF16)
    q3st = sba("q3st", [128, 2 * DT5], BF16)
    w3qb = sba("w3qb", [128, 2 * DT5], F32)
    tanh4 = sba("tanh4", [128, 2 * 512], F32)
    hidum, hid_at = sb("hidum", [128, C], F32)
    hidb = sba("hidb", [128, 2 * C], BF16, at=hid_at)
    gdum = hidum
    xsh = sba("xsh", [128, 2 * C], F32)
    big0, big0_at = sb("big0", [128, CCN * T_KV], BF16)
    union_at = off[0]
    wkb = sba("wkb", [128, 2 * 3 * CCN * 128], BF16)
    ekt, ekt_at = sb("ekt", [128, 2 * SCW], F32)
    vt = sba("vt", [128, 2 * SCW], F32)
    pbuf = sba("pbuf", [128, 2 * (SCW + 1)], F32)
    qbuf = sba("qbuf", [128, 2 * (SCW + 1)], F32)
    dte = sba("dte", [128, SCW], F32)
    ybf = sba("ybf", [128, 2 * SCW], BF16)
    trt = sba("trt", [128, T_OWN], F32)
    sgn = sba("sgn", [128, C], BF16)
    u_t = sba("u_t", [128, C], BF16)
    thr = sba("thr", [128, 8], F32)
    gapt = sba("gapt", [128, 512], F32)
    w12x = sba("w12x", [128, C], F32)
    end_at = off[0]
    qb = sba("qb", [128, 4 * C], BF16, at=ekt_at)
    xntok = sba("xntok", [128, 2 * C], BF16, at=ekt_at)
    QDTW = (HHN + CCN) * DT5
    qdt2 = sba("qdt2", [128, 2 * QDTW], BF16, at=big0_at + HHN * TT5 * 2)
    assert big0_at + HHN * TT5 * 2 + 2 * QDTW * 2 <= SB_TOP, "qdt2 overflow"
    y5 = sba("y5", [128, CCN * TT5], BF16, at=sgn3_at)
    xt5 = sba("xt5", [128, 4 * 512], F32, at=sgn3_at + CCN * TT5 * 2)
    ob5 = sba("ob5", [128, 4 * 512], F32, at=sgn3_at + CCN * TT5 * 2 + 4 * 512 * 4)
    assert end_at <= SB_TOP, f"sbuf overflow: {end_at} > {SB_TOP}"

    NMB = 6
    mmb = [es.enter_context(nc.psum_tensor(f"mmb{i}", [128, 512], F32)) for i in range(NMB)]
    ps_tr = [es.enter_context(nc.psum_tensor("ps_tr0", [128, 1024], BF16)),
             es.enter_context(nc.psum_tensor("ps_tr1", [128, 1024], BF16))]

    sem_names = ["sx", "sxA", "sxB", "swgA", "swgB", "sw5", "sws", "sscA", "sscB",
                 "smm", "str", "sevv", "sevs", "sv", "ss",
                 "sgy0", "sgy1", "sgh0", "sgh1", "sgq0", "sgq1",
                 "spx0", "spx1", "spx2", "spx3",
                 "sst0", "sst1", "sst2", "sst3", "sp5", "sth", "sws0", "sws1", "sws2", "sp5b", "sw5b"]
    sems = {n: es.enter_context(nc.semaphore(n)) for n in sem_names}
    cnt = {n: 0 for n in sem_names}
    OPS = {e: [] for e in ["sync", "scalar", "vector", "tensor", "gpsimd"]}

    def emit(eng, fn):
        OPS[eng].append(fn)

    def op_inc(eng, name, by, fn):
        cnt[name] += by
        v = cnt[name]
        sem = sems[name]
        emit(eng, lambda e, fn=fn, sem=sem, by=by: fn(e).then_inc(sem, by))
        return v

    def wait(eng, name, v):
        if v <= 0:
            return
        sem = sems[name]
        emit(eng, lambda e, sem=sem, v=v: e.wait_ge(sem, v))

    def vgap():
        emit("vector", lambda v: v.memset(gapt[:, 0:256], 0.0))

    def sgap():
        emit("scalar", lambda s: s.memzero(gapt[:, 256:512]))

    bank_free = [("sevv", 0)] * NMB
    bank_i = [0]

    def next_bank():
        b = bank_i[0] % NMB
        bank_i[0] += 1
        return b

    ST = {}

    def p0():
        op_inc("sync", "sx", 16, lambda e: e.dma_start(out=tdt[:], in_=td_d[:]))
        op_inc("sync", "sx", 16, lambda e: e.dma_start(out=tft[:], in_=tf_d[:]))
        op_inc("sync", "sx", 16, lambda e: e.dma_start(out=consts[:, 4:5], in_=mask_d[:]))
        op_inc("sync", "sx", 16, lambda e: e.dma_start(out=eye_t[:], in_=eye_d[:]))
        emit("vector", lambda v: v.memset(consts[:, 0:1], EPS_LN))
        emit("vector", lambda v: v.memset(consts[:, 1:2], 0.5))
        emit("vector", lambda v: v.memset(consts[:, 2:3], -1.0))
        emit("vector", lambda v: v.memset(consts[:, 3:4], 0.0))
        emit("vector", lambda v: v.memset(consts[:, 5:6], 0.25))
        emit("vector", lambda v: v.memset(pbuf[:, 0:1], 0.0))
        emit("vector", lambda v: v.memset(qbuf[:, 0:1], 0.0))
        emit("vector", lambda v: v.memset(gacc[:], 0.0))
        vv = op_inc("vector", "sv", 1, lambda v: v.memset(ones_t[:], 1.0))
        wait("scalar", "sx", 64)
        wait("scalar", "sv", vv)
        emit("scalar", lambda s: s.activation(out=etd[:], in_=tdt[:], func=ACT.Exp))
        op_inc("scalar", "ss", 1, lambda s: s.activation(out=euch[:], in_=tft[:], func=ACT.Exp))
        sgap()
        op_inc("scalar", "ss", 1, lambda s: s.activation(out=dch[:], in_=etd[:], func=ACT.Exp,
                                                         scale=consts[:, 2:3]))

    xn_done = {}
    sq_done = {}
    tr_last = {}
    tr_evac = {}
    trc = [0]

    def emit_xn(b):
        par = b % 2
        if b >= 2:
            wait("vector", "str", tr_last[b - 2])
        xv = op_inc("vector", "sevv", 1,
                    lambda v, b=b, par=par: v.tensor_scalar(
                        out=xntok[:, par * C:(par + 1) * C], in0=xsh[:, par * C:(par + 1) * C],
                        scalar1=mean_s[:, b:b + 1], scalar2=rstd_s[:, b:b + 1],
                        op0=ALU.subtract, op1=ALU.mult))
        xn_done[b] = xv
        wait("tensor", "sevv", xv)
        for g4 in range(CCN // 4):
            g = trc[0]
            bank = g % 2
            if g >= 2:
                wait("tensor", "sevv", tr_evac[g - 2])
            tv = 0
            for i in range(4):
                cc = g4 * 4 + i
                inst = [None]
                tv = op_inc("tensor", "str", 1,
                            lambda t, par=par, cc=cc, i=i, bank=bank: t.transpose(
                                ps_tr[bank][:, i * 256:(i + 1) * 256][:, 0:128]
                                if False else
                                ps_tr[bank][:, i * 128 + 0: i * 128 + 128],
                                xntok[:, par * C + cc * 128: par * C + (cc + 1) * 128],
                                eye_t[:]))
            wait("vector", "str", tv)
            ev = 0
            for i in range(4):
                cc = g4 * 4 + i
                ev = op_inc("vector", "sevv", 1,
                            lambda v, b=b, cc=cc, i=i, bank=bank: v.tensor_copy(
                                out=big0[:, cc * T_KV + b * 128: cc * T_KV + (b + 1) * 128],
                                in_=ps_tr[bank][:, i * 128: i * 128 + 128]))
            tr_evac[g] = ev
            trc[0] += 1
        tr_last[b] = cnt["str"]

    def p1():
        for b in range(NBLK):
            par = b % 2
            if b >= 2:
                wait("sync", "sevv", xn_done[b - 2])
                wait("sync", "sevs", sq_done[b - 2])
            sxp = "sxA" if par == 0 else "sxB"
            lv = op_inc("sync", sxp, 16,
                        lambda e, b=b, par=par: e.dma_start(
                            out=xsh[:, par * C:(par + 1) * C], in_=x_d[b * 128:(b + 1) * 128, :]))
            wait("vector", sxp, lv)
            rv = op_inc("vector", "sevv", 1,
                        lambda v, b=b, par=par: v.reduce_sum(
                            out=sum_s[:, b:b + 1], in_=xsh[:, par * C:(par + 1) * C], axis=AX.X))
            wait("scalar", sxp, lv)
            sqv = op_inc("scalar", "sevs", 1,
                         lambda s, b=b, par=par: s.activation(
                             out=gdum[:], in_=xsh[:, par * C:(par + 1) * C],
                             func=ACT.Square, accum_out=sq_s[:, b:b + 1]))
            sq_done[b] = sqv
            wait("scalar", "sevv", rv)
            emit("scalar", lambda s, b=b: s.mul(out=mean_s[:, b:b + 1], in_=sum_s[:, b:b + 1],
                                                mul=1.0 / C))
            sgap()
            mv = op_inc("scalar", "sevs", 1,
                        lambda s, b=b: s.activation(out=m2_s[:, b:b + 1], in_=mean_s[:, b:b + 1],
                                                    func=ACT.Square))
            wait("vector", "sevs", mv)
            vv = op_inc("vector", "sevv", 1,
                        lambda v, b=b: v.scalar_tensor_tensor(
                            out=var_s[:, b:b + 1], in0=sq_s[:, b:b + 1], scalar=1.0 / C,
                            in1=m2_s[:, b:b + 1], op0=ALU.mult, op1=ALU.subtract))
            wait("scalar", "sevv", vv)
            sv_ = op_inc("scalar", "sevs", 1,
                         lambda s, b=b: s.activation(out=std_s[:, b:b + 1], in_=var_s[:, b:b + 1],
                                                     func=ACT.Sqrt, bias=consts[:, 0:1]))
            wait("vector", "sevs", sv_)
            emit("vector", lambda v, b=b: v.reciprocal(out=rstd_s[:, b:b + 1],
                                                       in_=std_s[:, b:b + 1]))
            if b > 0:
                emit_xn(b - 1)
        emit("vector", lambda v: v.memset(consts[:, 9:10], 0.0))
        emit_xn(NBLK - 1)

    gunits = []

    def make_gamma_units():
        w3flat = w3t_d.rearrange("a b p f -> (a b) p f")
        for blk in range(NDT5 * HHN // 4):
            srcs = [w3flat[4 * blk + i, :, :] for i in range(4)]
            gunits.append((srcs, DT5, 96 + blk))
        for hh in range(HHN):
            for m in range(2):
                gunits.append(([w12_d[hh, m, :, :]], C, m * 48 + hh))

    gstate = {"load": 0, "use": 0, "loadv": {}}

    def pump_gamma(n):
        for _ in range(n):
            for _ld in range(2):
                if gstate["load"] < len(gunits) and gstate["load"] - gstate["use"] < 2:
                    i = gstate["load"]
                    srcs, ncols, col = gunits[i]
                    par = i % 2
                    if i == 0:
                        wait("scalar", "sevv", xn_done[NBLK - 1])
                    lv = 0
                    for si, s_ap in enumerate(srcs):
                        lv = op_inc("scalar", "sscA" if par == 0 else "sscB", 16,
                                    lambda e, s_ap=s_ap, ncols=ncols, par=par, si=si:
                                    e.dma_start(out=xsh[:, par * C + si * ncols:
                                                        par * C + (si + 1) * ncols], in_=s_ap))
                    gstate["loadv"][i] = lv
                    gstate["load"] += 1
            if gstate["use"] < gstate["load"]:
                i = gstate["use"]
                srcs, ncols, col = gunits[i]
                tot = ncols * len(srcs)
                par = i % 2
                wait("scalar", "sscA" if par == 0 else "sscB", gstate["loadv"][i])
                op_inc("scalar", "ss", 1,
                       lambda s, tot=tot, col=col, par=par: s.activation(
                           out=gdum[:, 0:tot], in_=xsh[:, par * C: par * C + tot],
                           func=ACT.Abs, accum_out=gacc[:, col:col + 1]))
                gstate["use"] += 1

    def gamma_finish():
        pump_gamma(len(gunits))
        ST["gamma_ss"] = cnt["ss"]
        wait("vector", "ss", cnt["ss"])
        rv = 0
        for m in range(3):
            rv = op_inc("vector", "sevv", 1,
                        lambda v, m=m: v.reduce_sum(out=gred[:, m:m + 1],
                                                    in_=gacc[:, m * 48:(m + 1) * 48], axis=AX.X))
        wait("tensor", "sevv", rv)
        b1 = next_bank()
        wait("tensor", *bank_free[b1])
        mv = op_inc("tensor", "smm", 1,
                    lambda t, b1=b1: t.matmul(mmb[b1][0:1, 0:3], ones_t[:, 0:1], gred[:, 0:3],
                                              start=True, stop=True))
        wait("vector", "smm", mv)
        cv = op_inc("vector", "sevv", 1, lambda v, b1=b1: v.tensor_copy(
            out=g1[0:1, 0:3], in_=mmb[b1][0:1, 0:3]))
        bank_free[b1] = ("sevv", cv)
        wait("tensor", "sevv", cv)
        b2 = next_bank()
        wait("tensor", *bank_free[b2])
        mv2 = op_inc("tensor", "smm", 1,
                     lambda t, b2=b2: t.matmul(mmb[b2][:, 4:7], ones_t[0:1, :], g1[0:1, 0:3],
                                               start=True, stop=True))
        wait("vector", "smm", mv2)
        tv_ = op_inc("vector", "sth", 1,
                     lambda v, b2=b2: v.tensor_scalar(out=thr[:, 0:3], in0=mmb[b2][:, 4:7],
                                                      scalar1=0.5 / NW, scalar2=0.5 * EPS_Q,
                                                      op0=ALU.mult, op1=ALU.add))
        bank_free[b2] = ("sth", tv_)
        vgap()
        vgap()
        ST["thr_done"] = op_inc("vector", "sth", 1,
                                lambda v: v.tensor_scalar(out=thr[:, 3:6], in0=thr[:, 0:3],
                                                          scalar1=-1.0, scalar2=None,
                                                          op0=ALU.mult))


    last_q = {"v": 0}

    def quantize_sgn(src_ap, dst_ap, m, sg_ap):
        wait("scalar", "sevv", last_q["v"])
        op_inc("scalar", "sevs", 1,
               lambda s, src_ap=src_ap, sg_ap=sg_ap, m=m: s.activation(
                   out=sg_ap, in_=src_ap, func=ACT.Sign, bias=thr[:, 3 + m:4 + m]))
        a2 = op_inc("scalar", "sevs", 1,
                    lambda s, src_ap=src_ap, dst_ap=dst_ap, m=m: s.activation(
                        out=dst_ap, in_=src_ap, func=ACT.Sign, bias=thr[:, m:m + 1]))
        wait("vector", "sevs", a2)
        qv = op_inc("vector", "sevv", 1,
                    lambda v, dst_ap=dst_ap, sg_ap=sg_ap: v.tensor_add(
                        out=dst_ap, in0=dst_ap, in1=sg_ap))
        last_q["v"] = qv
        return qv

    def quantize_vec(src_ap, dst_ap, m):
        op_inc("vector", "sevv", 1,
               lambda v, src_ap=src_ap, m=m: v.tensor_scalar(
                   out=u_t[:, 0:src_ap.shape[1]], in0=src_ap, scalar1=thr[:, 3 + m:4 + m],
                   scalar2=None, op0=ALU.is_le))
        vgap()
        return op_inc("vector", "sevv", 1,
                      lambda v, src_ap=src_ap, dst_ap=dst_ap, m=m: v.scalar_tensor_tensor(
                          out=dst_ap, in0=src_ap, scalar=thr[:, m:m + 1],
                          in1=u_t[:, 0:src_ap.shape[1]], op0=ALU.is_ge, op1=ALU.subtract))

    q3units = []

    def make_q3_units():
        for dt in range(NDT5):
            for hh in range(HHN):
                q3units.append((dt, hh))

    qstate = {"load": 0, "use": 0, "loadv": {}, "spw": {}}

    def pump_q3(n):
        for _ in range(n):
            for _ld in range(2):
                if qstate["load"] < len(q3units) and qstate["load"] - qstate["use"] < 2:
                    i = qstate["load"]
                    dt, hh = q3units[i]
                    par = i % 2
                    if i == 0:
                        wait("scalar", "sth", ST["thr_done"])
                    if i >= 2:
                        wait("scalar", "sgq0" if i % 2 == 0 else "sgq1", qstate["spw"][i - 2])
                    qstate["loadv"][i] = op_inc("scalar", "sscA" if par == 0 else "sscB", 16,
                                                lambda e, dt=dt, hh=hh, par=par: e.dma_start(
                                                    out=w3qb[:, par * DT5:(par + 1) * DT5],
                                                    in_=w3t_d[dt, hh, :, :]))
                    qstate["load"] += 1
            if qstate["use"] < qstate["load"]:
                i = qstate["use"]
                dt, hh = q3units[i]
                par = i % 2
                wait("scalar", "sscA" if par == 0 else "sscB", qstate["loadv"][i])
                qv = quantize_sgn(w3qb[:, par * DT5:(par + 1) * DT5],
                                  q3st[:, par * DT5:(par + 1) * DT5], 2, sgn3[:])
                wait("gpsimd", "sevv", qv)
                qstate["spw"][i] = op_inc("gpsimd", "sgq0" if i % 2 == 0 else "sgq1", 16,
                                          lambda e, dt=dt, hh=hh, par=par: e.dma_start(
                                              out=q3sp_d[:, (dt * HHN + hh) * DT5:
                                                         (dt * HHN + hh + 1) * DT5],
                                              in_=q3st[:, par * DT5:(par + 1) * DT5]))
                qstate["use"] += 1

    cons = {}
    ysp_done = {}

    def p3():
        import os as _os
        NCC = int(_os.environ.get("ATHENA_DBG_CC", str(CCN)))
        SKIP_SCAN = _os.environ.get("ATHENA_DBG_NOSCAN", "") == "1"
        NWKB = 3 * CCN * 128
        wk_used = {}
        for cc in range(NCC):
            par = cc % 2
            if cc >= 2:
                wait("gpsimd", "smm", wk_used[cc - 2])
            swp = "swgA" if par == 0 else "swgB"
            lw = op_inc("gpsimd", swp, 16,
                        lambda e, cc=cc, par=par: e.dma_start(
                            out=wkb[:, par * NWKB:(par + 1) * NWKB], in_=wrvk_d[cc, :, :]))
            wait("tensor", swp, lw)

            for j, tts in ((0, KV_TT), (1, KV_TT), (2, R_TT)):
                for (t0, tl) in tts:
                    b = next_bank()
                    wait("tensor", *bank_free[b])
                    for c in range(CCN - 1):
                        lhs = wkb[:, par * NWKB + (j * CCN + c) * 128:
                                  par * NWKB + (j * CCN + c) * 128 + 128]
                        rhs = big0[:, c * T_KV + t0: c * T_KV + t0 + tl]
                        emit("tensor", lambda t, lhs=lhs, rhs=rhs, b=b, c=c, tl=tl:
                             t.matmul(mmb[b][:, 0:tl], lhs, rhs,
                                      start=(c == 0), stop=False))
                    c = CCN - 1
                    lhs = wkb[:, par * NWKB + (j * CCN + c) * 128:
                              par * NWKB + (j * CCN + c) * 128 + 128]
                    rhs = big0[:, c * T_KV + t0: c * T_KV + t0 + tl]
                    gv = op_inc("tensor", "smm", 1,
                                lambda t, lhs=lhs, rhs=rhs, b=b, tl=tl:
                                t.matmul(mmb[b][:, 0:tl], lhs, rhs, start=False, stop=True))
                    wk_used[cc] = gv
                    pump_gamma(1)
                    ch = 0 if t0 < SC[1][0] else 1
                    c0 = SC[ch][0]
                    if j == 0:
                        dst = ekt[:, ch * SCW + (t0 - c0): ch * SCW + (t0 - c0) + tl]
                        wait("scalar", "smm", gv)
                        if cc >= 1:
                            wait("scalar", "sevv", cons[cc - 1])
                        ev = op_inc("scalar", "sevs", 1,
                                    lambda s, dst=dst, b=b, tl=tl: s.activation(
                                        out=dst, in_=mmb[b][:, 0:tl], func=ACT.Exp))
                        bank_free[b] = ("sevs", ev)
                    elif j == 1:
                        dst = vt[:, ch * SCW + (t0 - c0): ch * SCW + (t0 - c0) + tl]
                        wait("vector", "smm", gv)
                        if cc >= 1:
                            wait("vector", "sevv", cons[cc - 1])
                        ev = op_inc("vector", "sevv", 1,
                                    lambda v, dst=dst, b=b, tl=tl: v.tensor_copy(
                                        out=dst, in_=mmb[b][:, 0:tl]))
                        bank_free[b] = ("sevv", ev)
                    else:
                        o0 = t0 - WIN
                        wait("scalar", "smm", gv)
                        if cc >= 1:
                            wait("scalar", "sevv", cons[cc - 1])
                        ev = op_inc("scalar", "sevs", 1,
                                    lambda s, o0=o0, b=b, tl=tl: s.activation(
                                        out=trt[:, o0:o0 + tl], in_=mmb[b][:, 0:tl],
                                        func=ACT.Tanh, scale=consts[:, 1:2]))
                        bank_free[b] = ("sevs", ev)

            wait("vector", "sevs", cnt["sevs"])
            wait("vector", "sevv", cnt["sevv"])
            if SKIP_SCAN:
                cons[cc] = op_inc("vector", "sevv", 1, lambda v: v.memset(consts[:, 8:9], 0.0))
                for _ch in range(2):
                    k_ = cc * 2 + _ch
                    ysp_done[k_] = cnt["sgy0"] if k_ % 2 == 0 else cnt["sgy1"]
                if _os.environ.get("ATHENA_DBG_NOGAMMA", "") != "1":
                    pump_gamma(9)
                continue
            emit("vector", lambda v: v.tensor_scalar(out=ekt[:, 0:WIN], in0=ekt[:, 0:WIN],
                                                     scalar1=consts[:, 4:5], scalar2=None,
                                                     op0=ALU.mult))
            vgap()
            for ch, (c0, cl) in enumerate(SC):
                ek_c = ekt[:, ch * SCW: ch * SCW + cl]
                v_c = vt[:, ch * SCW: ch * SCW + cl]
                pb = pbuf[:, ch * (SCW + 1): ch * (SCW + 1) + SCW + 1]
                qbv = qbuf[:, ch * (SCW + 1): ch * (SCW + 1) + SCW + 1]
                d_bc = dch[:, cc:cc + 1].broadcast_to([128, cl])
                oo, ol = (WIN, 1024 - WIN) if ch == 0 else (0, cl)
                own_off = 0 if ch == 0 else 1024 - WIN
                emit("vector", lambda v, qbv=qbv, d_bc=d_bc, ek_c=ek_c, cl=cl:
                     v.tensor_tensor_scan(out=qbv[:, 1:1 + cl], data0=d_bc, data1=ek_c,
                                          initial=qbv[:, 0:1], op0=ALU.mult, op1=ALU.add))
                emit("vector", lambda v, own_off=own_off, ol=ol: v.tensor_scalar(
                    out=trt[:, own_off:own_off + ol], in0=trt[:, own_off:own_off + ol],
                    scalar1=0.5, scalar2=0.5, op0=ALU.mult, op1=ALU.add))
                emit("vector", lambda v, ek_c=ek_c, qbv=qbv, cc=cc, oo=oo, ol=ol:
                     v.scalar_tensor_tensor(out=dte[:, 0:ol], in0=ek_c[:, oo:oo + ol],
                                            scalar=euch[:, cc:cc + 1], in1=qbv[:, oo:oo + ol],
                                            op0=ALU.mult, op1=ALU.add))
                if ch + 1 < len(SC):
                    emit("vector", lambda v, qbv=qbv, cl=cl, ch=ch: v.tensor_copy(
                        out=qbuf[:, (ch + 1) * (SCW + 1):(ch + 1) * (SCW + 1) + 1],
                        in_=qbv[:, cl:cl + 1]))
                emit("vector", lambda v, ek_c=ek_c, v_c=v_c: v.tensor_mul(
                    out=ek_c, in0=ek_c, in1=v_c))
                vgap()
                emit("vector", lambda v, pb=pb, d_bc=d_bc, ek_c=ek_c, cl=cl:
                     v.tensor_tensor_scan(out=pb[:, 1:1 + cl], data0=d_bc, data1=ek_c,
                                          initial=pb[:, 0:1], op0=ALU.mult, op1=ALU.add))
                emit("vector", lambda v, ol=ol: v.reciprocal(
                    out=dte[:, 0:ol], in_=dte[:, 0:ol]))
                emit("vector", lambda v, v_c=v_c, ek_c=ek_c, pb=pb, cc=cc, oo=oo, ol=ol:
                     v.scalar_tensor_tensor(out=v_c[:, 0:ol], in0=ek_c[:, oo:oo + ol],
                                            scalar=euch[:, cc:cc + 1], in1=pb[:, oo:oo + ol],
                                            op0=ALU.mult, op1=ALU.add))
                if ch + 1 < len(SC):
                    emit("vector", lambda v, pb=pb, cl=cl, ch=ch: v.tensor_copy(
                        out=pbuf[:, (ch + 1) * (SCW + 1):(ch + 1) * (SCW + 1) + 1],
                        in_=pb[:, cl:cl + 1]))
                vgap()
                emit("vector", lambda v, qbv=qbv, v_c=v_c, ol=ol: v.tensor_mul(
                    out=qbv[:, 1:1 + ol], in0=v_c[:, 0:ol], in1=dte[:, 0:ol]))
                k = cc * 2 + ch
                ypar = k % 2
                if k >= 2:
                    wait("vector", "sgy0" if ypar == 0 else "sgy1", ysp_done[k - 2])
                vgap()
                yv = op_inc("vector", "sevv", 1,
                            lambda v, own_off=own_off, ol=ol, qbv=qbv, ypar=ypar:
                            v.tensor_mul(out=ybf[:, ypar * SCW: ypar * SCW + ol],
                                         in0=trt[:, own_off:own_off + ol],
                                         in1=qbv[:, 1:1 + ol]))
                wait("gpsimd", "sevv", yv)
                ysp_done[k] = op_inc("gpsimd", "sgy0" if ypar == 0 else "sgy1", 16,
                                     lambda e, cc=cc, own_off=own_off, ol=ol, ypar=ypar:
                                     e.dma_start(
                                         out=ysp_d[:, cc * T_OWN + own_off:
                                                   cc * T_OWN + own_off + ol],
                                         in_=ybf[:, ypar * SCW: ypar * SCW + ol]))
            cons[cc] = op_inc("vector", "sevv", 1, lambda v: v.memset(consts[:, 8:9], 0.0))

    hid_ev = {}
    hid_spill = {}
    mm_after_hh = {}

    def p4():
        wait("sync", "ss", ST["gamma_ss"])
        q_ready = {}
        qdone = {}
        wslot = {}
        slots = [xsh[:, 0:C], xsh[:, C:2 * C], w12x[:]]
        for hh in range(HHN):
            par = hh % 2
            for m in (0, 1):
                i = 2 * hh + m
                s_ = i % 3
                if i >= 3:
                    wait("sync", "sevv", qdone[i - 3])
                wslot[i] = op_inc("sync", f"sws{s_}", 16,
                                  lambda e, hh=hh, m=m, s_=s_: e.dma_start(
                                      out=slots[s_], in_=w12_d[hh, m, :, :]))
            if hh >= 2:
                wait("scalar", "smm", mm_after_hh[hh - 2])
                wait("vector", "smm", mm_after_hh[hh - 2])
            s1_, s2_ = (2 * hh) % 3, (2 * hh + 1) % 3
            wait("scalar", f"sws{s1_}", wslot[2 * hh])
            if hh == 0:
                wait("scalar", "sth", ST["thr_done"])
            qdone[2 * hh] = quantize_sgn(slots[s1_], qb[:, (par * 2) * C:(par * 2 + 1) * C],
                                         0, sgn[:])
            wait("vector", f"sws{s2_}", wslot[2 * hh + 1])
            qv = quantize_vec(slots[s2_], qb[:, (par * 2 + 1) * C:(par * 2 + 2) * C], 1)
            q_ready[hh] = qv
            qdone[2 * hh + 1] = qv
            wait("tensor", "sevv", qv)
            for ti, (t0, tl) in enumerate(R_TT):
                b1 = next_bank()
                wait("tensor", *bank_free[b1])
                for c in range(CCN - 1):
                    lhs = qb[:, (par * 2) * C + c * 128:(par * 2) * C + (c + 1) * 128]
                    rhs = big0[:, c * T_KV + t0: c * T_KV + t0 + tl]
                    emit("tensor", lambda t, lhs=lhs, rhs=rhs, b1=b1, c=c, tl=tl:
                         t.matmul(mmb[b1][:, 0:tl], lhs, rhs, start=(c == 0), stop=False))
                c = CCN - 1
                lhs = qb[:, (par * 2) * C + c * 128:(par * 2) * C + (c + 1) * 128]
                rhs = big0[:, c * T_KV + t0: c * T_KV + t0 + tl]
                g1v = op_inc("tensor", "smm", 1,
                             lambda t, lhs=lhs, rhs=rhs, b1=b1, tl=tl:
                             t.matmul(mmb[b1][:, 0:tl], lhs, rhs, start=False, stop=True))
                b2 = next_bank()
                wait("tensor", *bank_free[b2])
                for c in range(CCN - 1):
                    lhs = qb[:, (par * 2 + 1) * C + c * 128:(par * 2 + 1) * C + (c + 1) * 128]
                    rhs = big0[:, c * T_KV + t0: c * T_KV + t0 + tl]
                    emit("tensor", lambda t, lhs=lhs, rhs=rhs, b2=b2, c=c, tl=tl:
                         t.matmul(mmb[b2][:, 0:tl], lhs, rhs, start=(c == 0), stop=False))
                c = CCN - 1
                lhs = qb[:, (par * 2 + 1) * C + c * 128:(par * 2 + 1) * C + (c + 1) * 128]
                rhs = big0[:, c * T_KV + t0: c * T_KV + t0 + tl]
                g2v = op_inc("tensor", "smm", 1,
                             lambda t, lhs=lhs, rhs=rhs, b2=b2, tl=tl:
                             t.matmul(mmb[b2][:, 0:tl], lhs, rhs, start=False, stop=True))
                mm_after_hh[hh] = g2v
                k = hh * 4 + ti
                tpar = ti % 2
                wait("scalar", "smm", g1v)
                if k >= 2:
                    wait("scalar", "sevv", hid_ev[k - 2])
                e1 = op_inc("scalar", "sevs", 1,
                            lambda s, b1=b1, tpar=tpar, tl=tl: s.activation(
                                out=tanh4[:, tpar * 512: tpar * 512 + tl],
                                in_=mmb[b1][:, 0:tl], func=ACT.Tanh, scale=consts[:, 5:6]))
                bank_free[b1] = ("sevs", e1)
                o0 = t0 - WIN
                hpar = ti % 2
                wait("vector", "sevs", e1)
                wait("vector", "smm", g2v)
                if k >= 2:
                    wait("vector", "sgh0" if k % 2 == 0 else "sgh1", hid_spill[k - 2])
                emit("vector", lambda v, tpar=tpar, hpar=hpar, tl=tl: v.tensor_scalar(
                    out=hidb[:, hpar * C: hpar * C + tl],
                    in0=tanh4[:, tpar * 512: tpar * 512 + tl],
                    scalar1=0.25, scalar2=0.25, op0=ALU.mult, op1=ALU.add))
                vgap()
                e3 = op_inc("vector", "sevv", 1,
                            lambda v, b2=b2, hpar=hpar, tl=tl: v.tensor_mul(
                                out=hidb[:, hpar * C: hpar * C + tl],
                                in0=hidb[:, hpar * C: hpar * C + tl], in1=mmb[b2][:, 0:tl]))
                hid_ev[k] = e3
                bank_free[b2] = ("sevv", e3)
                wait("gpsimd", "sevv", e3)
                hid_spill[k] = op_inc("gpsimd", "sgh0" if k % 2 == 0 else "sgh1", 16,
                                      lambda e, hh=hh, o0=o0, tl=tl, hpar=hpar: e.dma_start(
                                          out=hidsp_d[:, hh * T_OWN + o0:
                                                      hh * T_OWN + o0 + tl],
                                          in_=hidb[:, hpar * C: hpar * C + tl]))
                pump_q3(1)
        pump_q3(len(q3units))
        ST["hid_sg"] = (cnt["sgh0"], cnt["sgh1"])
        ST["q3_sg"] = (cnt["sgq0"], cnt["sgq1"])
        ST["y_sg"] = (cnt["sgy0"], cnt["sgy1"])
        ST["p4_mm"] = cnt["smm"]

    def p5():
        hoff = 0
        mm_tti = {}
        mm_g = {}
        xused = {}
        slot_free = {}
        eidx = [0]
        for tti in range(NTT5):
            wait("sync", "sgh0", ST["hid_sg"][0])
            wait("sync", "sgh1", ST["hid_sg"][1])
            wait("sync", "sgq0", ST["q3_sg"][0])
            wait("sync", "sgq1", ST["q3_sg"][1])
            wait("sync", "smm", mm_tti.get(tti - 1, ST["p4_mm"]))
            for g0 in range(0, HHN, 11):
                src5 = hidsp_d.rearrange("p (a f) -> p a f", a=HHN)[
                    :, g0:g0 + 11, tti * TT5:(tti + 1) * TT5]
                dst5 = big0[:, hoff + g0 * TT5: hoff + (g0 + 11) * TT5].rearrange(
                    "p (a f) -> p a f", a=11)
                op_inc("sync", "sp5", 16,
                       lambda e, src5=src5, dst5=dst5: e.dma_start(out=dst5, in_=src5))
            wait("sync", "sgy0", ST["y_sg"][0])
            wait("sync", "sgy1", ST["y_sg"][1])
            ysrc = ysp_d.rearrange("p (a f) -> p a f", a=CCN)[:, :, tti * TT5:(tti + 1) * TT5]
            ydst = y5[:].rearrange("p (a f) -> p a f", a=CCN)
            op_inc("sync", "sp5", 16,
                   lambda e, ysrc=ysrc, ydst=ydst: e.dma_start(out=ydst, in_=ysrc))
            for dt in range(NDT5):
                g = tti * NDT5 + dt
                slot = g % 2
                qsem = "sp5" if slot == 0 else "sp5b"
                wsem = "sw5" if slot == 0 else "sw5b"
                prev = mm_g.get(g - 2, ST["p4_mm"])
                wait("sync", "smm", prev)
                wait("gpsimd", "smm", prev)
                ql = op_inc("sync", qsem, 16,
                            lambda e, dt=dt, slot=slot: e.dma_start(
                                out=qdt2[:, slot * QDTW: slot * QDTW + HHN * DT5],
                                in_=q3sp_d[:, dt * HHN * DT5:(dt + 1) * HHN * DT5]))
                wl = op_inc("gpsimd", wsem, 16,
                            lambda e, dt=dt, slot=slot: e.dma_start(
                                out=qdt2[:, slot * QDTW + HHN * DT5: (slot + 1) * QDTW],
                                in_=wout_d[dt, :, :]))
                for tb in range(NTB5):
                    tglob = tti * NTB5 + tb
                    xslot = eidx[0] % 4
                    if eidx[0] >= 4:
                        wait("sync", "sevv", xused[eidx[0] - 4])
                    xlv = op_inc("sync", f"spx{xslot}", 16,
                                 lambda e, tglob=tglob, dt=dt, xslot=xslot: e.dma_start(
                                     out=xt5[:, xslot * 512:(xslot + 1) * 512],
                                     in_=x_d[WIN + tglob * 128: WIN + (tglob + 1) * 128,
                                             dt * DT5:(dt + 1) * DT5]))
                    ST[("xl", eidx[0])] = xlv
                    b = next_bank()
                    wait("tensor", *bank_free[b])
                    wait("tensor", qsem, ql)
                    wait("tensor", "sp5", cnt["sp5"])
                    wait("tensor", wsem, wl)
                    for hh in range(HHN):
                        lhs = big0[:, hoff + hh * TT5 + tb * 128:
                                   hoff + hh * TT5 + (tb + 1) * 128]
                        rhs = qdt2[:, slot * QDTW + hh * DT5: slot * QDTW + (hh + 1) * DT5]
                        emit("tensor", lambda t, lhs=lhs, rhs=rhs, b=b, first=(hh == 0):
                             t.matmul(mmb[b][:], lhs, rhs, start=first, stop=False))
                    for cc in range(CCN - 1):
                        lhs = y5[:, cc * TT5 + tb * 128: cc * TT5 + (tb + 1) * 128]
                        rhs = qdt2[:, slot * QDTW + (HHN + cc) * DT5:
                                   slot * QDTW + (HHN + cc + 1) * DT5]
                        emit("tensor", lambda t, lhs=lhs, rhs=rhs, b=b:
                             t.matmul(mmb[b][:], lhs, rhs, start=False, stop=False))
                    cc = CCN - 1
                    lhs = y5[:, cc * TT5 + tb * 128: cc * TT5 + (tb + 1) * 128]
                    rhs = qdt2[:, slot * QDTW + (HHN + cc) * DT5:
                               slot * QDTW + (HHN + cc + 1) * DT5]
                    gv = op_inc("tensor", "smm", 1,
                                lambda t, lhs=lhs, rhs=rhs, b=b:
                                t.matmul(mmb[b][:], lhs, rhs, start=False, stop=True))
                    mm_g[g] = gv
                    mm_tti[tti] = gv
                    oslot = eidx[0] % 4
                    wait("vector", "smm", gv)
                    wait("vector", f"spx{xslot}", ST[("xl", eidx[0])])
                    if eidx[0] >= 4:
                        wait("vector", f"sst{oslot}", slot_free[eidx[0] - 4])
                    ev = op_inc("vector", "sevv", 1,
                                lambda v, b=b, oslot=oslot, xslot=xslot: v.tensor_add(
                                    out=ob5[:, oslot * 512:(oslot + 1) * 512],
                                    in0=mmb[b][:],
                                    in1=xt5[:, xslot * 512:(xslot + 1) * 512]))
                    bank_free[b] = ("sevv", ev)
                    xused[eidx[0]] = ev
                    wait("gpsimd", "sevv", ev)
                    sv = op_inc("gpsimd", f"sst{oslot}", 16,
                                lambda e, tglob=tglob, dt=dt, oslot=oslot: e.dma_start(
                                    out=out_d[tglob * 128:(tglob + 1) * 128,
                                              dt * DT5:(dt + 1) * DT5],
                                    in_=ob5[:, oslot * 512:(oslot + 1) * 512]))
                    slot_free[eidx[0]] = sv
                    eidx[0] += 1
        for s_ in range(4):
            wait("gpsimd", f"sst{s_}", cnt[f"sst{s_}"])

    import os
    DBG = int(os.environ.get("ATHENA_DBG_PHASES", "9"))
    make_gamma_units()
    make_q3_units()
    p0()
    if DBG >= 1:
        p1()
    if DBG >= 2:
        p3()
    if DBG >= 3:
        gamma_finish()
    if DBG >= 4:
        p4()
    if DBG >= 5:
        p5()
    if DBG < 5:
        for n in ("sevv", "sevs", "smm", "str", "ss", "sgy0", "sgy1",
                  "sgh0", "sgh1", "sgq0", "sgq1"):
            wait("gpsimd", n, cnt[n])

    with nc.Block() as block:
        @block.sync
        def _(e):
            for f in OPS["sync"]:
                f(e)

        @block.scalar
        def _(e):
            for f in OPS["scalar"]:
                f(e)

        @block.vector
        def _(e):
            for f in OPS["vector"]:
                f(e)

        @block.tensor
        def _(e):
            for f in OPS["tensor"]:
                f(e)

        @block.gpsimd
        def _(e):
            for f in OPS["gpsimd"]:
                f(e)

    es.close()
    return nc


def _prep_inputs(L0_attn_x, L0_attn_W_rvk, L0_attn_W_out, L0_attn_time_decay,
                 L0_attn_time_first, L0_ffn_w1, L0_ffn_w2, L0_ffn_w3):
    x = np.ascontiguousarray(np.asarray(L0_attn_x, dtype=np.float32))
    W_rvk = np.asarray(L0_attn_W_rvk, dtype=np.float32)
    W_out = np.asarray(L0_attn_W_out, dtype=np.float32)
    td = np.asarray(L0_attn_time_decay, dtype=np.float32)
    tf = np.asarray(L0_attn_time_first, dtype=np.float32)
    w1 = np.asarray(L0_ffn_w1, dtype=np.float32)
    w2 = np.asarray(L0_ffn_w2, dtype=np.float32)
    w3 = np.asarray(L0_ffn_w3, dtype=np.float32)

    WT = W_rvk.T
    wrvk = np.empty((CCN, 128, 3, CCN, 128), np.float32)
    for j, base in ((0, 2 * C), (1, C), (2, 0)):
        blk = WT[:, base:base + C].reshape(CCN, 128, CCN, 128)
        wrvk[:, :, j, :, :] = blk.transpose(2, 1, 0, 3)
    wrvk = np.ascontiguousarray(wrvk.reshape(CCN, 128, 3 * CCN * 128))

    w12 = np.empty((HHN, 2, 128, C), np.float32)
    for m, w in ((0, w1), (1, w2)):
        t = w.reshape(HHN, 128, CCN, 128)
        w12[:, m, :, :] = t.transpose(0, 3, 2, 1).reshape(HHN, 128, C)
    w12 = np.ascontiguousarray(w12)

    w3t = np.ascontiguousarray(w3.reshape(NDT5, DT5, HHN, 128).transpose(0, 2, 3, 1))

    wo = W_out.reshape(NDT5, DT5, CCN, 128).transpose(0, 3, 2, 1)
    wout = np.ascontiguousarray(wo.reshape(NDT5, 128, CCN * DT5))

    tdt = np.ascontiguousarray(td.reshape(CCN, 128).T)
    tft = np.ascontiguousarray(tf.reshape(CCN, 128).T)
    eye = np.eye(128, dtype=ml_dtypes.bfloat16)

    in_maps = []
    for core in range(8):
        b, h = core // 2, core % 2
        if h == 0:
            xs = np.concatenate([x[b, 0:WIN], x[b, 0:T_OWN]], axis=0)
            mask = np.zeros((128, 1), np.float32)
        else:
            xs = x[b, T_OWN - WIN: 2 * T_OWN]
            mask = np.ones((128, 1), np.float32)
        in_maps.append({
            "x": np.ascontiguousarray(xs),
            "wrvk": wrvk, "w12": w12, "w3t": w3t, "wout": wout,
            "td": tdt, "tf": tft, "mask": mask, "eye": eye,
        })
    return in_maps


def kernel(L0_attn_x, L0_attn_W_rvk, L0_attn_W_out, L0_attn_time_decay,
           L0_attn_time_first, L0_ffn_w1, L0_ffn_w2, L0_ffn_w3):
    in_maps = _prep_inputs(L0_attn_x, L0_attn_W_rvk, L0_attn_W_out,
                           L0_attn_time_decay, L0_attn_time_first,
                           L0_ffn_w1, L0_ffn_w2, L0_ffn_w3)
    nc = build_nc()
    res = run_bass_kernel_spmd(nc, in_maps, core_ids=list(range(8)))
    B, T = 4, 4096
    out = np.empty((B, T, C), np.float32)
    for core in range(8):
        b, h = core // 2, core % 2
        out[b, h * T_OWN:(h + 1) * T_OWN, :] = res.results[core]["out"]
    return out


# version 03
# speedup vs baseline: 1.1524x; 1.0035x over previous
import numpy as np
import ml_dtypes
import contextlib
import concourse.bass as bass
import concourse.mybir as mybir
from concourse.bass_utils import run_bass_kernel_spmd

F32 = mybir.dt.float32
BF16 = mybir.dt.bfloat16
ALU = mybir.AluOpType
ACT = mybir.ActivationFunctionType
AX = mybir.AxisListType

C = 2048
CCN = 16
H = 5632
HHN = 44
WIN = 128
T_OWN = 2048
T_KV = T_OWN + WIN
NBLK = T_KV // 128
EPS_LN = 1e-5
EPS_Q = 1e-8
NW = 5632.0 * 2048.0

SC = [(0, 1024), (1024, 1152)]
SCW = 1152
KV_TT = [(0, 512), (512, 512), (1024, 512), (1536, 512), (2048, 128)]
R_TT = [(WIN, 512), (WIN + 512, 512), (WIN + 1024, 512), (WIN + 1536, 512)]
TT5 = 512
NTT5 = T_OWN // TT5
NTB5 = TT5 // 128
DT5 = 512
NDT5 = C // DT5


def build_nc():
    import os as _os
    _norace = _os.environ.get("ATHENA_SIM_NORACE", "") == "1"
    nc = bass.Bass("TRN2", target_bir_lowering=False,
                   detect_race_conditions=not _norace)

    x_d = nc.dram_tensor("x", [T_KV, C], F32, kind="ExternalInput").ap()
    wrvk_d = nc.dram_tensor("wrvk", [CCN, 128, 3 * CCN * 128], F32, kind="ExternalInput").ap()
    w12_d = nc.dram_tensor("w12", [HHN, 2, 128, C], F32, kind="ExternalInput").ap()
    w3t_d = nc.dram_tensor("w3t", [NDT5, HHN, 128, DT5], F32, kind="ExternalInput").ap()
    wout_d = nc.dram_tensor("wout", [NDT5, 128, CCN * DT5], F32, kind="ExternalInput").ap()
    td_d = nc.dram_tensor("td", [128, CCN], F32, kind="ExternalInput").ap()
    tf_d = nc.dram_tensor("tf", [128, CCN], F32, kind="ExternalInput").ap()
    mask_d = nc.dram_tensor("mask", [128, 1], F32, kind="ExternalInput").ap()
    eye_d = nc.dram_tensor("eye", [128, 128], BF16, kind="ExternalInput").ap()
    out_d = nc.dram_tensor("out", [T_OWN, C], F32, kind="ExternalOutput").ap()
    ysp_d = nc.dram_tensor("ysp", [128, CCN * T_OWN], BF16, kind="Internal").ap()
    hidsp_d = nc.dram_tensor("hidsp", [128, HHN * T_OWN], BF16, kind="Internal").ap()
    q3sp_d = nc.dram_tensor("q3sp", [128, NDT5 * HHN * DT5], BF16, kind="Internal").ap()

    es = contextlib.ExitStack()

    SB_BASE = (nc._sbuf_addr_for_side("left") + 63) // 64 * 64
    SB_TOP = nc._sbuf_addr_for_side("right")
    off = [SB_BASE]

    def sb(name, shape, dtype, at=None):
        nbytes = int(np.prod(shape[1:])) * (2 if dtype == BF16 else 4)
        if at is None:
            at = off[0]
            off[0] = at + ((nbytes + 63) // 64) * 64
        return nc.alloc_sbuf_tensor_at(name, shape, dtype, offset=at), at

    def sba(name, shape, dtype, at=None):
        return sb(name, shape, dtype, at)[0]

    consts = sba("consts", [128, 16], F32)
    dch = sba("dch", [128, CCN], F32)
    euch = sba("euch", [128, CCN], F32)
    tdt = sba("tdt", [128, CCN], F32)
    etd = sba("etd", [128, CCN], F32)
    tft = sba("tft", [128, CCN], F32)
    eye_t = sba("eye", [128, 128], BF16)
    sum_s = sba("sum_s", [128, NBLK], F32)
    sq_s = sba("sq_s", [128, NBLK], F32)
    mean_s = sba("mean_s", [128, NBLK], F32)
    m2_s = sba("m2_s", [128, NBLK], F32)
    var_s = sba("var_s", [128, NBLK], F32)
    std_s = sba("std_s", [128, NBLK], F32)
    rstd_s = sba("rstd_s", [128, NBLK], F32)
    gacc = sba("gacc", [128, 144], F32)
    gred = sba("gred", [128, 8], F32)
    g1 = sba("g1", [1, 8], F32)
    canary = sba("canary", [128, 8], F32)
    ones_t = sba("ones_t", [128, 128], F32)
    sgn3, sgn3_at = sb("sgn3", [128, DT5], BF16)
    q3st = sba("q3st", [128, 2 * DT5], BF16)
    w3qb = sba("w3qb", [128, 2 * DT5], F32)
    tanh4 = sba("tanh4", [128, 2 * 512], F32)
    hidum, hid_at = sb("hidum", [128, C], F32)
    hidb = sba("hidb", [128, 2 * C], BF16, at=hid_at)
    gdum = hidum
    xsh = sba("xsh", [128, 2 * C], F32)
    big0, big0_at = sb("big0", [128, CCN * T_KV], BF16)
    union_at = off[0]
    wkb = sba("wkb", [128, 2 * 3 * CCN * 128], BF16)
    ekt, ekt_at = sb("ekt", [128, 4 * SCW], F32)
    vt = sba("vt", [128, 2 * SCW], F32)
    pbuf = sba("pbuf", [128, SCW + 1], F32)
    qbuf = sba("qbuf", [128, SCW + 1], F32)
    dte = sba("dte", [128, SCW], F32)
    ybf = sba("ybf", [128, 2 * SCW], BF16)
    trt = sba("trt", [128, 2 * T_OWN], F32)
    sgn = sba("sgn", [128, C], BF16)
    u_t = sba("u_t", [128, C], BF16)
    thr = sba("thr", [128, 8], F32)
    gapt = sba("gapt", [128, 128], F32)
    w12x = sba("w12x", [128, C], F32)
    end_at = off[0]
    qb = sba("qb", [128, 4 * C], BF16, at=ekt_at)
    xntok = sba("xntok", [128, 2 * C], BF16, at=ekt_at)
    QDTW = (HHN + CCN) * DT5
    qdt2 = sba("qdt2", [128, 2 * QDTW], BF16, at=big0_at + HHN * TT5 * 2)
    assert big0_at + HHN * TT5 * 2 + 2 * QDTW * 2 <= SB_TOP, "qdt2 overflow"
    y5 = sba("y5", [128, CCN * TT5], BF16, at=sgn3_at)
    xt5 = sba("xt5", [128, 4 * 512], F32, at=sgn3_at + CCN * TT5 * 2)
    ob5 = sba("ob5", [128, 4 * 512], F32, at=sgn3_at + CCN * TT5 * 2 + 4 * 512 * 4)
    assert end_at <= SB_TOP, f"sbuf overflow: {end_at} > {SB_TOP}"

    NMB = 6
    mmb = [es.enter_context(nc.psum_tensor(f"mmb{i}", [128, 512], F32)) for i in range(NMB)]
    ps_tr = [es.enter_context(nc.psum_tensor("ps_tr0", [128, 1024], BF16)),
             es.enter_context(nc.psum_tensor("ps_tr1", [128, 1024], BF16))]

    sem_names = ["sx", "sxA", "sxB", "swgA", "swgB", "sw5", "sws", "sscA", "sscB",
                 "smm", "str", "sevv", "sevs", "sv", "ss",
                 "sgy0", "sgy1", "sgh0", "sgh1", "sgq0", "sgq1",
                 "spx0", "spx1", "spx2", "spx3",
                 "sst0", "sst1", "sst2", "sst3", "sp5", "sth", "sws0", "sws1", "sws2", "sp5b", "sw5b"]
    sems = {n: es.enter_context(nc.semaphore(n)) for n in sem_names}
    cnt = {n: 0 for n in sem_names}
    OPS = {e: [] for e in ["sync", "scalar", "vector", "tensor", "gpsimd"]}

    def emit(eng, fn):
        OPS[eng].append(fn)

    def op_inc(eng, name, by, fn):
        cnt[name] += by
        v = cnt[name]
        sem = sems[name]
        emit(eng, lambda e, fn=fn, sem=sem, by=by: fn(e).then_inc(sem, by))
        return v

    def wait(eng, name, v):
        if v <= 0:
            return
        sem = sems[name]
        emit(eng, lambda e, sem=sem, v=v: e.wait_ge(sem, v))

    def vgap():
        emit("vector", lambda v: v.memset(gapt[:, 0:64], 0.0))

    def sgap():
        emit("scalar", lambda s: s.memzero(gapt[:, 64:128]))

    bank_free = [("sevv", 0)] * NMB
    bank_i = [0]

    def next_bank():
        b = bank_i[0] % NMB
        bank_i[0] += 1
        return b

    ST = {}

    def p0():
        op_inc("sync", "sx", 16, lambda e: e.dma_start(out=tdt[:], in_=td_d[:]))
        op_inc("sync", "sx", 16, lambda e: e.dma_start(out=tft[:], in_=tf_d[:]))
        op_inc("sync", "sx", 16, lambda e: e.dma_start(out=consts[:, 4:5], in_=mask_d[:]))
        op_inc("sync", "sx", 16, lambda e: e.dma_start(out=eye_t[:], in_=eye_d[:]))
        emit("vector", lambda v: v.memset(consts[:, 0:1], EPS_LN))
        emit("vector", lambda v: v.memset(consts[:, 1:2], 0.5))
        emit("vector", lambda v: v.memset(consts[:, 2:3], -1.0))
        emit("vector", lambda v: v.memset(consts[:, 3:4], 0.0))
        emit("vector", lambda v: v.memset(consts[:, 5:6], 0.25))
        emit("vector", lambda v: v.memset(pbuf[:, 0:1], 0.0))
        emit("vector", lambda v: v.memset(qbuf[:, 0:1], 0.0))
        emit("vector", lambda v: v.memset(gacc[:], 0.0))
        vv = op_inc("vector", "sv", 1, lambda v: v.memset(ones_t[:], 1.0))
        wait("scalar", "sx", 64)
        wait("scalar", "sv", vv)
        emit("scalar", lambda s: s.activation(out=etd[:], in_=tdt[:], func=ACT.Exp))
        op_inc("scalar", "ss", 1, lambda s: s.activation(out=euch[:], in_=tft[:], func=ACT.Exp))
        sgap()
        op_inc("scalar", "ss", 1, lambda s: s.activation(out=dch[:], in_=etd[:], func=ACT.Exp,
                                                         scale=consts[:, 2:3]))

    xn_done = {}
    sq_done = {}
    tr_last = {}
    tr_evac = {}
    trc = [0]

    def emit_xn(b):
        par = b % 2
        if b >= 2:
            wait("vector", "str", tr_last[b - 2])
        xv = op_inc("vector", "sevv", 1,
                    lambda v, b=b, par=par: v.tensor_scalar(
                        out=xntok[:, par * C:(par + 1) * C], in0=xsh[:, par * C:(par + 1) * C],
                        scalar1=mean_s[:, b:b + 1], scalar2=rstd_s[:, b:b + 1],
                        op0=ALU.subtract, op1=ALU.mult))
        xn_done[b] = xv
        wait("tensor", "sevv", xv)
        for g4 in range(CCN // 4):
            g = trc[0]
            bank = g % 2
            if g >= 2:
                wait("tensor", "sevv", tr_evac[g - 2])
            tv = 0
            for i in range(4):
                cc = g4 * 4 + i
                inst = [None]
                tv = op_inc("tensor", "str", 1,
                            lambda t, par=par, cc=cc, i=i, bank=bank: t.transpose(
                                ps_tr[bank][:, i * 256:(i + 1) * 256][:, 0:128]
                                if False else
                                ps_tr[bank][:, i * 128 + 0: i * 128 + 128],
                                xntok[:, par * C + cc * 128: par * C + (cc + 1) * 128],
                                eye_t[:]))
            wait("vector", "str", tv)
            ev = 0
            for i in range(4):
                cc = g4 * 4 + i
                ev = op_inc("vector", "sevv", 1,
                            lambda v, b=b, cc=cc, i=i, bank=bank: v.tensor_copy(
                                out=big0[:, cc * T_KV + b * 128: cc * T_KV + (b + 1) * 128],
                                in_=ps_tr[bank][:, i * 128: i * 128 + 128]))
            tr_evac[g] = ev
            trc[0] += 1
        tr_last[b] = cnt["str"]

    def p1():
        for b in range(NBLK):
            par = b % 2
            if b >= 2:
                wait("sync", "sevv", xn_done[b - 2])
                wait("sync", "sevs", sq_done[b - 2])
            sxp = "sxA" if par == 0 else "sxB"
            lv = op_inc("sync", sxp, 16,
                        lambda e, b=b, par=par: e.dma_start(
                            out=xsh[:, par * C:(par + 1) * C], in_=x_d[b * 128:(b + 1) * 128, :]))
            wait("vector", sxp, lv)
            rv = op_inc("vector", "sevv", 1,
                        lambda v, b=b, par=par: v.reduce_sum(
                            out=sum_s[:, b:b + 1], in_=xsh[:, par * C:(par + 1) * C], axis=AX.X))
            wait("scalar", sxp, lv)
            sqv = op_inc("scalar", "sevs", 1,
                         lambda s, b=b, par=par: s.activation(
                             out=gdum[:], in_=xsh[:, par * C:(par + 1) * C],
                             func=ACT.Square, accum_out=sq_s[:, b:b + 1]))
            sq_done[b] = sqv
            wait("scalar", "sevv", rv)
            emit("scalar", lambda s, b=b: s.mul(out=mean_s[:, b:b + 1], in_=sum_s[:, b:b + 1],
                                                mul=1.0 / C))
            sgap()
            mv = op_inc("scalar", "sevs", 1,
                        lambda s, b=b: s.activation(out=m2_s[:, b:b + 1], in_=mean_s[:, b:b + 1],
                                                    func=ACT.Square))
            wait("vector", "sevs", mv)
            vv = op_inc("vector", "sevv", 1,
                        lambda v, b=b: v.scalar_tensor_tensor(
                            out=var_s[:, b:b + 1], in0=sq_s[:, b:b + 1], scalar=1.0 / C,
                            in1=m2_s[:, b:b + 1], op0=ALU.mult, op1=ALU.subtract))
            wait("scalar", "sevv", vv)
            sv_ = op_inc("scalar", "sevs", 1,
                         lambda s, b=b: s.activation(out=std_s[:, b:b + 1], in_=var_s[:, b:b + 1],
                                                     func=ACT.Sqrt, bias=consts[:, 0:1]))
            wait("vector", "sevs", sv_)
            emit("vector", lambda v, b=b: v.reciprocal(out=rstd_s[:, b:b + 1],
                                                       in_=std_s[:, b:b + 1]))
            if b > 0:
                emit_xn(b - 1)
        emit("vector", lambda v: v.memset(consts[:, 9:10], 0.0))
        emit_xn(NBLK - 1)

    gunits = []

    def make_gamma_units():
        w3flat = w3t_d.rearrange("a b p f -> (a b) p f")
        for blk in range(NDT5 * HHN // 4):
            srcs = [w3flat[4 * blk + i, :, :] for i in range(4)]
            gunits.append((srcs, DT5, 96 + blk))
        for hh in range(HHN):
            for m in range(2):
                gunits.append(([w12_d[hh, m, :, :]], C, m * 48 + hh))

    gstate = {"load": 0, "use": 0, "loadv": {}}

    def pump_gamma(n):
        for _ in range(n):
            for _ld in range(2):
                if gstate["load"] < len(gunits) and gstate["load"] - gstate["use"] < 2:
                    i = gstate["load"]
                    srcs, ncols, col = gunits[i]
                    par = i % 2
                    if i == 0:
                        wait("scalar", "sevv", xn_done[NBLK - 1])
                    lv = 0
                    for si, s_ap in enumerate(srcs):
                        lv = op_inc("scalar", "sscA" if par == 0 else "sscB", 16,
                                    lambda e, s_ap=s_ap, ncols=ncols, par=par, si=si:
                                    e.dma_start(out=xsh[:, par * C + si * ncols:
                                                        par * C + (si + 1) * ncols], in_=s_ap))
                    gstate["loadv"][i] = lv
                    gstate["load"] += 1
            if gstate["use"] < gstate["load"]:
                i = gstate["use"]
                srcs, ncols, col = gunits[i]
                tot = ncols * len(srcs)
                par = i % 2
                wait("scalar", "sscA" if par == 0 else "sscB", gstate["loadv"][i])
                op_inc("scalar", "ss", 1,
                       lambda s, tot=tot, col=col, par=par: s.activation(
                           out=gdum[:, 0:tot], in_=xsh[:, par * C: par * C + tot],
                           func=ACT.Abs, accum_out=gacc[:, col:col + 1]))
                gstate["use"] += 1

    def gamma_finish():
        pump_gamma(len(gunits))
        ST["gamma_ss"] = cnt["ss"]
        wait("vector", "ss", cnt["ss"])
        rv = 0
        for m in range(3):
            rv = op_inc("vector", "sevv", 1,
                        lambda v, m=m: v.reduce_sum(out=gred[:, m:m + 1],
                                                    in_=gacc[:, m * 48:(m + 1) * 48], axis=AX.X))
        wait("tensor", "sevv", rv)
        b1 = next_bank()
        wait("tensor", *bank_free[b1])
        mv = op_inc("tensor", "smm", 1,
                    lambda t, b1=b1: t.matmul(mmb[b1][0:1, 0:3], ones_t[:, 0:1], gred[:, 0:3],
                                              start=True, stop=True))
        wait("vector", "smm", mv)
        cv = op_inc("vector", "sevv", 1, lambda v, b1=b1: v.tensor_copy(
            out=g1[0:1, 0:3], in_=mmb[b1][0:1, 0:3]))
        bank_free[b1] = ("sevv", cv)
        wait("tensor", "sevv", cv)
        b2 = next_bank()
        wait("tensor", *bank_free[b2])
        mv2 = op_inc("tensor", "smm", 1,
                     lambda t, b2=b2: t.matmul(mmb[b2][:, 4:7], ones_t[0:1, :], g1[0:1, 0:3],
                                               start=True, stop=True))
        wait("vector", "smm", mv2)
        tv_ = op_inc("vector", "sth", 1,
                     lambda v, b2=b2: v.tensor_scalar(out=thr[:, 0:3], in0=mmb[b2][:, 4:7],
                                                      scalar1=0.5 / NW, scalar2=0.5 * EPS_Q,
                                                      op0=ALU.mult, op1=ALU.add))
        bank_free[b2] = ("sth", tv_)
        vgap()
        vgap()
        ST["thr_done"] = op_inc("vector", "sth", 1,
                                lambda v: v.tensor_scalar(out=thr[:, 3:6], in0=thr[:, 0:3],
                                                          scalar1=-1.0, scalar2=None,
                                                          op0=ALU.mult))


    last_q = {"v": 0}

    def quantize_sgn(src_ap, dst_ap, m, sg_ap):
        wait("scalar", "sevv", last_q["v"])
        op_inc("scalar", "sevs", 1,
               lambda s, src_ap=src_ap, sg_ap=sg_ap, m=m: s.activation(
                   out=sg_ap, in_=src_ap, func=ACT.Sign, bias=thr[:, 3 + m:4 + m]))
        a2 = op_inc("scalar", "sevs", 1,
                    lambda s, src_ap=src_ap, dst_ap=dst_ap, m=m: s.activation(
                        out=dst_ap, in_=src_ap, func=ACT.Sign, bias=thr[:, m:m + 1]))
        wait("vector", "sevs", a2)
        qv = op_inc("vector", "sevv", 1,
                    lambda v, dst_ap=dst_ap, sg_ap=sg_ap: v.tensor_add(
                        out=dst_ap, in0=dst_ap, in1=sg_ap))
        last_q["v"] = qv
        return qv

    def quantize_vec(src_ap, dst_ap, m):
        op_inc("vector", "sevv", 1,
               lambda v, src_ap=src_ap, m=m: v.tensor_scalar(
                   out=u_t[:, 0:src_ap.shape[1]], in0=src_ap, scalar1=thr[:, 3 + m:4 + m],
                   scalar2=None, op0=ALU.is_le))
        vgap()
        return op_inc("vector", "sevv", 1,
                      lambda v, src_ap=src_ap, dst_ap=dst_ap, m=m: v.scalar_tensor_tensor(
                          out=dst_ap, in0=src_ap, scalar=thr[:, m:m + 1],
                          in1=u_t[:, 0:src_ap.shape[1]], op0=ALU.is_ge, op1=ALU.subtract))

    q3units = []

    def make_q3_units():
        for dt in range(NDT5):
            for hh in range(HHN):
                q3units.append((dt, hh))

    qstate = {"load": 0, "use": 0, "loadv": {}, "spw": {}}

    def pump_q3(n):
        for _ in range(n):
            for _ld in range(2):
                if qstate["load"] < len(q3units) and qstate["load"] - qstate["use"] < 2:
                    i = qstate["load"]
                    dt, hh = q3units[i]
                    par = i % 2
                    if i == 0:
                        wait("scalar", "sth", ST["thr_done"])
                    if i >= 2:
                        wait("scalar", "sgq0" if i % 2 == 0 else "sgq1", qstate["spw"][i - 2])
                    qstate["loadv"][i] = op_inc("scalar", "sscA" if par == 0 else "sscB", 16,
                                                lambda e, dt=dt, hh=hh, par=par: e.dma_start(
                                                    out=w3qb[:, par * DT5:(par + 1) * DT5],
                                                    in_=w3t_d[dt, hh, :, :]))
                    qstate["load"] += 1
            if qstate["use"] < qstate["load"]:
                i = qstate["use"]
                dt, hh = q3units[i]
                par = i % 2
                wait("scalar", "sscA" if par == 0 else "sscB", qstate["loadv"][i])
                qv = quantize_sgn(w3qb[:, par * DT5:(par + 1) * DT5],
                                  q3st[:, par * DT5:(par + 1) * DT5], 2, sgn3[:])
                wait("gpsimd", "sevv", qv)
                qstate["spw"][i] = op_inc("gpsimd", "sgq0" if i % 2 == 0 else "sgq1", 16,
                                          lambda e, dt=dt, hh=hh, par=par: e.dma_start(
                                              out=q3sp_d[:, (dt * HHN + hh) * DT5:
                                                         (dt * HHN + hh + 1) * DT5],
                                              in_=q3st[:, par * DT5:(par + 1) * DT5]))
                qstate["use"] += 1

    cons = {}
    ysp_done = {}

    def p3():
        import os as _os
        NCC = int(_os.environ.get("ATHENA_DBG_CC", str(CCN)))
        SKIP_SCAN = _os.environ.get("ATHENA_DBG_NOSCAN", "") == "1"
        NWKB = 3 * CCN * 128
        wk_used = {}
        for cc in range(NCC):
            par = cc % 2
            if cc >= 2:
                wait("gpsimd", "smm", wk_used[cc - 2])
            swp = "swgA" if par == 0 else "swgB"
            lw = op_inc("gpsimd", swp, 16,
                        lambda e, cc=cc, par=par: e.dma_start(
                            out=wkb[:, par * NWKB:(par + 1) * NWKB], in_=wrvk_d[cc, :, :]))
            wait("tensor", swp, lw)

            for j, tts in ((0, KV_TT), (1, KV_TT), (2, R_TT)):
                for (t0, tl) in tts:
                    b = next_bank()
                    wait("tensor", *bank_free[b])
                    for c in range(CCN - 1):
                        lhs = wkb[:, par * NWKB + (j * CCN + c) * 128:
                                  par * NWKB + (j * CCN + c) * 128 + 128]
                        rhs = big0[:, c * T_KV + t0: c * T_KV + t0 + tl]
                        emit("tensor", lambda t, lhs=lhs, rhs=rhs, b=b, c=c, tl=tl:
                             t.matmul(mmb[b][:, 0:tl], lhs, rhs,
                                      start=(c == 0), stop=False))
                    c = CCN - 1
                    lhs = wkb[:, par * NWKB + (j * CCN + c) * 128:
                              par * NWKB + (j * CCN + c) * 128 + 128]
                    rhs = big0[:, c * T_KV + t0: c * T_KV + t0 + tl]
                    gv = op_inc("tensor", "smm", 1,
                                lambda t, lhs=lhs, rhs=rhs, b=b, tl=tl:
                                t.matmul(mmb[b][:, 0:tl], lhs, rhs, start=False, stop=True))
                    wk_used[cc] = gv
                    pump_gamma(1)
                    ch = 0 if t0 < SC[1][0] else 1
                    c0 = SC[ch][0]
                    if j == 0:
                        ep = (cc % 2) * 2 * SCW
                        dst = ekt[:, ep + ch * SCW + (t0 - c0): ep + ch * SCW + (t0 - c0) + tl]
                        wait("scalar", "smm", gv)
                        if cc >= 2:
                            wait("scalar", "sevv", cons[cc - 2])
                        ev = op_inc("scalar", "sevs", 1,
                                    lambda s, dst=dst, b=b, tl=tl: s.activation(
                                        out=dst, in_=mmb[b][:, 0:tl], func=ACT.Exp))
                        bank_free[b] = ("sevs", ev)
                    elif j == 1:
                        dst = vt[:, ch * SCW + (t0 - c0): ch * SCW + (t0 - c0) + tl]
                        wait("vector", "smm", gv)
                        ev = op_inc("vector", "sevv", 1,
                                    lambda v, dst=dst, b=b, tl=tl: v.tensor_copy(
                                        out=dst, in_=mmb[b][:, 0:tl]))
                        bank_free[b] = ("sevv", ev)
                    else:
                        o0 = (cc % 2) * T_OWN + t0 - WIN
                        wait("scalar", "smm", gv)
                        if cc >= 2:
                            wait("scalar", "sevv", cons[cc - 2])
                        ev = op_inc("scalar", "sevs", 1,
                                    lambda s, o0=o0, b=b, tl=tl: s.activation(
                                        out=trt[:, o0:o0 + tl], in_=mmb[b][:, 0:tl],
                                        func=ACT.Tanh, scale=consts[:, 1:2]))
                        bank_free[b] = ("sevs", ev)

            wait("vector", "sevs", cnt["sevs"])
            if SKIP_SCAN:
                cons[cc] = op_inc("vector", "sevv", 1, lambda v: v.memset(consts[:, 8:9], 0.0))
                for _ch in range(2):
                    k_ = cc * 2 + _ch
                    ysp_done[k_] = cnt["sgy0"] if k_ % 2 == 0 else cnt["sgy1"]
                continue
            ep = (cc % 2) * 2 * SCW
            tp = (cc % 2) * T_OWN
            emit("vector", lambda v, ep=ep: v.tensor_scalar(
                out=ekt[:, ep: ep + WIN], in0=ekt[:, ep: ep + WIN],
                scalar1=consts[:, 4:5], scalar2=None, op0=ALU.mult))
            vgap()
            for ch, (c0, cl) in enumerate(SC):
                ek_c = ekt[:, ep + ch * SCW: ep + ch * SCW + cl]
                v_c = vt[:, ch * SCW: ch * SCW + cl]
                pb = pbuf[:]
                qbv = qbuf[:]
                d_bc = dch[:, cc:cc + 1].broadcast_to([128, cl])
                oo, ol = (WIN, 1024 - WIN) if ch == 0 else (0, cl)
                own_off = tp + (0 if ch == 0 else 1024 - WIN)
                init_q = consts[:, 3:4] if ch == 0 else qbv[:, 0:1]
                init_p = consts[:, 3:4] if ch == 0 else pb[:, 0:1]
                emit("vector", lambda v, qbv=qbv, d_bc=d_bc, ek_c=ek_c, cl=cl, init_q=init_q:
                     v.tensor_tensor_scan(out=qbv[:, 1:1 + cl], data0=d_bc, data1=ek_c,
                                          initial=init_q, op0=ALU.mult, op1=ALU.add))
                emit("vector", lambda v, own_off=own_off, ol=ol: v.tensor_scalar(
                    out=trt[:, own_off:own_off + ol], in0=trt[:, own_off:own_off + ol],
                    scalar1=0.5, scalar2=0.5, op0=ALU.mult, op1=ALU.add))
                emit("vector", lambda v, ek_c=ek_c, qbv=qbv, cc=cc, oo=oo, ol=ol:
                     v.scalar_tensor_tensor(out=dte[:, 0:ol], in0=ek_c[:, oo:oo + ol],
                                            scalar=euch[:, cc:cc + 1],
                                            in1=qbv[:, oo:oo + ol] if ol != 1024 - WIN
                                            else qbv[:, 1 + oo - 1: 1 + oo - 1 + ol],
                                            op0=ALU.mult, op1=ALU.add))
                if ch + 1 < len(SC):
                    emit("vector", lambda v, qbv=qbv, cl=cl: v.tensor_copy(
                        out=qbuf[:, 0:1], in_=qbv[:, cl:cl + 1]))
                emit("vector", lambda v, ek_c=ek_c, v_c=v_c: v.tensor_mul(
                    out=ek_c, in0=ek_c, in1=v_c))
                vgap()
                emit("vector", lambda v, pb=pb, d_bc=d_bc, ek_c=ek_c, cl=cl, init_p=init_p:
                     v.tensor_tensor_scan(out=pb[:, 1:1 + cl], data0=d_bc, data1=ek_c,
                                          initial=init_p, op0=ALU.mult, op1=ALU.add))
                emit("vector", lambda v, ol=ol: v.reciprocal(
                    out=dte[:, 0:ol], in_=dte[:, 0:ol]))
                emit("vector", lambda v, v_c=v_c, ek_c=ek_c, pb=pb, cc=cc, oo=oo, ol=ol:
                     v.scalar_tensor_tensor(out=v_c[:, 0:ol], in0=ek_c[:, oo:oo + ol],
                                            scalar=euch[:, cc:cc + 1],
                                            in1=pb[:, oo:oo + ol] if ol != 1024 - WIN
                                            else pb[:, 1 + oo - 1: 1 + oo - 1 + ol],
                                            op0=ALU.mult, op1=ALU.add))
                if ch + 1 < len(SC):
                    emit("vector", lambda v, pb=pb, cl=cl: v.tensor_copy(
                        out=pbuf[:, 0:1], in_=pb[:, cl:cl + 1]))
                vgap()
                emit("vector", lambda v, qbv=qbv, v_c=v_c, ol=ol: v.tensor_mul(
                    out=qbv[:, 1:1 + ol], in0=v_c[:, 0:ol], in1=dte[:, 0:ol]))
                k = cc * 2 + ch
                ypar = k % 2
                if k >= 2:
                    wait("vector", "sgy0" if ypar == 0 else "sgy1", ysp_done[k - 2])
                vgap()
                yv = op_inc("vector", "sevv", 1,
                            lambda v, own_off=own_off, ol=ol, qbv=qbv, ypar=ypar:
                            v.tensor_mul(out=ybf[:, ypar * SCW: ypar * SCW + ol],
                                         in0=trt[:, own_off:own_off + ol],
                                         in1=qbv[:, 1:1 + ol]))
                wait("gpsimd", "sevv", yv)
                ysp_done[k] = op_inc("gpsimd", "sgy0" if ypar == 0 else "sgy1", 16,
                                     lambda e, cc=cc, own_off=own_off, tp=tp, ol=ol, ypar=ypar:
                                     e.dma_start(
                                         out=ysp_d[:, cc * T_OWN + own_off - tp:
                                                   cc * T_OWN + own_off - tp + ol],
                                         in_=ybf[:, ypar * SCW: ypar * SCW + ol]))
            cons[cc] = op_inc("vector", "sevv", 1, lambda v: v.memset(consts[:, 8:9], 0.0))


    hid_ev = {}
    hid_spill = {}
    mm_after_hh = {}

    def p4():
        wait("sync", "ss", ST["gamma_ss"])
        q_ready = {}
        qdone = {}
        wslot = {}
        slots = [xsh[:, 0:C], xsh[:, C:2 * C], w12x[:]]
        for hh in range(HHN):
            par = hh % 2
            for m in (0, 1):
                i = 2 * hh + m
                s_ = i % 3
                if i >= 3:
                    wait("sync", "sevv", qdone[i - 3])
                wslot[i] = op_inc("sync", f"sws{s_}", 16,
                                  lambda e, hh=hh, m=m, s_=s_: e.dma_start(
                                      out=slots[s_], in_=w12_d[hh, m, :, :]))
            if hh >= 2:
                wait("scalar", "smm", mm_after_hh[hh - 2])
                wait("vector", "smm", mm_after_hh[hh - 2])
            s1_, s2_ = (2 * hh) % 3, (2 * hh + 1) % 3
            wait("scalar", f"sws{s1_}", wslot[2 * hh])
            if hh == 0:
                wait("scalar", "sth", ST["thr_done"])
            qdone[2 * hh] = quantize_sgn(slots[s1_], qb[:, (par * 2) * C:(par * 2 + 1) * C],
                                         0, sgn[:])
            wait("vector", f"sws{s2_}", wslot[2 * hh + 1])
            qv = quantize_vec(slots[s2_], qb[:, (par * 2 + 1) * C:(par * 2 + 2) * C], 1)
            q_ready[hh] = qv
            qdone[2 * hh + 1] = qv
            wait("tensor", "sevv", qv)
            for ti, (t0, tl) in enumerate(R_TT):
                b1 = next_bank()
                wait("tensor", *bank_free[b1])
                for c in range(CCN - 1):
                    lhs = qb[:, (par * 2) * C + c * 128:(par * 2) * C + (c + 1) * 128]
                    rhs = big0[:, c * T_KV + t0: c * T_KV + t0 + tl]
                    emit("tensor", lambda t, lhs=lhs, rhs=rhs, b1=b1, c=c, tl=tl:
                         t.matmul(mmb[b1][:, 0:tl], lhs, rhs, start=(c == 0), stop=False))
                c = CCN - 1
                lhs = qb[:, (par * 2) * C + c * 128:(par * 2) * C + (c + 1) * 128]
                rhs = big0[:, c * T_KV + t0: c * T_KV + t0 + tl]
                g1v = op_inc("tensor", "smm", 1,
                             lambda t, lhs=lhs, rhs=rhs, b1=b1, tl=tl:
                             t.matmul(mmb[b1][:, 0:tl], lhs, rhs, start=False, stop=True))
                b2 = next_bank()
                wait("tensor", *bank_free[b2])
                for c in range(CCN - 1):
                    lhs = qb[:, (par * 2 + 1) * C + c * 128:(par * 2 + 1) * C + (c + 1) * 128]
                    rhs = big0[:, c * T_KV + t0: c * T_KV + t0 + tl]
                    emit("tensor", lambda t, lhs=lhs, rhs=rhs, b2=b2, c=c, tl=tl:
                         t.matmul(mmb[b2][:, 0:tl], lhs, rhs, start=(c == 0), stop=False))
                c = CCN - 1
                lhs = qb[:, (par * 2 + 1) * C + c * 128:(par * 2 + 1) * C + (c + 1) * 128]
                rhs = big0[:, c * T_KV + t0: c * T_KV + t0 + tl]
                g2v = op_inc("tensor", "smm", 1,
                             lambda t, lhs=lhs, rhs=rhs, b2=b2, tl=tl:
                             t.matmul(mmb[b2][:, 0:tl], lhs, rhs, start=False, stop=True))
                mm_after_hh[hh] = g2v
                k = hh * 4 + ti
                tpar = ti % 2
                wait("scalar", "smm", g1v)
                if k >= 2:
                    wait("scalar", "sevv", hid_ev[k - 2])
                e1 = op_inc("scalar", "sevs", 1,
                            lambda s, b1=b1, tpar=tpar, tl=tl: s.activation(
                                out=tanh4[:, tpar * 512: tpar * 512 + tl],
                                in_=mmb[b1][:, 0:tl], func=ACT.Tanh, scale=consts[:, 5:6]))
                bank_free[b1] = ("sevs", e1)
                o0 = t0 - WIN
                hpar = ti % 2
                wait("vector", "sevs", e1)
                wait("vector", "smm", g2v)
                if k >= 2:
                    wait("vector", "sgh0" if k % 2 == 0 else "sgh1", hid_spill[k - 2])
                emit("vector", lambda v, tpar=tpar, hpar=hpar, tl=tl: v.tensor_scalar(
                    out=hidb[:, hpar * C: hpar * C + tl],
                    in0=tanh4[:, tpar * 512: tpar * 512 + tl],
                    scalar1=0.25, scalar2=0.25, op0=ALU.mult, op1=ALU.add))
                vgap()
                e3 = op_inc("vector", "sevv", 1,
                            lambda v, b2=b2, hpar=hpar, tl=tl: v.tensor_mul(
                                out=hidb[:, hpar * C: hpar * C + tl],
                                in0=hidb[:, hpar * C: hpar * C + tl], in1=mmb[b2][:, 0:tl]))
                hid_ev[k] = e3
                bank_free[b2] = ("sevv", e3)
                wait("gpsimd", "sevv", e3)
                hid_spill[k] = op_inc("gpsimd", "sgh0" if k % 2 == 0 else "sgh1", 16,
                                      lambda e, hh=hh, o0=o0, tl=tl, hpar=hpar: e.dma_start(
                                          out=hidsp_d[:, hh * T_OWN + o0:
                                                      hh * T_OWN + o0 + tl],
                                          in_=hidb[:, hpar * C: hpar * C + tl]))
                pump_q3(1)
        pump_q3(len(q3units))
        ST["hid_sg"] = (cnt["sgh0"], cnt["sgh1"])
        ST["q3_sg"] = (cnt["sgq0"], cnt["sgq1"])
        ST["y_sg"] = (cnt["sgy0"], cnt["sgy1"])
        ST["p4_mm"] = cnt["smm"]

    def p5():
        hoff = 0
        mm_tti = {}
        mm_g = {}
        xused = {}
        slot_free = {}
        eidx = [0]
        for tti in range(NTT5):
            wait("sync", "sgh0", ST["hid_sg"][0])
            wait("sync", "sgh1", ST["hid_sg"][1])
            wait("sync", "sgq0", ST["q3_sg"][0])
            wait("sync", "sgq1", ST["q3_sg"][1])
            wait("sync", "smm", mm_tti.get(tti - 1, ST["p4_mm"]))
            for g0 in range(0, HHN, 11):
                src5 = hidsp_d.rearrange("p (a f) -> p a f", a=HHN)[
                    :, g0:g0 + 11, tti * TT5:(tti + 1) * TT5]
                dst5 = big0[:, hoff + g0 * TT5: hoff + (g0 + 11) * TT5].rearrange(
                    "p (a f) -> p a f", a=11)
                op_inc("sync", "sp5", 16,
                       lambda e, src5=src5, dst5=dst5: e.dma_start(out=dst5, in_=src5))
            wait("sync", "sgy0", ST["y_sg"][0])
            wait("sync", "sgy1", ST["y_sg"][1])
            ysrc = ysp_d.rearrange("p (a f) -> p a f", a=CCN)[:, :, tti * TT5:(tti + 1) * TT5]
            ydst = y5[:].rearrange("p (a f) -> p a f", a=CCN)
            op_inc("sync", "sp5", 16,
                   lambda e, ysrc=ysrc, ydst=ydst: e.dma_start(out=ydst, in_=ysrc))
            for dt in range(NDT5):
                g = tti * NDT5 + dt
                slot = g % 2
                qsem = "sp5" if slot == 0 else "sp5b"
                wsem = "sw5" if slot == 0 else "sw5b"
                prev = mm_g.get(g - 2, ST["p4_mm"])
                wait("sync", "smm", prev)
                wait("gpsimd", "smm", prev)
                ql = op_inc("sync", qsem, 16,
                            lambda e, dt=dt, slot=slot: e.dma_start(
                                out=qdt2[:, slot * QDTW: slot * QDTW + HHN * DT5],
                                in_=q3sp_d[:, dt * HHN * DT5:(dt + 1) * HHN * DT5]))
                wl = op_inc("gpsimd", wsem, 16,
                            lambda e, dt=dt, slot=slot: e.dma_start(
                                out=qdt2[:, slot * QDTW + HHN * DT5: (slot + 1) * QDTW],
                                in_=wout_d[dt, :, :]))
                for tb in range(NTB5):
                    tglob = tti * NTB5 + tb
                    xslot = eidx[0] % 4
                    if eidx[0] >= 4:
                        wait("sync", "sevv", xused[eidx[0] - 4])
                    xlv = op_inc("sync", f"spx{xslot}", 16,
                                 lambda e, tglob=tglob, dt=dt, xslot=xslot: e.dma_start(
                                     out=xt5[:, xslot * 512:(xslot + 1) * 512],
                                     in_=x_d[WIN + tglob * 128: WIN + (tglob + 1) * 128,
                                             dt * DT5:(dt + 1) * DT5]))
                    ST[("xl", eidx[0])] = xlv
                    b = next_bank()
                    wait("tensor", *bank_free[b])
                    wait("tensor", qsem, ql)
                    wait("tensor", "sp5", cnt["sp5"])
                    wait("tensor", wsem, wl)
                    for hh in range(HHN):
                        lhs = big0[:, hoff + hh * TT5 + tb * 128:
                                   hoff + hh * TT5 + (tb + 1) * 128]
                        rhs = qdt2[:, slot * QDTW + hh * DT5: slot * QDTW + (hh + 1) * DT5]
                        emit("tensor", lambda t, lhs=lhs, rhs=rhs, b=b, first=(hh == 0):
                             t.matmul(mmb[b][:], lhs, rhs, start=first, stop=False))
                    for cc in range(CCN - 1):
                        lhs = y5[:, cc * TT5 + tb * 128: cc * TT5 + (tb + 1) * 128]
                        rhs = qdt2[:, slot * QDTW + (HHN + cc) * DT5:
                                   slot * QDTW + (HHN + cc + 1) * DT5]
                        emit("tensor", lambda t, lhs=lhs, rhs=rhs, b=b:
                             t.matmul(mmb[b][:], lhs, rhs, start=False, stop=False))
                    cc = CCN - 1
                    lhs = y5[:, cc * TT5 + tb * 128: cc * TT5 + (tb + 1) * 128]
                    rhs = qdt2[:, slot * QDTW + (HHN + cc) * DT5:
                               slot * QDTW + (HHN + cc + 1) * DT5]
                    gv = op_inc("tensor", "smm", 1,
                                lambda t, lhs=lhs, rhs=rhs, b=b:
                                t.matmul(mmb[b][:], lhs, rhs, start=False, stop=True))
                    mm_g[g] = gv
                    mm_tti[tti] = gv
                    oslot = eidx[0] % 4
                    wait("vector", "smm", gv)
                    wait("vector", f"spx{xslot}", ST[("xl", eidx[0])])
                    if eidx[0] >= 4:
                        wait("vector", f"sst{oslot}", slot_free[eidx[0] - 4])
                    ev = op_inc("vector", "sevv", 1,
                                lambda v, b=b, oslot=oslot, xslot=xslot: v.tensor_add(
                                    out=ob5[:, oslot * 512:(oslot + 1) * 512],
                                    in0=mmb[b][:],
                                    in1=xt5[:, xslot * 512:(xslot + 1) * 512]))
                    bank_free[b] = ("sevv", ev)
                    xused[eidx[0]] = ev
                    wait("gpsimd", "sevv", ev)
                    sv = op_inc("gpsimd", f"sst{oslot}", 16,
                                lambda e, tglob=tglob, dt=dt, oslot=oslot: e.dma_start(
                                    out=out_d[tglob * 128:(tglob + 1) * 128,
                                              dt * DT5:(dt + 1) * DT5],
                                    in_=ob5[:, oslot * 512:(oslot + 1) * 512]))
                    slot_free[eidx[0]] = sv
                    eidx[0] += 1
        for s_ in range(4):
            wait("gpsimd", f"sst{s_}", cnt[f"sst{s_}"])

    import os
    DBG = int(os.environ.get("ATHENA_DBG_PHASES", "9"))
    make_gamma_units()
    make_q3_units()
    p0()
    if DBG >= 1:
        p1()
    if DBG >= 2:
        p3()
    if DBG >= 3:
        gamma_finish()
    if DBG >= 4:
        p4()
    if DBG >= 5:
        p5()
    if DBG < 5:
        for n in ("sevv", "sevs", "smm", "str", "ss", "sgy0", "sgy1",
                  "sgh0", "sgh1", "sgq0", "sgq1"):
            wait("gpsimd", n, cnt[n])

    with nc.Block() as block:
        @block.sync
        def _(e):
            for f in OPS["sync"]:
                f(e)

        @block.scalar
        def _(e):
            for f in OPS["scalar"]:
                f(e)

        @block.vector
        def _(e):
            for f in OPS["vector"]:
                f(e)

        @block.tensor
        def _(e):
            for f in OPS["tensor"]:
                f(e)

        @block.gpsimd
        def _(e):
            for f in OPS["gpsimd"]:
                f(e)

    es.close()
    return nc


def _prep_inputs(L0_attn_x, L0_attn_W_rvk, L0_attn_W_out, L0_attn_time_decay,
                 L0_attn_time_first, L0_ffn_w1, L0_ffn_w2, L0_ffn_w3):
    x = np.ascontiguousarray(np.asarray(L0_attn_x, dtype=np.float32))
    W_rvk = np.asarray(L0_attn_W_rvk, dtype=np.float32)
    W_out = np.asarray(L0_attn_W_out, dtype=np.float32)
    td = np.asarray(L0_attn_time_decay, dtype=np.float32)
    tf = np.asarray(L0_attn_time_first, dtype=np.float32)
    w1 = np.asarray(L0_ffn_w1, dtype=np.float32)
    w2 = np.asarray(L0_ffn_w2, dtype=np.float32)
    w3 = np.asarray(L0_ffn_w3, dtype=np.float32)

    WT = W_rvk.T
    wrvk = np.empty((CCN, 128, 3, CCN, 128), np.float32)
    for j, base in ((0, 2 * C), (1, C), (2, 0)):
        blk = WT[:, base:base + C].reshape(CCN, 128, CCN, 128)
        wrvk[:, :, j, :, :] = blk.transpose(2, 1, 0, 3)
    wrvk = np.ascontiguousarray(wrvk.reshape(CCN, 128, 3 * CCN * 128))

    w12 = np.empty((HHN, 2, 128, C), np.float32)
    for m, w in ((0, w1), (1, w2)):
        t = w.reshape(HHN, 128, CCN, 128)
        w12[:, m, :, :] = t.transpose(0, 3, 2, 1).reshape(HHN, 128, C)
    w12 = np.ascontiguousarray(w12)

    w3t = np.ascontiguousarray(w3.reshape(NDT5, DT5, HHN, 128).transpose(0, 2, 3, 1))

    wo = W_out.reshape(NDT5, DT5, CCN, 128).transpose(0, 3, 2, 1)
    wout = np.ascontiguousarray(wo.reshape(NDT5, 128, CCN * DT5))

    tdt = np.ascontiguousarray(td.reshape(CCN, 128).T)
    tft = np.ascontiguousarray(tf.reshape(CCN, 128).T)
    eye = np.eye(128, dtype=ml_dtypes.bfloat16)

    in_maps = []
    for core in range(8):
        b, h = core // 2, core % 2
        if h == 0:
            xs = np.concatenate([x[b, 0:WIN], x[b, 0:T_OWN]], axis=0)
            mask = np.zeros((128, 1), np.float32)
        else:
            xs = x[b, T_OWN - WIN: 2 * T_OWN]
            mask = np.ones((128, 1), np.float32)
        in_maps.append({
            "x": np.ascontiguousarray(xs),
            "wrvk": wrvk, "w12": w12, "w3t": w3t, "wout": wout,
            "td": tdt, "tf": tft, "mask": mask, "eye": eye,
        })
    return in_maps


def kernel(L0_attn_x, L0_attn_W_rvk, L0_attn_W_out, L0_attn_time_decay,
           L0_attn_time_first, L0_ffn_w1, L0_ffn_w2, L0_ffn_w3):
    in_maps = _prep_inputs(L0_attn_x, L0_attn_W_rvk, L0_attn_W_out,
                           L0_attn_time_decay, L0_attn_time_first,
                           L0_ffn_w1, L0_ffn_w2, L0_ffn_w3)
    nc = build_nc()
    res = run_bass_kernel_spmd(nc, in_maps, core_ids=list(range(8)))
    B, T = 4, 4096
    out = np.empty((B, T, C), np.float32)
    for core in range(8):
        b, h = core // 2, core % 2
        out[b, h * T_OWN:(h + 1) * T_OWN, :] = res.results[core]["out"]
    return out


# version 04
# speedup vs baseline: 1.3245x; 1.1493x over previous
import numpy as np
import ml_dtypes
import contextlib
import concourse.bass as bass
import concourse.mybir as mybir
from concourse.bass_utils import run_bass_kernel_spmd

F32 = mybir.dt.float32
BF16 = mybir.dt.bfloat16
ALU = mybir.AluOpType
ACT = mybir.ActivationFunctionType
AX = mybir.AxisListType

C = 2048
CCN = 16
H = 5632
HHN = 44
WIN = 128
T_OWN = 2048
T_KV = T_OWN + WIN
NBLK = T_KV // 128
EPS_LN = 1e-5
EPS_Q = 1e-8
NW = 5632.0 * 2048.0

SC = [(0, 1024), (1024, 1152)]
SCW = 1152
KV_TT = [(0, 512), (512, 512), (1024, 512), (1536, 512), (2048, 128)]
R_TT = [(WIN, 512), (WIN + 512, 512), (WIN + 1024, 512), (WIN + 1536, 512)]
TT5 = 512
NTT5 = T_OWN // TT5
NTB5 = TT5 // 128
DT5 = 512
NDT5 = C // DT5


def build_nc():
    import os as _os
    _norace = _os.environ.get("ATHENA_SIM_NORACE", "") == "1"
    nc = bass.Bass("TRN2", target_bir_lowering=False,
                   detect_race_conditions=not _norace)

    x_d = nc.dram_tensor("x", [T_KV, C], F32, kind="ExternalInput").ap()
    wrvk_d = nc.dram_tensor("wrvk", [CCN, 128, 3 * CCN * 128], F32, kind="ExternalInput").ap()
    w12_d = nc.dram_tensor("w12", [HHN, 2, 128, C], F32, kind="ExternalInput").ap()
    w3t_d = nc.dram_tensor("w3t", [NDT5, HHN, 128, DT5], F32, kind="ExternalInput").ap()
    wout_d = nc.dram_tensor("wout", [NDT5, 128, CCN * DT5], F32, kind="ExternalInput").ap()
    td_d = nc.dram_tensor("td", [128, CCN], F32, kind="ExternalInput").ap()
    tf_d = nc.dram_tensor("tf", [128, CCN], F32, kind="ExternalInput").ap()
    mask_d = nc.dram_tensor("mask", [128, 1], F32, kind="ExternalInput").ap()
    eye_d = nc.dram_tensor("eye", [128, 128], BF16, kind="ExternalInput").ap()
    out_d = nc.dram_tensor("out", [T_OWN, C], F32, kind="ExternalOutput").ap()
    ysp_d = nc.dram_tensor("ysp", [128, CCN * T_OWN], BF16, kind="Internal").ap()
    hidsp_d = nc.dram_tensor("hidsp", [128, HHN * T_OWN], BF16, kind="Internal").ap()
    q3sp_d = nc.dram_tensor("q3sp", [128, NDT5 * HHN * DT5], BF16, kind="Internal").ap()

    es = contextlib.ExitStack()

    SB_BASE = (nc._sbuf_addr_for_side("left") + 63) // 64 * 64
    SB_TOP = nc._sbuf_addr_for_side("right")
    off = [SB_BASE]

    def sb(name, shape, dtype, at=None):
        nbytes = int(np.prod(shape[1:])) * (2 if dtype == BF16 else 4)
        if at is None:
            at = off[0]
            off[0] = at + ((nbytes + 63) // 64) * 64
        return nc.alloc_sbuf_tensor_at(name, shape, dtype, offset=at), at

    def sba(name, shape, dtype, at=None):
        return sb(name, shape, dtype, at)[0]

    consts = sba("consts", [128, 16], F32)
    dch = sba("dch", [128, CCN], F32)
    euch = sba("euch", [128, CCN], F32)
    tdt = sba("tdt", [128, CCN], F32)
    etd = sba("etd", [128, CCN], F32)
    tft = sba("tft", [128, CCN], F32)
    eye_t = sba("eye", [128, 128], BF16)
    sum_s = sba("sum_s", [128, NBLK], F32)
    sq_s = sba("sq_s", [128, NBLK], F32)
    mean_s = sba("mean_s", [128, NBLK], F32)
    m2_s = sba("m2_s", [128, NBLK], F32)
    var_s = sba("var_s", [128, NBLK], F32)
    std_s = sba("std_s", [128, NBLK], F32)
    rstd_s = sba("rstd_s", [128, NBLK], F32)
    gacc = sba("gacc", [128, 144], F32)
    gred = sba("gred", [128, 8], F32)
    g1 = sba("g1", [1, 8], F32)
    canary = sba("canary", [128, 8], F32)
    ones_t = sba("ones_t", [128, 128], F32)
    sgn3, sgn3_at = sb("sgn3", [128, DT5], BF16)
    q3st = sba("q3st", [128, 2 * DT5], BF16)
    w3qb = sba("w3qb", [128, 2 * DT5], F32)
    tanh4 = sba("tanh4", [128, 2 * 512], F32)
    hidum, hid_at = sb("hidum", [128, C], F32)
    hidb = sba("hidb", [128, 2 * C], BF16, at=hid_at)
    gdum = hidum
    xsh = sba("xsh", [128, 2 * C], F32)
    big0, big0_at = sb("big0", [128, CCN * T_KV], BF16)
    union_at = off[0]
    wkb = sba("wkb", [128, 2 * 3 * CCN * 128], BF16)
    ekt, ekt_at = sb("ekt", [128, 4 * SCW], F32)
    vt = sba("vt", [128, 2 * SCW], F32)
    pbuf = sba("pbuf", [128, SCW + 1], F32)
    qbuf = sba("qbuf", [128, SCW + 1], F32)
    dte = sba("dte", [128, SCW], F32)
    ybf = sba("ybf", [128, 2 * SCW], BF16)
    trt = sba("trt", [128, 2 * T_OWN], F32)
    sgn = sba("sgn", [128, C], BF16)
    u_t = sba("u_t", [128, C], BF16)
    thr = sba("thr", [128, 8], F32)
    gapt = sba("gapt", [128, 128], F32)
    w12x = sba("w12x", [128, C], F32)
    end_at = off[0]
    qb = sba("qb", [128, 4 * C], BF16, at=ekt_at)
    xntok = sba("xntok", [128, 2 * C], BF16, at=ekt_at)
    QDTW = (HHN + CCN) * DT5
    qdt2 = sba("qdt2", [128, 2 * QDTW], BF16, at=big0_at + HHN * TT5 * 2)
    assert big0_at + HHN * TT5 * 2 + 2 * QDTW * 2 <= SB_TOP, "qdt2 overflow"
    y5 = sba("y5", [128, CCN * TT5], BF16, at=sgn3_at)
    xt5 = sba("xt5", [128, 4 * 512], F32, at=sgn3_at + CCN * TT5 * 2)
    ob5 = sba("ob5", [128, 4 * 512], F32, at=sgn3_at + CCN * TT5 * 2 + 4 * 512 * 4)
    assert end_at <= SB_TOP, f"sbuf overflow: {end_at} > {SB_TOP}"

    NMB = 6
    mmb = [es.enter_context(nc.psum_tensor(f"mmb{i}", [128, 512], F32)) for i in range(NMB)]
    ps_tr = [es.enter_context(nc.psum_tensor("ps_tr0", [128, 1024], BF16)),
             es.enter_context(nc.psum_tensor("ps_tr1", [128, 1024], BF16))]

    sem_names = ["sx", "sxA", "sxB", "swgA", "swgB", "sw5", "sws", "sscA", "sscB",
                 "smm", "str", "sevv", "sevs", "sv", "ss",
                 "sgy0", "sgy1", "sgh0", "sgh1", "sgq0", "sgq1",
                 "spx0", "spx1", "spx2", "spx3",
                 "sst0", "sst1", "sst2", "sst3", "sp5", "sth", "sws0", "sws1", "sws2", "sp5b", "sw5b"]
    sems = {n: es.enter_context(nc.semaphore(n)) for n in sem_names}
    cnt = {n: 0 for n in sem_names}
    OPS = {e: [] for e in ["sync", "scalar", "vector", "tensor", "gpsimd"]}

    def emit(eng, fn):
        OPS[eng].append(fn)

    def op_inc(eng, name, by, fn):
        cnt[name] += by
        v = cnt[name]
        sem = sems[name]
        emit(eng, lambda e, fn=fn, sem=sem, by=by: fn(e).then_inc(sem, by))
        return v

    def wait(eng, name, v):
        if v <= 0:
            return
        sem = sems[name]
        emit(eng, lambda e, sem=sem, v=v: e.wait_ge(sem, v))

    def vgap():
        emit("vector", lambda v: v.memset(gapt[:, 0:64], 0.0))

    def sgap():
        emit("scalar", lambda s: s.memzero(gapt[:, 64:128]))

    bank_free = [("sevv", 0)] * NMB
    bank_i = [0]

    def next_bank():
        b = bank_i[0] % NMB
        bank_i[0] += 1
        return b

    ST = {}

    def p0():
        op_inc("sync", "sx", 16, lambda e: e.dma_start(out=tdt[:], in_=td_d[:]))
        op_inc("sync", "sx", 16, lambda e: e.dma_start(out=tft[:], in_=tf_d[:]))
        op_inc("sync", "sx", 16, lambda e: e.dma_start(out=consts[:, 4:5], in_=mask_d[:]))
        op_inc("sync", "sx", 16, lambda e: e.dma_start(out=eye_t[:], in_=eye_d[:]))
        emit("vector", lambda v: v.memset(consts[:, 0:1], EPS_LN))
        emit("vector", lambda v: v.memset(consts[:, 1:2], 0.5))
        emit("vector", lambda v: v.memset(consts[:, 2:3], -1.0))
        emit("vector", lambda v: v.memset(consts[:, 3:4], 0.0))
        emit("vector", lambda v: v.memset(consts[:, 5:6], 0.25))
        emit("vector", lambda v: v.memset(pbuf[:, 0:1], 0.0))
        emit("vector", lambda v: v.memset(qbuf[:, 0:1], 0.0))
        emit("vector", lambda v: v.memset(gacc[:], 0.0))
        vv = op_inc("vector", "sv", 1, lambda v: v.memset(ones_t[:], 1.0))
        wait("scalar", "sx", 64)
        wait("scalar", "sv", vv)
        emit("scalar", lambda s: s.activation(out=etd[:], in_=tdt[:], func=ACT.Exp))
        op_inc("scalar", "ss", 1, lambda s: s.activation(out=euch[:], in_=tft[:], func=ACT.Exp))
        sgap()
        op_inc("scalar", "ss", 1, lambda s: s.activation(out=dch[:], in_=etd[:], func=ACT.Exp,
                                                         scale=consts[:, 2:3]))

    xn_done = {}
    sq_done = {}
    tr_last = {}
    tr_evac = {}
    trc = [0]

    def emit_xn(b):
        par = b % 2
        if b >= 2:
            wait("vector", "str", tr_last[b - 2])
        xv = op_inc("vector", "sevv", 1,
                    lambda v, b=b, par=par: v.tensor_scalar(
                        out=xntok[:, par * C:(par + 1) * C], in0=xsh[:, par * C:(par + 1) * C],
                        scalar1=mean_s[:, b:b + 1], scalar2=rstd_s[:, b:b + 1],
                        op0=ALU.subtract, op1=ALU.mult))
        xn_done[b] = xv
        wait("tensor", "sevv", xv)
        for g4 in range(CCN // 4):
            g = trc[0]
            bank = g % 2
            if g >= 2:
                wait("tensor", "sevv", tr_evac[g - 2])
            tv = 0
            for i in range(4):
                cc = g4 * 4 + i
                inst = [None]
                tv = op_inc("tensor", "str", 1,
                            lambda t, par=par, cc=cc, i=i, bank=bank: t.transpose(
                                ps_tr[bank][:, i * 256:(i + 1) * 256][:, 0:128]
                                if False else
                                ps_tr[bank][:, i * 128 + 0: i * 128 + 128],
                                xntok[:, par * C + cc * 128: par * C + (cc + 1) * 128],
                                eye_t[:]))
            wait("vector", "str", tv)
            ev = 0
            for i in range(4):
                cc = g4 * 4 + i
                ev = op_inc("vector", "sevv", 1,
                            lambda v, b=b, cc=cc, i=i, bank=bank: v.tensor_copy(
                                out=big0[:, cc * T_KV + b * 128: cc * T_KV + (b + 1) * 128],
                                in_=ps_tr[bank][:, i * 128: i * 128 + 128]))
            tr_evac[g] = ev
            trc[0] += 1
        tr_last[b] = cnt["str"]

    def p1():
        for b in range(NBLK):
            par = b % 2
            if b >= 2:
                wait("sync", "sevv", xn_done[b - 2])
                wait("sync", "sevs", sq_done[b - 2])
            sxp = "sxA" if par == 0 else "sxB"
            lv = op_inc("sync", sxp, 16,
                        lambda e, b=b, par=par: e.dma_start(
                            out=xsh[:, par * C:(par + 1) * C], in_=x_d[b * 128:(b + 1) * 128, :]))
            wait("vector", sxp, lv)
            rv = op_inc("vector", "sevv", 1,
                        lambda v, b=b, par=par: v.reduce_sum(
                            out=sum_s[:, b:b + 1], in_=xsh[:, par * C:(par + 1) * C], axis=AX.X))
            wait("scalar", sxp, lv)
            sqv = op_inc("scalar", "sevs", 1,
                         lambda s, b=b, par=par: s.activation(
                             out=gdum[:], in_=xsh[:, par * C:(par + 1) * C],
                             func=ACT.Square, accum_out=sq_s[:, b:b + 1]))
            sq_done[b] = sqv
            wait("scalar", "sevv", rv)
            emit("scalar", lambda s, b=b: s.mul(out=mean_s[:, b:b + 1], in_=sum_s[:, b:b + 1],
                                                mul=1.0 / C))
            sgap()
            mv = op_inc("scalar", "sevs", 1,
                        lambda s, b=b: s.activation(out=m2_s[:, b:b + 1], in_=mean_s[:, b:b + 1],
                                                    func=ACT.Square))
            wait("vector", "sevs", mv)
            vv = op_inc("vector", "sevv", 1,
                        lambda v, b=b: v.scalar_tensor_tensor(
                            out=var_s[:, b:b + 1], in0=sq_s[:, b:b + 1], scalar=1.0 / C,
                            in1=m2_s[:, b:b + 1], op0=ALU.mult, op1=ALU.subtract))
            wait("scalar", "sevv", vv)
            sv_ = op_inc("scalar", "sevs", 1,
                         lambda s, b=b: s.activation(out=std_s[:, b:b + 1], in_=var_s[:, b:b + 1],
                                                     func=ACT.Sqrt, bias=consts[:, 0:1]))
            wait("vector", "sevs", sv_)
            emit("vector", lambda v, b=b: v.reciprocal(out=rstd_s[:, b:b + 1],
                                                       in_=std_s[:, b:b + 1]))
            if b > 0:
                emit_xn(b - 1)
        emit("vector", lambda v: v.memset(consts[:, 9:10], 0.0))
        emit_xn(NBLK - 1)

    gunits = []

    def make_gamma_units():
        w3flat = w3t_d.rearrange("a b p f -> (a b) p f")
        for blk in range(NDT5 * HHN // 4):
            srcs = [w3flat[4 * blk + i, :, :] for i in range(4)]
            gunits.append((srcs, DT5, 96 + blk))
        for hh in range(HHN):
            for m in range(2):
                gunits.append(([w12_d[hh, m, :, :]], C, m * 48 + hh))

    gstate = {"load": 0, "use": 0, "loadv": {}}

    def pump_gamma(n):
        for _ in range(n):
            for _ld in range(2):
                if gstate["load"] < len(gunits) and gstate["load"] - gstate["use"] < 2:
                    i = gstate["load"]
                    srcs, ncols, col = gunits[i]
                    par = i % 2
                    if i == 0:
                        wait("scalar", "sevv", xn_done[NBLK - 1])
                    lv = 0
                    for si, s_ap in enumerate(srcs):
                        lv = op_inc("scalar", "sscA" if par == 0 else "sscB", 16,
                                    lambda e, s_ap=s_ap, ncols=ncols, par=par, si=si:
                                    e.dma_start(out=xsh[:, par * C + si * ncols:
                                                        par * C + (si + 1) * ncols], in_=s_ap))
                    gstate["loadv"][i] = lv
                    gstate["load"] += 1
            if gstate["use"] < gstate["load"]:
                i = gstate["use"]
                srcs, ncols, col = gunits[i]
                tot = ncols * len(srcs)
                par = i % 2
                wait("scalar", "sscA" if par == 0 else "sscB", gstate["loadv"][i])
                op_inc("scalar", "ss", 1,
                       lambda s, tot=tot, col=col, par=par: s.activation(
                           out=gdum[:, 0:tot], in_=xsh[:, par * C: par * C + tot],
                           func=ACT.Abs, accum_out=gacc[:, col:col + 1]))
                gstate["use"] += 1

    def gamma_finish():
        pump_gamma(len(gunits))
        ST["gamma_ss"] = cnt["ss"]
        wait("vector", "ss", cnt["ss"])
        rv = 0
        for m in range(3):
            rv = op_inc("vector", "sevv", 1,
                        lambda v, m=m: v.reduce_sum(out=gred[:, m:m + 1],
                                                    in_=gacc[:, m * 48:(m + 1) * 48], axis=AX.X))
        wait("tensor", "sevv", rv)
        b1 = next_bank()
        wait("tensor", *bank_free[b1])
        mv = op_inc("tensor", "smm", 1,
                    lambda t, b1=b1: t.matmul(mmb[b1][0:1, 0:3], ones_t[:, 0:1], gred[:, 0:3],
                                              start=True, stop=True))
        wait("vector", "smm", mv)
        cv = op_inc("vector", "sevv", 1, lambda v, b1=b1: v.tensor_copy(
            out=g1[0:1, 0:3], in_=mmb[b1][0:1, 0:3]))
        bank_free[b1] = ("sevv", cv)
        wait("tensor", "sevv", cv)
        b2 = next_bank()
        wait("tensor", *bank_free[b2])
        mv2 = op_inc("tensor", "smm", 1,
                     lambda t, b2=b2: t.matmul(mmb[b2][:, 4:7], ones_t[0:1, :], g1[0:1, 0:3],
                                               start=True, stop=True))
        wait("vector", "smm", mv2)
        tv_ = op_inc("vector", "sth", 1,
                     lambda v, b2=b2: v.tensor_scalar(out=thr[:, 0:3], in0=mmb[b2][:, 4:7],
                                                      scalar1=0.5 / NW, scalar2=0.5 * EPS_Q,
                                                      op0=ALU.mult, op1=ALU.add))
        bank_free[b2] = ("sth", tv_)
        vgap()
        vgap()
        ST["thr_done"] = op_inc("vector", "sth", 1,
                                lambda v: v.tensor_scalar(out=thr[:, 3:6], in0=thr[:, 0:3],
                                                          scalar1=-1.0, scalar2=None,
                                                          op0=ALU.mult))


    last_q = {"v": 0}

    def quantize_sgn(src_ap, dst_ap, m, sg_ap):
        wait("scalar", "sevv", last_q["v"])
        op_inc("scalar", "sevs", 1,
               lambda s, src_ap=src_ap, sg_ap=sg_ap, m=m: s.activation(
                   out=sg_ap, in_=src_ap, func=ACT.Sign, bias=thr[:, 3 + m:4 + m]))
        a2 = op_inc("scalar", "sevs", 1,
                    lambda s, src_ap=src_ap, dst_ap=dst_ap, m=m: s.activation(
                        out=dst_ap, in_=src_ap, func=ACT.Sign, bias=thr[:, m:m + 1]))
        wait("vector", "sevs", a2)
        qv = op_inc("vector", "sevv", 1,
                    lambda v, dst_ap=dst_ap, sg_ap=sg_ap: v.tensor_add(
                        out=dst_ap, in0=dst_ap, in1=sg_ap))
        last_q["v"] = qv
        return qv

    def quantize_vec(src_ap, dst_ap, m):
        op_inc("vector", "sevv", 1,
               lambda v, src_ap=src_ap, m=m: v.tensor_scalar(
                   out=u_t[:, 0:src_ap.shape[1]], in0=src_ap, scalar1=thr[:, 3 + m:4 + m],
                   scalar2=None, op0=ALU.is_le))
        vgap()
        return op_inc("vector", "sevv", 1,
                      lambda v, src_ap=src_ap, dst_ap=dst_ap, m=m: v.scalar_tensor_tensor(
                          out=dst_ap, in0=src_ap, scalar=thr[:, m:m + 1],
                          in1=u_t[:, 0:src_ap.shape[1]], op0=ALU.is_ge, op1=ALU.subtract))

    q3units = []

    def make_q3_units():
        for dt in range(NDT5):
            for hh in range(HHN):
                q3units.append((dt, hh))

    qstate = {"load": 0, "use": 0, "loadv": {}, "spw": {}}

    def pump_q3(n):
        for _ in range(n):
            for _ld in range(2):
                if qstate["load"] < len(q3units) and qstate["load"] - qstate["use"] < 2:
                    i = qstate["load"]
                    dt, hh = q3units[i]
                    par = i % 2
                    if i == 0:
                        wait("scalar", "sth", ST["thr_done"])
                    if i >= 2:
                        wait("scalar", "sgq0" if i % 2 == 0 else "sgq1", qstate["spw"][i - 2])
                    qstate["loadv"][i] = op_inc("scalar", "sscA" if par == 0 else "sscB", 16,
                                                lambda e, dt=dt, hh=hh, par=par: e.dma_start(
                                                    out=w3qb[:, par * DT5:(par + 1) * DT5],
                                                    in_=w3t_d[dt, hh, :, :]))
                    qstate["load"] += 1
            if qstate["use"] < qstate["load"]:
                i = qstate["use"]
                dt, hh = q3units[i]
                par = i % 2
                wait("scalar", "sscA" if par == 0 else "sscB", qstate["loadv"][i])
                qv = quantize_sgn(w3qb[:, par * DT5:(par + 1) * DT5],
                                  q3st[:, par * DT5:(par + 1) * DT5], 2, sgn3[:])
                wait("gpsimd", "sevv", qv)
                qstate["spw"][i] = op_inc("gpsimd", "sgq0" if i % 2 == 0 else "sgq1", 16,
                                          lambda e, dt=dt, hh=hh, par=par: e.dma_start(
                                              out=q3sp_d[:, (dt * HHN + hh) * DT5:
                                                         (dt * HHN + hh + 1) * DT5],
                                              in_=q3st[:, par * DT5:(par + 1) * DT5]))
                qstate["use"] += 1

    cons = {}
    ysp_done = {}

    def p3():
        import os as _os
        NCC = int(_os.environ.get("ATHENA_DBG_CC", str(CCN)))
        SKIP_SCAN = _os.environ.get("ATHENA_DBG_NOSCAN", "") == "1"
        NWKB = 3 * CCN * 128
        wk_used = {}
        for cc in range(NCC):
            par = cc % 2
            if cc >= 2:
                wait("gpsimd", "smm", wk_used[cc - 2])
            swp = "swgA" if par == 0 else "swgB"
            lw = op_inc("gpsimd", swp, 16,
                        lambda e, cc=cc, par=par: e.dma_start(
                            out=wkb[:, par * NWKB:(par + 1) * NWKB], in_=wrvk_d[cc, :, :]))
            wait("tensor", swp, lw)

            for j, tts in ((0, KV_TT), (1, KV_TT), (2, R_TT)):
                for (t0, tl) in tts:
                    b = next_bank()
                    wait("tensor", *bank_free[b])
                    for c in range(CCN - 1):
                        lhs = wkb[:, par * NWKB + (j * CCN + c) * 128:
                                  par * NWKB + (j * CCN + c) * 128 + 128]
                        rhs = big0[:, c * T_KV + t0: c * T_KV + t0 + tl]
                        emit("tensor", lambda t, lhs=lhs, rhs=rhs, b=b, c=c, tl=tl:
                             t.matmul(mmb[b][:, 0:tl], lhs, rhs,
                                      start=(c == 0), stop=False))
                    c = CCN - 1
                    lhs = wkb[:, par * NWKB + (j * CCN + c) * 128:
                              par * NWKB + (j * CCN + c) * 128 + 128]
                    rhs = big0[:, c * T_KV + t0: c * T_KV + t0 + tl]
                    gv = op_inc("tensor", "smm", 1,
                                lambda t, lhs=lhs, rhs=rhs, b=b, tl=tl:
                                t.matmul(mmb[b][:, 0:tl], lhs, rhs, start=False, stop=True))
                    wk_used[cc] = gv
                    pump_gamma(1)
                    ch = 0 if t0 < SC[1][0] else 1
                    c0 = SC[ch][0]
                    if j == 0:
                        ep = (cc % 2) * 2 * SCW
                        dst = ekt[:, ep + ch * SCW + (t0 - c0): ep + ch * SCW + (t0 - c0) + tl]
                        wait("scalar", "smm", gv)
                        if cc >= 2:
                            wait("scalar", "sevv", cons[cc - 2])
                        ev = op_inc("scalar", "sevs", 1,
                                    lambda s, dst=dst, b=b, tl=tl: s.activation(
                                        out=dst, in_=mmb[b][:, 0:tl], func=ACT.Exp))
                        bank_free[b] = ("sevs", ev)
                    elif j == 1:
                        dst = vt[:, ch * SCW + (t0 - c0): ch * SCW + (t0 - c0) + tl]
                        wait("vector", "smm", gv)
                        ev = op_inc("vector", "sevv", 1,
                                    lambda v, dst=dst, b=b, tl=tl: v.tensor_copy(
                                        out=dst, in_=mmb[b][:, 0:tl]))
                        bank_free[b] = ("sevv", ev)
                    else:
                        o0 = (cc % 2) * T_OWN + t0 - WIN
                        wait("scalar", "smm", gv)
                        if cc >= 2:
                            wait("scalar", "sevv", cons[cc - 2])
                        ev = op_inc("scalar", "sevs", 1,
                                    lambda s, o0=o0, b=b, tl=tl: s.activation(
                                        out=trt[:, o0:o0 + tl], in_=mmb[b][:, 0:tl],
                                        func=ACT.Tanh, scale=consts[:, 1:2]))
                        bank_free[b] = ("sevs", ev)

            wait("vector", "sevs", cnt["sevs"])
            if SKIP_SCAN:
                cons[cc] = op_inc("vector", "sevv", 1, lambda v: v.memset(consts[:, 8:9], 0.0))
                for _ch in range(2):
                    k_ = cc * 2 + _ch
                    ysp_done[k_] = cnt["sgy0"] if k_ % 2 == 0 else cnt["sgy1"]
                continue
            ep = (cc % 2) * 2 * SCW
            tp = (cc % 2) * T_OWN
            emit("vector", lambda v, ep=ep: v.tensor_scalar(
                out=ekt[:, ep: ep + WIN], in0=ekt[:, ep: ep + WIN],
                scalar1=consts[:, 4:5], scalar2=None, op0=ALU.mult))
            vgap()
            for ch, (c0, cl) in enumerate(SC):
                ek_c = ekt[:, ep + ch * SCW: ep + ch * SCW + cl]
                v_c = vt[:, ch * SCW: ch * SCW + cl]
                pb = pbuf[:]
                qbv = qbuf[:]
                d_bc = dch[:, cc:cc + 1].broadcast_to([128, cl])
                oo, ol = (WIN, 1024 - WIN) if ch == 0 else (0, cl)
                own_off = tp + (0 if ch == 0 else 1024 - WIN)
                init_q = consts[:, 3:4] if ch == 0 else qbv[:, 0:1]
                init_p = consts[:, 3:4] if ch == 0 else pb[:, 0:1]
                emit("vector", lambda v, qbv=qbv, d_bc=d_bc, ek_c=ek_c, cl=cl, init_q=init_q:
                     v.tensor_tensor_scan(out=qbv[:, 1:1 + cl], data0=d_bc, data1=ek_c,
                                          initial=init_q, op0=ALU.mult, op1=ALU.add))
                emit("vector", lambda v, own_off=own_off, ol=ol: v.tensor_scalar(
                    out=trt[:, own_off:own_off + ol], in0=trt[:, own_off:own_off + ol],
                    scalar1=0.5, scalar2=0.5, op0=ALU.mult, op1=ALU.add))
                emit("vector", lambda v, ek_c=ek_c, qbv=qbv, cc=cc, oo=oo, ol=ol:
                     v.scalar_tensor_tensor(out=dte[:, 0:ol], in0=ek_c[:, oo:oo + ol],
                                            scalar=euch[:, cc:cc + 1],
                                            in1=qbv[:, oo:oo + ol] if ol != 1024 - WIN
                                            else qbv[:, 1 + oo - 1: 1 + oo - 1 + ol],
                                            op0=ALU.mult, op1=ALU.add))
                if ch + 1 < len(SC):
                    emit("vector", lambda v, qbv=qbv, cl=cl: v.tensor_copy(
                        out=qbuf[:, 0:1], in_=qbv[:, cl:cl + 1]))
                emit("vector", lambda v, ek_c=ek_c, v_c=v_c: v.tensor_mul(
                    out=ek_c, in0=ek_c, in1=v_c))
                vgap()
                emit("vector", lambda v, pb=pb, d_bc=d_bc, ek_c=ek_c, cl=cl, init_p=init_p:
                     v.tensor_tensor_scan(out=pb[:, 1:1 + cl], data0=d_bc, data1=ek_c,
                                          initial=init_p, op0=ALU.mult, op1=ALU.add))
                emit("vector", lambda v, ol=ol: v.reciprocal(
                    out=dte[:, 0:ol], in_=dte[:, 0:ol]))
                emit("vector", lambda v, v_c=v_c, ek_c=ek_c, pb=pb, cc=cc, oo=oo, ol=ol:
                     v.scalar_tensor_tensor(out=v_c[:, 0:ol], in0=ek_c[:, oo:oo + ol],
                                            scalar=euch[:, cc:cc + 1],
                                            in1=pb[:, oo:oo + ol] if ol != 1024 - WIN
                                            else pb[:, 1 + oo - 1: 1 + oo - 1 + ol],
                                            op0=ALU.mult, op1=ALU.add))
                if ch + 1 < len(SC):
                    emit("vector", lambda v, pb=pb, cl=cl: v.tensor_copy(
                        out=pbuf[:, 0:1], in_=pb[:, cl:cl + 1]))
                vgap()
                emit("vector", lambda v, qbv=qbv, v_c=v_c, ol=ol: v.tensor_mul(
                    out=qbv[:, 1:1 + ol], in0=v_c[:, 0:ol], in1=dte[:, 0:ol]))
                k = cc * 2 + ch
                ypar = k % 2
                if k >= 2:
                    wait("vector", "sgy0" if ypar == 0 else "sgy1", ysp_done[k - 2])
                vgap()
                yv = op_inc("vector", "sevv", 1,
                            lambda v, own_off=own_off, ol=ol, qbv=qbv, ypar=ypar:
                            v.tensor_mul(out=ybf[:, ypar * SCW: ypar * SCW + ol],
                                         in0=trt[:, own_off:own_off + ol],
                                         in1=qbv[:, 1:1 + ol]))
                wait("sync", "sevv", yv)
                ysp_done[k] = op_inc("sync", "sgy0" if ypar == 0 else "sgy1", 16,
                                     lambda e, cc=cc, own_off=own_off, tp=tp, ol=ol, ypar=ypar:
                                     e.dma_start(
                                         out=ysp_d[:, cc * T_OWN + own_off - tp:
                                                   cc * T_OWN + own_off - tp + ol],
                                         in_=ybf[:, ypar * SCW: ypar * SCW + ol]))
            cons[cc] = op_inc("vector", "sevv", 1, lambda v: v.memset(consts[:, 8:9], 0.0))


    hid_ev = {}
    hid_spill = {}
    mm_after_hh = {}

    def p4():
        wait("sync", "ss", ST["gamma_ss"])
        q_ready = {}
        qdone = {}
        wslot = {}
        slots = [xsh[:, 0:C], xsh[:, C:2 * C], w12x[:]]
        for hh in range(HHN):
            par = hh % 2
            for m in (0, 1):
                i = 2 * hh + m
                s_ = i % 3
                if i >= 3:
                    wait("sync", "sevv", qdone[i - 3])
                wslot[i] = op_inc("sync", f"sws{s_}", 16,
                                  lambda e, hh=hh, m=m, s_=s_: e.dma_start(
                                      out=slots[s_], in_=w12_d[hh, m, :, :]))
            if hh >= 2:
                wait("scalar", "smm", mm_after_hh[hh - 2])
                wait("vector", "smm", mm_after_hh[hh - 2])
            s1_, s2_ = (2 * hh) % 3, (2 * hh + 1) % 3
            wait("scalar", f"sws{s1_}", wslot[2 * hh])
            if hh == 0:
                wait("scalar", "sth", ST["thr_done"])
            qdone[2 * hh] = quantize_sgn(slots[s1_], qb[:, (par * 2) * C:(par * 2 + 1) * C],
                                         0, sgn[:])
            wait("vector", f"sws{s2_}", wslot[2 * hh + 1])
            qv = quantize_vec(slots[s2_], qb[:, (par * 2 + 1) * C:(par * 2 + 2) * C], 1)
            q_ready[hh] = qv
            qdone[2 * hh + 1] = qv
            wait("tensor", "sevv", qv)
            for ti, (t0, tl) in enumerate(R_TT):
                b1 = next_bank()
                wait("tensor", *bank_free[b1])
                for c in range(CCN - 1):
                    lhs = qb[:, (par * 2) * C + c * 128:(par * 2) * C + (c + 1) * 128]
                    rhs = big0[:, c * T_KV + t0: c * T_KV + t0 + tl]
                    emit("tensor", lambda t, lhs=lhs, rhs=rhs, b1=b1, c=c, tl=tl:
                         t.matmul(mmb[b1][:, 0:tl], lhs, rhs, start=(c == 0), stop=False))
                c = CCN - 1
                lhs = qb[:, (par * 2) * C + c * 128:(par * 2) * C + (c + 1) * 128]
                rhs = big0[:, c * T_KV + t0: c * T_KV + t0 + tl]
                g1v = op_inc("tensor", "smm", 1,
                             lambda t, lhs=lhs, rhs=rhs, b1=b1, tl=tl:
                             t.matmul(mmb[b1][:, 0:tl], lhs, rhs, start=False, stop=True))
                b2 = next_bank()
                wait("tensor", *bank_free[b2])
                for c in range(CCN - 1):
                    lhs = qb[:, (par * 2 + 1) * C + c * 128:(par * 2 + 1) * C + (c + 1) * 128]
                    rhs = big0[:, c * T_KV + t0: c * T_KV + t0 + tl]
                    emit("tensor", lambda t, lhs=lhs, rhs=rhs, b2=b2, c=c, tl=tl:
                         t.matmul(mmb[b2][:, 0:tl], lhs, rhs, start=(c == 0), stop=False))
                c = CCN - 1
                lhs = qb[:, (par * 2 + 1) * C + c * 128:(par * 2 + 1) * C + (c + 1) * 128]
                rhs = big0[:, c * T_KV + t0: c * T_KV + t0 + tl]
                g2v = op_inc("tensor", "smm", 1,
                             lambda t, lhs=lhs, rhs=rhs, b2=b2, tl=tl:
                             t.matmul(mmb[b2][:, 0:tl], lhs, rhs, start=False, stop=True))
                mm_after_hh[hh] = g2v
                k = hh * 4 + ti
                tpar = ti % 2
                wait("scalar", "smm", g1v)
                if k >= 2:
                    wait("scalar", "sevv", hid_ev[k - 2])
                e1 = op_inc("scalar", "sevs", 1,
                            lambda s, b1=b1, tpar=tpar, tl=tl: s.activation(
                                out=tanh4[:, tpar * 512: tpar * 512 + tl],
                                in_=mmb[b1][:, 0:tl], func=ACT.Tanh, scale=consts[:, 5:6]))
                bank_free[b1] = ("sevs", e1)
                o0 = t0 - WIN
                hpar = ti % 2
                wait("vector", "sevs", e1)
                wait("vector", "smm", g2v)
                if k >= 2:
                    wait("vector", "sgh0" if k % 2 == 0 else "sgh1", hid_spill[k - 2])
                emit("vector", lambda v, tpar=tpar, hpar=hpar, tl=tl: v.tensor_scalar(
                    out=hidb[:, hpar * C: hpar * C + tl],
                    in0=tanh4[:, tpar * 512: tpar * 512 + tl],
                    scalar1=0.25, scalar2=0.25, op0=ALU.mult, op1=ALU.add))
                vgap()
                e3 = op_inc("vector", "sevv", 1,
                            lambda v, b2=b2, hpar=hpar, tl=tl: v.tensor_mul(
                                out=hidb[:, hpar * C: hpar * C + tl],
                                in0=hidb[:, hpar * C: hpar * C + tl], in1=mmb[b2][:, 0:tl]))
                hid_ev[k] = e3
                bank_free[b2] = ("sevv", e3)
                wait("gpsimd", "sevv", e3)
                hid_spill[k] = op_inc("gpsimd", "sgh0" if k % 2 == 0 else "sgh1", 16,
                                      lambda e, hh=hh, o0=o0, tl=tl, hpar=hpar: e.dma_start(
                                          out=hidsp_d[:, hh * T_OWN + o0:
                                                      hh * T_OWN + o0 + tl],
                                          in_=hidb[:, hpar * C: hpar * C + tl]))
                pump_q3(1)
        pump_q3(len(q3units))
        ST["hid_sg"] = (cnt["sgh0"], cnt["sgh1"])
        ST["q3_sg"] = (cnt["sgq0"], cnt["sgq1"])
        ST["y_sg"] = (cnt["sgy0"], cnt["sgy1"])
        ST["p4_mm"] = cnt["smm"]

    def p5():
        hoff = 0
        mm_tti = {}
        mm_g = {}
        xused = {}
        slot_free = {}
        eidx = [0]
        for tti in range(NTT5):
            wait("sync", "sgh0", ST["hid_sg"][0])
            wait("sync", "sgh1", ST["hid_sg"][1])
            wait("sync", "sgq0", ST["q3_sg"][0])
            wait("sync", "sgq1", ST["q3_sg"][1])
            wait("sync", "smm", mm_tti.get(tti - 1, ST["p4_mm"]))
            for g0 in range(0, HHN, 11):
                src5 = hidsp_d.rearrange("p (a f) -> p a f", a=HHN)[
                    :, g0:g0 + 11, tti * TT5:(tti + 1) * TT5]
                dst5 = big0[:, hoff + g0 * TT5: hoff + (g0 + 11) * TT5].rearrange(
                    "p (a f) -> p a f", a=11)
                op_inc("sync", "sp5", 16,
                       lambda e, src5=src5, dst5=dst5: e.dma_start(out=dst5, in_=src5))
            wait("sync", "sgy0", ST["y_sg"][0])
            wait("sync", "sgy1", ST["y_sg"][1])
            ysrc = ysp_d.rearrange("p (a f) -> p a f", a=CCN)[:, :, tti * TT5:(tti + 1) * TT5]
            ydst = y5[:].rearrange("p (a f) -> p a f", a=CCN)
            op_inc("sync", "sp5", 16,
                   lambda e, ysrc=ysrc, ydst=ydst: e.dma_start(out=ydst, in_=ysrc))
            for dt in range(NDT5):
                g = tti * NDT5 + dt
                slot = g % 2
                qsem = "sp5" if slot == 0 else "sp5b"
                wsem = "sw5" if slot == 0 else "sw5b"
                prev = mm_g.get(g - 2, ST["p4_mm"])
                wait("sync", "smm", prev)
                wait("gpsimd", "smm", prev)
                ql = op_inc("sync", qsem, 16,
                            lambda e, dt=dt, slot=slot: e.dma_start(
                                out=qdt2[:, slot * QDTW: slot * QDTW + HHN * DT5],
                                in_=q3sp_d[:, dt * HHN * DT5:(dt + 1) * HHN * DT5]))
                wl = op_inc("gpsimd", wsem, 16,
                            lambda e, dt=dt, slot=slot: e.dma_start(
                                out=qdt2[:, slot * QDTW + HHN * DT5: (slot + 1) * QDTW],
                                in_=wout_d[dt, :, :]))
                for tb in range(NTB5):
                    tglob = tti * NTB5 + tb
                    xslot = eidx[0] % 4
                    if eidx[0] >= 4:
                        wait("sync", "sevv", xused[eidx[0] - 4])
                    xlv = op_inc("sync", f"spx{xslot}", 16,
                                 lambda e, tglob=tglob, dt=dt, xslot=xslot: e.dma_start(
                                     out=xt5[:, xslot * 512:(xslot + 1) * 512],
                                     in_=x_d[WIN + tglob * 128: WIN + (tglob + 1) * 128,
                                             dt * DT5:(dt + 1) * DT5]))
                    ST[("xl", eidx[0])] = xlv
                    b = next_bank()
                    wait("tensor", *bank_free[b])
                    wait("tensor", qsem, ql)
                    wait("tensor", "sp5", cnt["sp5"])
                    wait("tensor", wsem, wl)
                    for hh in range(HHN):
                        lhs = big0[:, hoff + hh * TT5 + tb * 128:
                                   hoff + hh * TT5 + (tb + 1) * 128]
                        rhs = qdt2[:, slot * QDTW + hh * DT5: slot * QDTW + (hh + 1) * DT5]
                        emit("tensor", lambda t, lhs=lhs, rhs=rhs, b=b, first=(hh == 0):
                             t.matmul(mmb[b][:], lhs, rhs, start=first, stop=False))
                    for cc in range(CCN - 1):
                        lhs = y5[:, cc * TT5 + tb * 128: cc * TT5 + (tb + 1) * 128]
                        rhs = qdt2[:, slot * QDTW + (HHN + cc) * DT5:
                                   slot * QDTW + (HHN + cc + 1) * DT5]
                        emit("tensor", lambda t, lhs=lhs, rhs=rhs, b=b:
                             t.matmul(mmb[b][:], lhs, rhs, start=False, stop=False))
                    cc = CCN - 1
                    lhs = y5[:, cc * TT5 + tb * 128: cc * TT5 + (tb + 1) * 128]
                    rhs = qdt2[:, slot * QDTW + (HHN + cc) * DT5:
                               slot * QDTW + (HHN + cc + 1) * DT5]
                    gv = op_inc("tensor", "smm", 1,
                                lambda t, lhs=lhs, rhs=rhs, b=b:
                                t.matmul(mmb[b][:], lhs, rhs, start=False, stop=True))
                    mm_g[g] = gv
                    mm_tti[tti] = gv
                    oslot = eidx[0] % 4
                    wait("vector", "smm", gv)
                    wait("vector", f"spx{xslot}", ST[("xl", eidx[0])])
                    if eidx[0] >= 4:
                        wait("vector", f"sst{oslot}", slot_free[eidx[0] - 4])
                    ev = op_inc("vector", "sevv", 1,
                                lambda v, b=b, oslot=oslot, xslot=xslot: v.tensor_add(
                                    out=ob5[:, oslot * 512:(oslot + 1) * 512],
                                    in0=mmb[b][:],
                                    in1=xt5[:, xslot * 512:(xslot + 1) * 512]))
                    bank_free[b] = ("sevv", ev)
                    xused[eidx[0]] = ev
                    wait("gpsimd", "sevv", ev)
                    sv = op_inc("gpsimd", f"sst{oslot}", 16,
                                lambda e, tglob=tglob, dt=dt, oslot=oslot: e.dma_start(
                                    out=out_d[tglob * 128:(tglob + 1) * 128,
                                              dt * DT5:(dt + 1) * DT5],
                                    in_=ob5[:, oslot * 512:(oslot + 1) * 512]))
                    slot_free[eidx[0]] = sv
                    eidx[0] += 1
        for s_ in range(4):
            wait("gpsimd", f"sst{s_}", cnt[f"sst{s_}"])

    import os
    DBG = int(os.environ.get("ATHENA_DBG_PHASES", "9"))
    make_gamma_units()
    make_q3_units()
    p0()
    if DBG >= 1:
        p1()
    if DBG >= 2:
        p3()
    if DBG >= 3:
        gamma_finish()
    if DBG >= 4:
        p4()
    if DBG >= 5:
        p5()
    if DBG < 5:
        for n in ("sevv", "sevs", "smm", "str", "ss", "sgy0", "sgy1",
                  "sgh0", "sgh1", "sgq0", "sgq1"):
            wait("gpsimd", n, cnt[n])

    with nc.Block() as block:
        @block.sync
        def _(e):
            for f in OPS["sync"]:
                f(e)

        @block.scalar
        def _(e):
            for f in OPS["scalar"]:
                f(e)

        @block.vector
        def _(e):
            for f in OPS["vector"]:
                f(e)

        @block.tensor
        def _(e):
            for f in OPS["tensor"]:
                f(e)

        @block.gpsimd
        def _(e):
            for f in OPS["gpsimd"]:
                f(e)

    es.close()
    return nc


def _prep_inputs(L0_attn_x, L0_attn_W_rvk, L0_attn_W_out, L0_attn_time_decay,
                 L0_attn_time_first, L0_ffn_w1, L0_ffn_w2, L0_ffn_w3):
    x = np.ascontiguousarray(np.asarray(L0_attn_x, dtype=np.float32))
    W_rvk = np.asarray(L0_attn_W_rvk, dtype=np.float32)
    W_out = np.asarray(L0_attn_W_out, dtype=np.float32)
    td = np.asarray(L0_attn_time_decay, dtype=np.float32)
    tf = np.asarray(L0_attn_time_first, dtype=np.float32)
    w1 = np.asarray(L0_ffn_w1, dtype=np.float32)
    w2 = np.asarray(L0_ffn_w2, dtype=np.float32)
    w3 = np.asarray(L0_ffn_w3, dtype=np.float32)

    WT = W_rvk.T
    wrvk = np.empty((CCN, 128, 3, CCN, 128), np.float32)
    for j, base in ((0, 2 * C), (1, C), (2, 0)):
        blk = WT[:, base:base + C].reshape(CCN, 128, CCN, 128)
        wrvk[:, :, j, :, :] = blk.transpose(2, 1, 0, 3)
    wrvk = np.ascontiguousarray(wrvk.reshape(CCN, 128, 3 * CCN * 128))

    w12 = np.empty((HHN, 2, 128, C), np.float32)
    for m, w in ((0, w1), (1, w2)):
        t = w.reshape(HHN, 128, CCN, 128)
        w12[:, m, :, :] = t.transpose(0, 3, 2, 1).reshape(HHN, 128, C)
    w12 = np.ascontiguousarray(w12)

    w3t = np.ascontiguousarray(w3.reshape(NDT5, DT5, HHN, 128).transpose(0, 2, 3, 1))

    wo = W_out.reshape(NDT5, DT5, CCN, 128).transpose(0, 3, 2, 1)
    wout = np.ascontiguousarray(wo.reshape(NDT5, 128, CCN * DT5))

    tdt = np.ascontiguousarray(td.reshape(CCN, 128).T)
    tft = np.ascontiguousarray(tf.reshape(CCN, 128).T)
    eye = np.eye(128, dtype=ml_dtypes.bfloat16)

    in_maps = []
    for core in range(8):
        b, h = core // 2, core % 2
        if h == 0:
            xs = np.concatenate([x[b, 0:WIN], x[b, 0:T_OWN]], axis=0)
            mask = np.zeros((128, 1), np.float32)
        else:
            xs = x[b, T_OWN - WIN: 2 * T_OWN]
            mask = np.ones((128, 1), np.float32)
        in_maps.append({
            "x": np.ascontiguousarray(xs),
            "wrvk": wrvk, "w12": w12, "w3t": w3t, "wout": wout,
            "td": tdt, "tf": tft, "mask": mask, "eye": eye,
        })
    return in_maps


def kernel(L0_attn_x, L0_attn_W_rvk, L0_attn_W_out, L0_attn_time_decay,
           L0_attn_time_first, L0_ffn_w1, L0_ffn_w2, L0_ffn_w3):
    in_maps = _prep_inputs(L0_attn_x, L0_attn_W_rvk, L0_attn_W_out,
                           L0_attn_time_decay, L0_attn_time_first,
                           L0_ffn_w1, L0_ffn_w2, L0_ffn_w3)
    nc = build_nc()
    res = run_bass_kernel_spmd(nc, in_maps, core_ids=list(range(8)))
    B, T = 4, 4096
    out = np.empty((B, T, C), np.float32)
    for core in range(8):
        b, h = core // 2, core % 2
        out[b, h * T_OWN:(h + 1) * T_OWN, :] = res.results[core]["out"]
    return out


# version 05
# speedup vs baseline: 1.4456x; 1.0914x over previous
import numpy as np
import ml_dtypes
import contextlib
import concourse.bass as bass
import concourse.mybir as mybir
from concourse.bass_utils import run_bass_kernel_spmd

F32 = mybir.dt.float32
BF16 = mybir.dt.bfloat16
ALU = mybir.AluOpType
ACT = mybir.ActivationFunctionType
AX = mybir.AxisListType

C = 2048
CCN = 16
H = 5632
HHN = 44
WIN = 128
T_OWN = 2048
T_KV = T_OWN + WIN
NBLK = T_KV // 128
EPS_LN = 1e-5
EPS_Q = 1e-8
NW = 5632.0 * 2048.0

SC = [(0, 1024), (1024, 1152)]
SCW = 1152
KV_TT = [(0, 512), (512, 512), (1024, 512), (1536, 512), (2048, 128)]
R_TT = [(WIN, 512), (WIN + 512, 512), (WIN + 1024, 512), (WIN + 1536, 512)]
TT5 = 512
NTT5 = T_OWN // TT5
NTB5 = TT5 // 128
DT5 = 512
NDT5 = C // DT5


def build_nc():
    import os as _os
    _norace = _os.environ.get("ATHENA_SIM_NORACE", "") == "1"
    nc = bass.Bass("TRN2", target_bir_lowering=False,
                   detect_race_conditions=not _norace)

    x_d = nc.dram_tensor("x", [T_KV, C], F32, kind="ExternalInput").ap()
    wrvk_d = nc.dram_tensor("wrvk", [CCN, 128, 3 * CCN * 128], F32, kind="ExternalInput").ap()
    w12_d = nc.dram_tensor("w12", [HHN, 2, 128, C], F32, kind="ExternalInput").ap()
    w3t_d = nc.dram_tensor("w3t", [NDT5, HHN, 128, DT5], F32, kind="ExternalInput").ap()
    wout_d = nc.dram_tensor("wout", [NDT5, 128, CCN * DT5], F32, kind="ExternalInput").ap()
    td_d = nc.dram_tensor("td", [128, CCN], F32, kind="ExternalInput").ap()
    tf_d = nc.dram_tensor("tf", [128, CCN], F32, kind="ExternalInput").ap()
    mask_d = nc.dram_tensor("mask", [128, 1], F32, kind="ExternalInput").ap()
    eye_d = nc.dram_tensor("eye", [128, 128], BF16, kind="ExternalInput").ap()
    out_d = nc.dram_tensor("out", [T_OWN, C], F32, kind="ExternalOutput").ap()
    ysp_d = nc.dram_tensor("ysp", [128, CCN * T_OWN], BF16, kind="Internal").ap()
    hidsp_d = nc.dram_tensor("hidsp", [128, HHN * T_OWN], BF16, kind="Internal").ap()
    q3sp_d = nc.dram_tensor("q3sp", [128, NDT5 * HHN * DT5], BF16, kind="Internal").ap()

    es = contextlib.ExitStack()

    SB_BASE = (nc._sbuf_addr_for_side("left") + 63) // 64 * 64
    SB_TOP = nc._sbuf_addr_for_side("right")
    off = [SB_BASE]

    def sb(name, shape, dtype, at=None):
        nbytes = int(np.prod(shape[1:])) * (2 if dtype == BF16 else 4)
        if at is None:
            at = off[0]
            off[0] = at + ((nbytes + 63) // 64) * 64
        return nc.alloc_sbuf_tensor_at(name, shape, dtype, offset=at), at

    def sba(name, shape, dtype, at=None):
        return sb(name, shape, dtype, at)[0]

    consts = sba("consts", [128, 16], F32)
    dch = sba("dch", [128, CCN], F32)
    euch = sba("euch", [128, CCN], F32)
    tdt = sba("tdt", [128, CCN], F32)
    etd = sba("etd", [128, CCN], F32)
    tft = sba("tft", [128, CCN], F32)
    eye_t = sba("eye", [128, 128], BF16)
    sum_s = sba("sum_s", [128, NBLK], F32)
    sq_s = sba("sq_s", [128, NBLK], F32)
    mean_s = sba("mean_s", [128, NBLK], F32)
    m2_s = sba("m2_s", [128, NBLK], F32)
    var_s = sba("var_s", [128, NBLK], F32)
    std_s = sba("std_s", [128, NBLK], F32)
    rstd_s = sba("rstd_s", [128, NBLK], F32)
    gacc = sba("gacc", [128, 144], F32)
    gred = sba("gred", [128, 8], F32)
    g1 = sba("g1", [1, 8], F32)
    canary = sba("canary", [128, 8], F32)
    ones_t = sba("ones_t", [128, 128], F32)
    sgn3, sgn3_at = sb("sgn3", [128, DT5], BF16)
    q3st = sba("q3st", [128, 2 * DT5], BF16)
    w3qb = sba("w3qb", [128, 2 * DT5], F32)
    tanh4 = sba("tanh4", [128, 2 * 512], F32)
    hidum, hid_at = sb("hidum", [128, C], F32)
    hidb = sba("hidb", [128, 2 * C], BF16, at=hid_at)
    gdum = hidum
    xsh = sba("xsh", [128, 2 * C], F32)
    big0, big0_at = sb("big0", [128, CCN * T_KV], BF16)
    union_at = off[0]
    wkb = sba("wkb", [128, 2 * 3 * CCN * 128], BF16)
    ekt, ekt_at = sb("ekt", [128, 4 * SCW], F32)
    vt = sba("vt", [128, 2 * SCW], F32)
    pbuf = sba("pbuf", [128, SCW + 1], F32)
    qbuf = sba("qbuf", [128, SCW + 1], F32)
    dte = sba("dte", [128, SCW], F32)
    ybf = sba("ybf", [128, 2 * SCW], BF16)
    trt = sba("trt", [128, 2 * T_OWN], F32)
    sgn = sba("sgn", [128, C], BF16)
    u_t = sba("u_t", [128, C], BF16)
    thr = sba("thr", [128, 8], F32)
    gapt = sba("gapt", [128, 128], F32)
    w12x = sba("w12x", [128, C], F32)
    end_at = off[0]
    qb = sba("qb", [128, 4 * C], BF16, at=ekt_at)
    xntok = sba("xntok", [128, 2 * C], BF16, at=ekt_at)
    QDTW = (HHN + CCN) * DT5
    qdt2 = sba("qdt2", [128, 2 * QDTW], BF16, at=big0_at + HHN * TT5 * 2)
    assert big0_at + HHN * TT5 * 2 + 2 * QDTW * 2 <= SB_TOP, "qdt2 overflow"
    y5 = sba("y5", [128, CCN * TT5], BF16, at=sgn3_at)
    xt5 = sba("xt5", [128, 4 * 512], F32, at=sgn3_at + CCN * TT5 * 2)
    ob5 = sba("ob5", [128, 4 * 512], F32, at=sgn3_at + CCN * TT5 * 2 + 4 * 512 * 4)
    assert end_at <= SB_TOP, f"sbuf overflow: {end_at} > {SB_TOP}"

    NMB = 6
    mmb = [es.enter_context(nc.psum_tensor(f"mmb{i}", [128, 512], F32)) for i in range(NMB)]
    ps_tr = [es.enter_context(nc.psum_tensor("ps_tr0", [128, 1024], BF16)),
             es.enter_context(nc.psum_tensor("ps_tr1", [128, 1024], BF16))]

    sem_names = ["sx", "sxA", "sxB", "swgA", "swgB", "sw5", "sws", "sscA", "sscB",
                 "smm", "str", "sevv", "sevs", "sv", "ss",
                 "sgy0", "sgy1", "sgh0", "sgh1", "sgq0", "sgq1",
                 "spx0", "spx1", "spx2", "spx3",
                 "sst0", "sst1", "sst2", "sst3", "sp5", "sth", "sws0", "sws1", "sws2", "sp5b", "sw5b"]
    sems = {n: es.enter_context(nc.semaphore(n)) for n in sem_names}
    cnt = {n: 0 for n in sem_names}
    OPS = {e: [] for e in ["sync", "scalar", "vector", "tensor", "gpsimd"]}

    def emit(eng, fn):
        OPS[eng].append(fn)

    def op_inc(eng, name, by, fn):
        cnt[name] += by
        v = cnt[name]
        sem = sems[name]
        emit(eng, lambda e, fn=fn, sem=sem, by=by: fn(e).then_inc(sem, by))
        return v

    def wait(eng, name, v):
        if v <= 0:
            return
        sem = sems[name]
        emit(eng, lambda e, sem=sem, v=v: e.wait_ge(sem, v))

    def vgap():
        emit("vector", lambda v: v.memset(gapt[:, 0:64], 0.0))

    def sgap():
        emit("scalar", lambda s: s.memzero(gapt[:, 64:128]))

    bank_free = [("sevv", 0)] * NMB
    bank_i = [0]

    def next_bank():
        b = bank_i[0] % NMB
        bank_i[0] += 1
        return b

    ST = {}

    def p0():
        op_inc("sync", "sx", 16, lambda e: e.dma_start(out=tdt[:], in_=td_d[:]))
        op_inc("sync", "sx", 16, lambda e: e.dma_start(out=tft[:], in_=tf_d[:]))
        op_inc("sync", "sx", 16, lambda e: e.dma_start(out=consts[:, 4:5], in_=mask_d[:]))
        op_inc("sync", "sx", 16, lambda e: e.dma_start(out=eye_t[:], in_=eye_d[:]))
        emit("vector", lambda v: v.memset(consts[:, 0:1], EPS_LN))
        emit("vector", lambda v: v.memset(consts[:, 1:2], 0.5))
        emit("vector", lambda v: v.memset(consts[:, 2:3], -1.0))
        emit("vector", lambda v: v.memset(consts[:, 3:4], 0.0))
        emit("vector", lambda v: v.memset(consts[:, 5:6], 0.25))
        emit("vector", lambda v: v.memset(pbuf[:, 0:1], 0.0))
        emit("vector", lambda v: v.memset(qbuf[:, 0:1], 0.0))
        emit("vector", lambda v: v.memset(gacc[:], 0.0))
        vv = op_inc("vector", "sv", 1, lambda v: v.memset(ones_t[:], 1.0))
        wait("scalar", "sx", 64)
        wait("scalar", "sv", vv)
        emit("scalar", lambda s: s.activation(out=etd[:], in_=tdt[:], func=ACT.Exp))
        op_inc("scalar", "ss", 1, lambda s: s.activation(out=euch[:], in_=tft[:], func=ACT.Exp))
        sgap()
        op_inc("scalar", "ss", 1, lambda s: s.activation(out=dch[:], in_=etd[:], func=ACT.Exp,
                                                         scale=consts[:, 2:3]))

    xn_done = {}
    sq_done = {}
    tr_last = {}
    tr_evac = {}
    trc = [0]

    def emit_xn(b):
        par = b % 2
        if b >= 2:
            wait("vector", "str", tr_last[b - 2])
        xv = op_inc("vector", "sevv", 1,
                    lambda v, b=b, par=par: v.tensor_scalar(
                        out=xntok[:, par * C:(par + 1) * C], in0=xsh[:, par * C:(par + 1) * C],
                        scalar1=mean_s[:, b:b + 1], scalar2=rstd_s[:, b:b + 1],
                        op0=ALU.subtract, op1=ALU.mult))
        xn_done[b] = xv
        wait("tensor", "sevv", xv)
        for g4 in range(CCN // 4):
            g = trc[0]
            bank = g % 2
            if g >= 2:
                wait("tensor", "sevv", tr_evac[g - 2])
            tv = 0
            for i in range(4):
                cc = g4 * 4 + i
                inst = [None]
                tv = op_inc("tensor", "str", 1,
                            lambda t, par=par, cc=cc, i=i, bank=bank: t.transpose(
                                ps_tr[bank][:, i * 256:(i + 1) * 256][:, 0:128]
                                if False else
                                ps_tr[bank][:, i * 128 + 0: i * 128 + 128],
                                xntok[:, par * C + cc * 128: par * C + (cc + 1) * 128],
                                eye_t[:]))
            wait("vector", "str", tv)
            ev = 0
            for i in range(4):
                cc = g4 * 4 + i
                ev = op_inc("vector", "sevv", 1,
                            lambda v, b=b, cc=cc, i=i, bank=bank: v.tensor_copy(
                                out=big0[:, cc * T_KV + b * 128: cc * T_KV + (b + 1) * 128],
                                in_=ps_tr[bank][:, i * 128: i * 128 + 128]))
            tr_evac[g] = ev
            trc[0] += 1
        tr_last[b] = cnt["str"]

    def p1():
        for b in range(NBLK):
            par = b % 2
            if b >= 2:
                wait("sync", "sevv", xn_done[b - 2])
                wait("sync", "sevs", sq_done[b - 2])
            sxp = "sxA" if par == 0 else "sxB"
            lv = op_inc("sync", sxp, 16,
                        lambda e, b=b, par=par: e.dma_start(
                            out=xsh[:, par * C:(par + 1) * C], in_=x_d[b * 128:(b + 1) * 128, :]))
            wait("vector", sxp, lv)
            rv = op_inc("vector", "sevv", 1,
                        lambda v, b=b, par=par: v.reduce_sum(
                            out=sum_s[:, b:b + 1], in_=xsh[:, par * C:(par + 1) * C], axis=AX.X))
            wait("scalar", sxp, lv)
            sqv = op_inc("scalar", "sevs", 1,
                         lambda s, b=b, par=par: s.activation(
                             out=gdum[:], in_=xsh[:, par * C:(par + 1) * C],
                             func=ACT.Square, accum_out=sq_s[:, b:b + 1]))
            sq_done[b] = sqv
            wait("scalar", "sevv", rv)
            emit("scalar", lambda s, b=b: s.mul(out=mean_s[:, b:b + 1], in_=sum_s[:, b:b + 1],
                                                mul=1.0 / C))
            sgap()
            mv = op_inc("scalar", "sevs", 1,
                        lambda s, b=b: s.activation(out=m2_s[:, b:b + 1], in_=mean_s[:, b:b + 1],
                                                    func=ACT.Square))
            wait("vector", "sevs", mv)
            vv = op_inc("vector", "sevv", 1,
                        lambda v, b=b: v.scalar_tensor_tensor(
                            out=var_s[:, b:b + 1], in0=sq_s[:, b:b + 1], scalar=1.0 / C,
                            in1=m2_s[:, b:b + 1], op0=ALU.mult, op1=ALU.subtract))
            wait("scalar", "sevv", vv)
            sv_ = op_inc("scalar", "sevs", 1,
                         lambda s, b=b: s.activation(out=std_s[:, b:b + 1], in_=var_s[:, b:b + 1],
                                                     func=ACT.Sqrt, bias=consts[:, 0:1]))
            wait("vector", "sevs", sv_)
            emit("vector", lambda v, b=b: v.reciprocal(out=rstd_s[:, b:b + 1],
                                                       in_=std_s[:, b:b + 1]))
            if b > 0:
                emit_xn(b - 1)
        emit("vector", lambda v: v.memset(consts[:, 9:10], 0.0))
        emit_xn(NBLK - 1)

    gunits = []

    def make_gamma_units():
        w3flat = w3t_d.rearrange("a b p f -> (a b) p f")
        for blk in range(NDT5 * HHN // 4):
            srcs = [w3flat[4 * blk + i, :, :] for i in range(4)]
            gunits.append((srcs, DT5, 96 + blk))
        for hh in range(HHN):
            for m in range(2):
                gunits.append(([w12_d[hh, m, :, :]], C, m * 48 + hh))

    gstate = {"load": 0, "use": 0, "loadv": {}}

    def pump_gamma(n):
        for _ in range(n):
            for _ld in range(2):
                if gstate["load"] < len(gunits) and gstate["load"] - gstate["use"] < 2:
                    i = gstate["load"]
                    srcs, ncols, col = gunits[i]
                    par = i % 2
                    if i == 0:
                        wait("scalar", "sevv", xn_done[NBLK - 1])
                    lv = 0
                    for si, s_ap in enumerate(srcs):
                        lv = op_inc("scalar", "sscA" if par == 0 else "sscB", 16,
                                    lambda e, s_ap=s_ap, ncols=ncols, par=par, si=si:
                                    e.dma_start(out=xsh[:, par * C + si * ncols:
                                                        par * C + (si + 1) * ncols], in_=s_ap))
                    gstate["loadv"][i] = lv
                    gstate["load"] += 1
            if gstate["use"] < gstate["load"]:
                i = gstate["use"]
                srcs, ncols, col = gunits[i]
                tot = ncols * len(srcs)
                par = i % 2
                wait("scalar", "sscA" if par == 0 else "sscB", gstate["loadv"][i])
                op_inc("scalar", "ss", 1,
                       lambda s, tot=tot, col=col, par=par: s.activation(
                           out=gdum[:, 0:tot], in_=xsh[:, par * C: par * C + tot],
                           func=ACT.Abs, accum_out=gacc[:, col:col + 1]))
                gstate["use"] += 1

    def gamma_finish():
        pump_gamma(len(gunits))
        ST["gamma_ss"] = cnt["ss"]
        wait("vector", "ss", cnt["ss"])
        rv = 0
        for m in range(3):
            rv = op_inc("vector", "sevv", 1,
                        lambda v, m=m: v.reduce_sum(out=gred[:, m:m + 1],
                                                    in_=gacc[:, m * 48:(m + 1) * 48], axis=AX.X))
        wait("tensor", "sevv", rv)
        b1 = next_bank()
        wait("tensor", *bank_free[b1])
        mv = op_inc("tensor", "smm", 1,
                    lambda t, b1=b1: t.matmul(mmb[b1][0:1, 0:3], ones_t[:, 0:1], gred[:, 0:3],
                                              start=True, stop=True))
        wait("vector", "smm", mv)
        cv = op_inc("vector", "sevv", 1, lambda v, b1=b1: v.tensor_copy(
            out=g1[0:1, 0:3], in_=mmb[b1][0:1, 0:3]))
        bank_free[b1] = ("sevv", cv)
        wait("tensor", "sevv", cv)
        b2 = next_bank()
        wait("tensor", *bank_free[b2])
        mv2 = op_inc("tensor", "smm", 1,
                     lambda t, b2=b2: t.matmul(mmb[b2][:, 4:7], ones_t[0:1, :], g1[0:1, 0:3],
                                               start=True, stop=True))
        wait("vector", "smm", mv2)
        tv_ = op_inc("vector", "sth", 1,
                     lambda v, b2=b2: v.tensor_scalar(out=thr[:, 0:3], in0=mmb[b2][:, 4:7],
                                                      scalar1=0.5 / NW, scalar2=0.5 * EPS_Q,
                                                      op0=ALU.mult, op1=ALU.add))
        bank_free[b2] = ("sth", tv_)
        vgap()
        vgap()
        ST["thr_done"] = op_inc("vector", "sth", 1,
                                lambda v: v.tensor_scalar(out=thr[:, 3:6], in0=thr[:, 0:3],
                                                          scalar1=-1.0, scalar2=None,
                                                          op0=ALU.mult))


    last_q = {"v": 0}

    def quantize_sgn(src_ap, dst_ap, m, sg_ap):
        wait("scalar", "sevv", last_q["v"])
        op_inc("scalar", "sevs", 1,
               lambda s, src_ap=src_ap, sg_ap=sg_ap, m=m: s.activation(
                   out=sg_ap, in_=src_ap, func=ACT.Sign, bias=thr[:, 3 + m:4 + m]))
        a2 = op_inc("scalar", "sevs", 1,
                    lambda s, src_ap=src_ap, dst_ap=dst_ap, m=m: s.activation(
                        out=dst_ap, in_=src_ap, func=ACT.Sign, bias=thr[:, m:m + 1]))
        wait("vector", "sevs", a2)
        qv = op_inc("vector", "sevv", 1,
                    lambda v, dst_ap=dst_ap, sg_ap=sg_ap: v.tensor_add(
                        out=dst_ap, in0=dst_ap, in1=sg_ap))
        last_q["v"] = qv
        return qv

    def quantize_vec(src_ap, dst_ap, m):
        op_inc("vector", "sevv", 1,
               lambda v, src_ap=src_ap, m=m: v.tensor_scalar(
                   out=u_t[:, 0:src_ap.shape[1]], in0=src_ap, scalar1=thr[:, 3 + m:4 + m],
                   scalar2=None, op0=ALU.is_le))
        vgap()
        return op_inc("vector", "sevv", 1,
                      lambda v, src_ap=src_ap, dst_ap=dst_ap, m=m: v.scalar_tensor_tensor(
                          out=dst_ap, in0=src_ap, scalar=thr[:, m:m + 1],
                          in1=u_t[:, 0:src_ap.shape[1]], op0=ALU.is_ge, op1=ALU.subtract))

    q3units = []

    def make_q3_units():
        for dt in range(NDT5):
            for hh in range(HHN):
                q3units.append((dt, hh))

    qstate = {"load": 0, "use": 0, "loadv": {}, "spw": {}}

    def pump_q3(n):
        for _ in range(n):
            for _ld in range(2):
                if qstate["load"] < len(q3units) and qstate["load"] - qstate["use"] < 2:
                    i = qstate["load"]
                    dt, hh = q3units[i]
                    par = i % 2
                    if i == 0:
                        wait("scalar", "sth", ST["thr_done"])
                    if i >= 2:
                        wait("scalar", "sgq0" if i % 2 == 0 else "sgq1", qstate["spw"][i - 2])
                    qstate["loadv"][i] = op_inc("scalar", "sscA" if par == 0 else "sscB", 16,
                                                lambda e, dt=dt, hh=hh, par=par: e.dma_start(
                                                    out=w3qb[:, par * DT5:(par + 1) * DT5],
                                                    in_=w3t_d[dt, hh, :, :]))
                    qstate["load"] += 1
            if qstate["use"] < qstate["load"]:
                i = qstate["use"]
                dt, hh = q3units[i]
                par = i % 2
                wait("scalar", "sscA" if par == 0 else "sscB", qstate["loadv"][i])
                qv = quantize_sgn(w3qb[:, par * DT5:(par + 1) * DT5],
                                  q3st[:, par * DT5:(par + 1) * DT5], 2, sgn3[:])
                wait("gpsimd", "sevv", qv)
                qstate["spw"][i] = op_inc("gpsimd", "sgq0" if i % 2 == 0 else "sgq1", 16,
                                          lambda e, dt=dt, hh=hh, par=par: e.dma_start(
                                              out=q3sp_d[:, (dt * HHN + hh) * DT5:
                                                         (dt * HHN + hh + 1) * DT5],
                                              in_=q3st[:, par * DT5:(par + 1) * DT5]))
                qstate["use"] += 1

    cons = {}
    ysp_done = {}

    def p3():
        import os as _os
        NCC = int(_os.environ.get("ATHENA_DBG_CC", str(CCN)))
        SKIP_SCAN = _os.environ.get("ATHENA_DBG_NOSCAN", "") == "1"
        NWKB = 3 * CCN * 128
        wk_used = {}
        for cc in range(NCC):
            par = cc % 2
            if cc >= 2:
                wait("gpsimd", "smm", wk_used[cc - 2])
            swp = "swgA" if par == 0 else "swgB"
            lw = op_inc("gpsimd", swp, 16,
                        lambda e, cc=cc, par=par: e.dma_start(
                            out=wkb[:, par * NWKB:(par + 1) * NWKB], in_=wrvk_d[cc, :, :]))
            wait("tensor", swp, lw)

            for j, tts in ((0, KV_TT), (1, KV_TT), (2, R_TT)):
                for (t0, tl) in tts:
                    b = next_bank()
                    wait("tensor", *bank_free[b])
                    for c in range(CCN - 1):
                        lhs = wkb[:, par * NWKB + (j * CCN + c) * 128:
                                  par * NWKB + (j * CCN + c) * 128 + 128]
                        rhs = big0[:, c * T_KV + t0: c * T_KV + t0 + tl]
                        emit("tensor", lambda t, lhs=lhs, rhs=rhs, b=b, c=c, tl=tl:
                             t.matmul(mmb[b][:, 0:tl], lhs, rhs,
                                      start=(c == 0), stop=False))
                    c = CCN - 1
                    lhs = wkb[:, par * NWKB + (j * CCN + c) * 128:
                              par * NWKB + (j * CCN + c) * 128 + 128]
                    rhs = big0[:, c * T_KV + t0: c * T_KV + t0 + tl]
                    gv = op_inc("tensor", "smm", 1,
                                lambda t, lhs=lhs, rhs=rhs, b=b, tl=tl:
                                t.matmul(mmb[b][:, 0:tl], lhs, rhs, start=False, stop=True))
                    wk_used[cc] = gv
                    pump_gamma(1)
                    ch = 0 if t0 < SC[1][0] else 1
                    c0 = SC[ch][0]
                    if j == 0:
                        ep = (cc % 2) * 2 * SCW
                        dst = ekt[:, ep + ch * SCW + (t0 - c0): ep + ch * SCW + (t0 - c0) + tl]
                        wait("scalar", "smm", gv)
                        if cc >= 2:
                            wait("scalar", "sevv", cons[cc - 2])
                        ev = op_inc("scalar", "sevs", 1,
                                    lambda s, dst=dst, b=b, tl=tl: s.activation(
                                        out=dst, in_=mmb[b][:, 0:tl], func=ACT.Exp))
                        bank_free[b] = ("sevs", ev)
                    elif j == 1:
                        dst = vt[:, ch * SCW + (t0 - c0): ch * SCW + (t0 - c0) + tl]
                        wait("vector", "smm", gv)
                        ev = op_inc("vector", "sevv", 1,
                                    lambda v, dst=dst, b=b, tl=tl: v.tensor_copy(
                                        out=dst, in_=mmb[b][:, 0:tl]))
                        bank_free[b] = ("sevv", ev)
                    else:
                        o0 = (cc % 2) * T_OWN + t0 - WIN
                        wait("scalar", "smm", gv)
                        if cc >= 2:
                            wait("scalar", "sevv", cons[cc - 2])
                        ev = op_inc("scalar", "sevs", 1,
                                    lambda s, o0=o0, b=b, tl=tl: s.activation(
                                        out=trt[:, o0:o0 + tl], in_=mmb[b][:, 0:tl],
                                        func=ACT.Tanh, scale=consts[:, 1:2]))
                        bank_free[b] = ("sevs", ev)

            wait("vector", "sevs", cnt["sevs"])
            if SKIP_SCAN:
                cons[cc] = op_inc("vector", "sevv", 1, lambda v: v.memset(consts[:, 8:9], 0.0))
                for _ch in range(2):
                    k_ = cc * 2 + _ch
                    ysp_done[k_] = cnt["sgy0"] if k_ % 2 == 0 else cnt["sgy1"]
                continue
            ep = (cc % 2) * 2 * SCW
            tp = (cc % 2) * T_OWN
            emit("vector", lambda v, ep=ep: v.tensor_scalar(
                out=ekt[:, ep: ep + WIN], in0=ekt[:, ep: ep + WIN],
                scalar1=consts[:, 4:5], scalar2=None, op0=ALU.mult))
            vgap()
            for ch, (c0, cl) in enumerate(SC):
                ek_c = ekt[:, ep + ch * SCW: ep + ch * SCW + cl]
                v_c = vt[:, ch * SCW: ch * SCW + cl]
                pb = pbuf[:]
                qbv = qbuf[:]
                d_bc = dch[:, cc:cc + 1].broadcast_to([128, cl])
                oo, ol = (WIN, 1024 - WIN) if ch == 0 else (0, cl)
                own_off = tp + (0 if ch == 0 else 1024 - WIN)
                init_q = consts[:, 3:4] if ch == 0 else qbv[:, 0:1]
                init_p = consts[:, 3:4] if ch == 0 else pb[:, 0:1]
                emit("vector", lambda v, qbv=qbv, d_bc=d_bc, ek_c=ek_c, cl=cl, init_q=init_q:
                     v.tensor_tensor_scan(out=qbv[:, 1:1 + cl], data0=d_bc, data1=ek_c,
                                          initial=init_q, op0=ALU.mult, op1=ALU.add))
                emit("vector", lambda v, own_off=own_off, ol=ol: v.tensor_scalar(
                    out=trt[:, own_off:own_off + ol], in0=trt[:, own_off:own_off + ol],
                    scalar1=0.5, scalar2=0.5, op0=ALU.mult, op1=ALU.add))
                emit("vector", lambda v, ek_c=ek_c, qbv=qbv, cc=cc, oo=oo, ol=ol:
                     v.scalar_tensor_tensor(out=dte[:, 0:ol], in0=ek_c[:, oo:oo + ol],
                                            scalar=euch[:, cc:cc + 1],
                                            in1=qbv[:, oo:oo + ol] if ol != 1024 - WIN
                                            else qbv[:, 1 + oo - 1: 1 + oo - 1 + ol],
                                            op0=ALU.mult, op1=ALU.add))
                if ch + 1 < len(SC):
                    emit("vector", lambda v, qbv=qbv, cl=cl: v.tensor_copy(
                        out=qbuf[:, 0:1], in_=qbv[:, cl:cl + 1]))
                emit("vector", lambda v, ek_c=ek_c, v_c=v_c: v.tensor_mul(
                    out=ek_c, in0=ek_c, in1=v_c))
                vgap()
                emit("vector", lambda v, pb=pb, d_bc=d_bc, ek_c=ek_c, cl=cl, init_p=init_p:
                     v.tensor_tensor_scan(out=pb[:, 1:1 + cl], data0=d_bc, data1=ek_c,
                                          initial=init_p, op0=ALU.mult, op1=ALU.add))
                emit("vector", lambda v, ol=ol: v.reciprocal(
                    out=dte[:, 0:ol], in_=dte[:, 0:ol]))
                emit("vector", lambda v, v_c=v_c, ek_c=ek_c, pb=pb, cc=cc, oo=oo, ol=ol:
                     v.scalar_tensor_tensor(out=v_c[:, 0:ol], in0=ek_c[:, oo:oo + ol],
                                            scalar=euch[:, cc:cc + 1],
                                            in1=pb[:, oo:oo + ol] if ol != 1024 - WIN
                                            else pb[:, 1 + oo - 1: 1 + oo - 1 + ol],
                                            op0=ALU.mult, op1=ALU.add))
                if ch + 1 < len(SC):
                    emit("vector", lambda v, pb=pb, cl=cl: v.tensor_copy(
                        out=pbuf[:, 0:1], in_=pb[:, cl:cl + 1]))
                vgap()
                emit("vector", lambda v, qbv=qbv, v_c=v_c, ol=ol: v.tensor_mul(
                    out=qbv[:, 1:1 + ol], in0=v_c[:, 0:ol], in1=dte[:, 0:ol]))
                k = cc * 2 + ch
                ypar = k % 2
                if k >= 2:
                    wait("vector", "sgy0" if ypar == 0 else "sgy1", ysp_done[k - 2])
                vgap()
                yv = op_inc("vector", "sevv", 1,
                            lambda v, own_off=own_off, ol=ol, qbv=qbv, ypar=ypar:
                            v.tensor_mul(out=ybf[:, ypar * SCW: ypar * SCW + ol],
                                         in0=trt[:, own_off:own_off + ol],
                                         in1=qbv[:, 1:1 + ol]))
                wait("sync", "sevv", yv)
                o_lo = own_off - tp
                sv_ = 0
                for tt_ in range(o_lo // 512, (o_lo + ol + 511) // 512):
                    lo = max(o_lo, tt_ * 512)
                    hi = min(o_lo + ol, (tt_ + 1) * 512)
                    sv_ = op_inc("sync", "sgy0" if ypar == 0 else "sgy1", 16,
                                 lambda e, cc=cc, tt_=tt_, lo=lo, hi=hi, o_lo=o_lo, ypar=ypar:
                                 e.dma_start(
                                     out=ysp_d[:, tt_ * CCN * 512 + cc * 512 + lo - tt_ * 512:
                                               tt_ * CCN * 512 + cc * 512 + hi - tt_ * 512],
                                     in_=ybf[:, ypar * SCW + lo - o_lo: ypar * SCW + hi - o_lo]))
                ysp_done[k] = sv_
            cons[cc] = op_inc("vector", "sevv", 1, lambda v: v.memset(consts[:, 8:9], 0.0))


    hid_ev = {}
    hid_spill = {}
    mm_after_hh = {}

    def p4():
        wait("sync", "ss", ST["gamma_ss"])
        q_ready = {}
        qdone = {}
        wslot = {}
        slots = [xsh[:, 0:C], xsh[:, C:2 * C], w12x[:]]

        def load_w12(hh):
            for m in (0, 1):
                i = 2 * hh + m
                s_ = i % 3
                if i >= 3:
                    wait("sync", "sevv", qdone[i - 3])
                wslot[i] = op_inc("sync", f"sws{s_}", 16,
                                  lambda e, hh=hh, m=m, s_=s_: e.dma_start(
                                      out=slots[s_], in_=w12_d[hh, m, :, :]))

        def quant_w12(hh):
            par = hh % 2
            if hh >= 2:
                wait("scalar", "smm", mm_after_hh[hh - 2] if hh - 2 in mm_after_hh
                     else 0)
                wait("vector", "smm", mm_after_hh[hh - 2] if hh - 2 in mm_after_hh
                     else 0)
            s1_, s2_ = (2 * hh) % 3, (2 * hh + 1) % 3
            wait("scalar", f"sws{s1_}", wslot[2 * hh])
            if hh == 0:
                wait("scalar", "sth", ST["thr_done"])
            qdone[2 * hh] = quantize_sgn(slots[s1_], qb[:, (par * 2) * C:(par * 2 + 1) * C],
                                         0, sgn[:])
            wait("vector", f"sws{s2_}", wslot[2 * hh + 1])
            qv = quantize_vec(slots[s2_], qb[:, (par * 2 + 1) * C:(par * 2 + 2) * C], 1)
            q_ready[hh] = qv
            qdone[2 * hh + 1] = qv

        load_w12(0)
        quant_w12(0)
        for hh in range(HHN):
            par = hh % 2
            if hh + 1 < HHN:
                load_w12(hh + 1)
                quant_w12(hh + 1)
            wait("tensor", "sevv", q_ready[hh])
            for ti, (t0, tl) in enumerate(R_TT):
                b1 = next_bank()
                wait("tensor", *bank_free[b1])
                for c in range(CCN - 1):
                    lhs = qb[:, (par * 2) * C + c * 128:(par * 2) * C + (c + 1) * 128]
                    rhs = big0[:, c * T_KV + t0: c * T_KV + t0 + tl]
                    emit("tensor", lambda t, lhs=lhs, rhs=rhs, b1=b1, c=c, tl=tl:
                         t.matmul(mmb[b1][:, 0:tl], lhs, rhs, start=(c == 0), stop=False))
                c = CCN - 1
                lhs = qb[:, (par * 2) * C + c * 128:(par * 2) * C + (c + 1) * 128]
                rhs = big0[:, c * T_KV + t0: c * T_KV + t0 + tl]
                g1v = op_inc("tensor", "smm", 1,
                             lambda t, lhs=lhs, rhs=rhs, b1=b1, tl=tl:
                             t.matmul(mmb[b1][:, 0:tl], lhs, rhs, start=False, stop=True))
                b2 = next_bank()
                wait("tensor", *bank_free[b2])
                for c in range(CCN - 1):
                    lhs = qb[:, (par * 2 + 1) * C + c * 128:(par * 2 + 1) * C + (c + 1) * 128]
                    rhs = big0[:, c * T_KV + t0: c * T_KV + t0 + tl]
                    emit("tensor", lambda t, lhs=lhs, rhs=rhs, b2=b2, c=c, tl=tl:
                         t.matmul(mmb[b2][:, 0:tl], lhs, rhs, start=(c == 0), stop=False))
                c = CCN - 1
                lhs = qb[:, (par * 2 + 1) * C + c * 128:(par * 2 + 1) * C + (c + 1) * 128]
                rhs = big0[:, c * T_KV + t0: c * T_KV + t0 + tl]
                g2v = op_inc("tensor", "smm", 1,
                             lambda t, lhs=lhs, rhs=rhs, b2=b2, tl=tl:
                             t.matmul(mmb[b2][:, 0:tl], lhs, rhs, start=False, stop=True))
                mm_after_hh[hh] = g2v
                k = hh * 4 + ti
                tpar = ti % 2
                wait("scalar", "smm", g1v)
                if k >= 2:
                    wait("scalar", "sevv", hid_ev[k - 2])
                e1 = op_inc("scalar", "sevs", 1,
                            lambda s, b1=b1, tpar=tpar, tl=tl: s.activation(
                                out=tanh4[:, tpar * 512: tpar * 512 + tl],
                                in_=mmb[b1][:, 0:tl], func=ACT.Tanh, scale=consts[:, 5:6]))
                bank_free[b1] = ("sevs", e1)
                o0 = t0 - WIN
                hpar = ti % 2
                wait("vector", "sevs", e1)
                wait("vector", "smm", g2v)
                if k >= 2:
                    wait("vector", "sgh0" if k % 2 == 0 else "sgh1", hid_spill[k - 2])
                emit("vector", lambda v, tpar=tpar, hpar=hpar, tl=tl: v.tensor_scalar(
                    out=hidb[:, hpar * C: hpar * C + tl],
                    in0=tanh4[:, tpar * 512: tpar * 512 + tl],
                    scalar1=0.25, scalar2=0.25, op0=ALU.mult, op1=ALU.add))
                vgap()
                e3 = op_inc("vector", "sevv", 1,
                            lambda v, b2=b2, hpar=hpar, tl=tl: v.tensor_mul(
                                out=hidb[:, hpar * C: hpar * C + tl],
                                in0=hidb[:, hpar * C: hpar * C + tl], in1=mmb[b2][:, 0:tl]))
                hid_ev[k] = e3
                bank_free[b2] = ("sevv", e3)
                wait("gpsimd", "sevv", e3)
                hid_spill[k] = op_inc("gpsimd", "sgh0" if k % 2 == 0 else "sgh1", 16,
                                      lambda e, hh=hh, ti=ti, tl=tl, hpar=hpar: e.dma_start(
                                          out=hidsp_d[:, ti * HHN * 512 + hh * 512:
                                                      ti * HHN * 512 + hh * 512 + tl],
                                          in_=hidb[:, hpar * C: hpar * C + tl]))
                pump_q3(1)
        pump_q3(len(q3units))
        ST["hid_sg"] = (cnt["sgh0"], cnt["sgh1"])
        ST["q3_sg"] = (cnt["sgq0"], cnt["sgq1"])
        ST["y_sg"] = (cnt["sgy0"], cnt["sgy1"])
        ST["p4_mm"] = cnt["smm"]

    def p5():
        hoff = 0
        mm_tti = {}
        mm_g = {}
        xused = {}
        slot_free = {}
        eidx = [0]
        for tti in range(NTT5):
            wait("sync", "sgh0", ST["hid_sg"][0])
            wait("sync", "sgh1", ST["hid_sg"][1])
            wait("sync", "sgq0", ST["q3_sg"][0])
            wait("sync", "sgq1", ST["q3_sg"][1])
            wait("sync", "smm", mm_tti.get(tti - 1, ST["p4_mm"]))
            wait("scalar", "sgh0", ST["hid_sg"][0])
            wait("scalar", "sgh1", ST["hid_sg"][1])
            wait("scalar", "sgy0", ST["y_sg"][0])
            wait("scalar", "sgy1", ST["y_sg"][1])
            wait("scalar", "sgq0", ST["q3_sg"][0])
            wait("scalar", "sgq1", ST["q3_sg"][1])
            wait("scalar", "smm", mm_tti.get(tti - 1, ST["p4_mm"]))
            HB = HHN * 512
            op_inc("sync", "sp5", 16,
                   lambda e, tti=tti: e.dma_start(
                       out=big0[:, hoff: hoff + HB // 2],
                       in_=hidsp_d[:, tti * HB: tti * HB + HB // 2]))
            op_inc("scalar", "sscA", 16,
                   lambda e, tti=tti: e.dma_start(
                       out=big0[:, hoff + HB // 2: hoff + HB],
                       in_=hidsp_d[:, tti * HB + HB // 2: (tti + 1) * HB]))
            yl = op_inc("scalar", "sscB", 16,
                        lambda e, tti=tti: e.dma_start(
                            out=y5[:], in_=ysp_d[:, tti * CCN * 512:(tti + 1) * CCN * 512]))
            for dt in range(NDT5):
                g = tti * NDT5 + dt
                slot = g % 2
                qsem = "sp5" if slot == 0 else "sp5b"
                wsem = "sw5" if slot == 0 else "sw5b"
                prev = mm_g.get(g - 2, ST["p4_mm"])
                wait("scalar", "smm", prev)
                wait("gpsimd", "smm", prev)
                ql = op_inc("scalar", qsem, 16,
                            lambda e, dt=dt, slot=slot: e.dma_start(
                                out=qdt2[:, slot * QDTW: slot * QDTW + HHN * DT5],
                                in_=q3sp_d[:, dt * HHN * DT5:(dt + 1) * HHN * DT5]))
                wl = op_inc("gpsimd", wsem, 16,
                            lambda e, dt=dt, slot=slot: e.dma_start(
                                out=qdt2[:, slot * QDTW + HHN * DT5: (slot + 1) * QDTW],
                                in_=wout_d[dt, :, :]))
                for tb in range(NTB5):
                    tglob = tti * NTB5 + tb
                    xslot = eidx[0] % 4
                    if eidx[0] >= 4:
                        wait("sync", "sevv", xused[eidx[0] - 4])
                    xlv = op_inc("sync", f"spx{xslot}", 16,
                                 lambda e, tglob=tglob, dt=dt, xslot=xslot: e.dma_start(
                                     out=xt5[:, xslot * 512:(xslot + 1) * 512],
                                     in_=x_d[WIN + tglob * 128: WIN + (tglob + 1) * 128,
                                             dt * DT5:(dt + 1) * DT5]))
                    ST[("xl", eidx[0])] = xlv
                    b = next_bank()
                    wait("tensor", *bank_free[b])
                    wait("tensor", qsem, ql)
                    wait("tensor", "sp5", cnt["sp5"])
                    wait("tensor", "sscA", cnt["sscA"])
                    wait("tensor", "sscB", cnt["sscB"])
                    wait("tensor", wsem, wl)
                    for hh in range(HHN):
                        lhs = big0[:, hoff + hh * TT5 + tb * 128:
                                   hoff + hh * TT5 + (tb + 1) * 128]
                        rhs = qdt2[:, slot * QDTW + hh * DT5: slot * QDTW + (hh + 1) * DT5]
                        emit("tensor", lambda t, lhs=lhs, rhs=rhs, b=b, first=(hh == 0):
                             t.matmul(mmb[b][:], lhs, rhs, start=first, stop=False))
                    for cc in range(CCN - 1):
                        lhs = y5[:, cc * TT5 + tb * 128: cc * TT5 + (tb + 1) * 128]
                        rhs = qdt2[:, slot * QDTW + (HHN + cc) * DT5:
                                   slot * QDTW + (HHN + cc + 1) * DT5]
                        emit("tensor", lambda t, lhs=lhs, rhs=rhs, b=b:
                             t.matmul(mmb[b][:], lhs, rhs, start=False, stop=False))
                    cc = CCN - 1
                    lhs = y5[:, cc * TT5 + tb * 128: cc * TT5 + (tb + 1) * 128]
                    rhs = qdt2[:, slot * QDTW + (HHN + cc) * DT5:
                               slot * QDTW + (HHN + cc + 1) * DT5]
                    gv = op_inc("tensor", "smm", 1,
                                lambda t, lhs=lhs, rhs=rhs, b=b:
                                t.matmul(mmb[b][:], lhs, rhs, start=False, stop=True))
                    mm_g[g] = gv
                    mm_tti[tti] = gv
                    oslot = eidx[0] % 4
                    wait("vector", "smm", gv)
                    wait("vector", f"spx{xslot}", ST[("xl", eidx[0])])
                    if eidx[0] >= 4:
                        wait("vector", f"sst{oslot}", slot_free[eidx[0] - 4])
                    ev = op_inc("vector", "sevv", 1,
                                lambda v, b=b, oslot=oslot, xslot=xslot: v.tensor_add(
                                    out=ob5[:, oslot * 512:(oslot + 1) * 512],
                                    in0=mmb[b][:],
                                    in1=xt5[:, xslot * 512:(xslot + 1) * 512]))
                    bank_free[b] = ("sevv", ev)
                    xused[eidx[0]] = ev
                    wait("gpsimd", "sevv", ev)
                    sv = op_inc("gpsimd", f"sst{oslot}", 16,
                                lambda e, tglob=tglob, dt=dt, oslot=oslot: e.dma_start(
                                    out=out_d[tglob * 128:(tglob + 1) * 128,
                                              dt * DT5:(dt + 1) * DT5],
                                    in_=ob5[:, oslot * 512:(oslot + 1) * 512]))
                    slot_free[eidx[0]] = sv
                    eidx[0] += 1
        for s_ in range(4):
            wait("gpsimd", f"sst{s_}", cnt[f"sst{s_}"])

    import os
    DBG = int(os.environ.get("ATHENA_DBG_PHASES", "9"))
    make_gamma_units()
    make_q3_units()
    p0()
    if DBG >= 1:
        p1()
    if DBG >= 2:
        p3()
    if DBG >= 3:
        gamma_finish()
    if DBG >= 4:
        p4()
    if DBG >= 5:
        p5()
    if DBG < 5:
        for n in ("sevv", "sevs", "smm", "str", "ss", "sgy0", "sgy1",
                  "sgh0", "sgh1", "sgq0", "sgq1"):
            wait("gpsimd", n, cnt[n])

    with nc.Block() as block:
        @block.sync
        def _(e):
            for f in OPS["sync"]:
                f(e)

        @block.scalar
        def _(e):
            for f in OPS["scalar"]:
                f(e)

        @block.vector
        def _(e):
            for f in OPS["vector"]:
                f(e)

        @block.tensor
        def _(e):
            for f in OPS["tensor"]:
                f(e)

        @block.gpsimd
        def _(e):
            for f in OPS["gpsimd"]:
                f(e)

    es.close()
    return nc


def _prep_inputs(L0_attn_x, L0_attn_W_rvk, L0_attn_W_out, L0_attn_time_decay,
                 L0_attn_time_first, L0_ffn_w1, L0_ffn_w2, L0_ffn_w3):
    x = np.ascontiguousarray(np.asarray(L0_attn_x, dtype=np.float32))
    W_rvk = np.asarray(L0_attn_W_rvk, dtype=np.float32)
    W_out = np.asarray(L0_attn_W_out, dtype=np.float32)
    td = np.asarray(L0_attn_time_decay, dtype=np.float32)
    tf = np.asarray(L0_attn_time_first, dtype=np.float32)
    w1 = np.asarray(L0_ffn_w1, dtype=np.float32)
    w2 = np.asarray(L0_ffn_w2, dtype=np.float32)
    w3 = np.asarray(L0_ffn_w3, dtype=np.float32)

    WT = W_rvk.T
    wrvk = np.empty((CCN, 128, 3, CCN, 128), np.float32)
    for j, base in ((0, 2 * C), (1, C), (2, 0)):
        blk = WT[:, base:base + C].reshape(CCN, 128, CCN, 128)
        wrvk[:, :, j, :, :] = blk.transpose(2, 1, 0, 3)
    wrvk = np.ascontiguousarray(wrvk.reshape(CCN, 128, 3 * CCN * 128))

    w12 = np.empty((HHN, 2, 128, C), np.float32)
    for m, w in ((0, w1), (1, w2)):
        t = w.reshape(HHN, 128, CCN, 128)
        w12[:, m, :, :] = t.transpose(0, 3, 2, 1).reshape(HHN, 128, C)
    w12 = np.ascontiguousarray(w12)

    w3t = np.ascontiguousarray(w3.reshape(NDT5, DT5, HHN, 128).transpose(0, 2, 3, 1))

    wo = W_out.reshape(NDT5, DT5, CCN, 128).transpose(0, 3, 2, 1)
    wout = np.ascontiguousarray(wo.reshape(NDT5, 128, CCN * DT5))

    tdt = np.ascontiguousarray(td.reshape(CCN, 128).T)
    tft = np.ascontiguousarray(tf.reshape(CCN, 128).T)
    eye = np.eye(128, dtype=ml_dtypes.bfloat16)

    in_maps = []
    for core in range(8):
        b, h = core // 2, core % 2
        if h == 0:
            xs = np.concatenate([x[b, 0:WIN], x[b, 0:T_OWN]], axis=0)
            mask = np.zeros((128, 1), np.float32)
        else:
            xs = x[b, T_OWN - WIN: 2 * T_OWN]
            mask = np.ones((128, 1), np.float32)
        in_maps.append({
            "x": np.ascontiguousarray(xs),
            "wrvk": wrvk, "w12": w12, "w3t": w3t, "wout": wout,
            "td": tdt, "tf": tft, "mask": mask, "eye": eye,
        })
    return in_maps


def kernel(L0_attn_x, L0_attn_W_rvk, L0_attn_W_out, L0_attn_time_decay,
           L0_attn_time_first, L0_ffn_w1, L0_ffn_w2, L0_ffn_w3):
    in_maps = _prep_inputs(L0_attn_x, L0_attn_W_rvk, L0_attn_W_out,
                           L0_attn_time_decay, L0_attn_time_first,
                           L0_ffn_w1, L0_ffn_w2, L0_ffn_w3)
    nc = build_nc()
    res = run_bass_kernel_spmd(nc, in_maps, core_ids=list(range(8)))
    B, T = 4, 4096
    out = np.empty((B, T, C), np.float32)
    for core in range(8):
        b, h = core // 2, core % 2
        out[b, h * T_OWN:(h + 1) * T_OWN, :] = res.results[core]["out"]
    return out


# version 06
# speedup vs baseline: 1.6460x; 1.1387x over previous
import numpy as np
import ml_dtypes
import contextlib
import concourse.bass as bass
import concourse.mybir as mybir
from concourse.bass_utils import run_bass_kernel_spmd

F32 = mybir.dt.float32
BF16 = mybir.dt.bfloat16
ALU = mybir.AluOpType
ACT = mybir.ActivationFunctionType
AX = mybir.AxisListType

C = 2048
CCN = 16
H = 5632
HHN = 44
WIN = 128
T_OWN = 2048
T_KV = T_OWN + WIN
NBLK = T_KV // 128
EPS_LN = 1e-5
EPS_Q = 1e-8
NW = 5632.0 * 2048.0

SC = [(0, 1024), (1024, 1152)]
SCW = 1152
KV_TT = [(0, 512), (512, 512), (1024, 512), (1536, 512), (2048, 128)]
R_TT = [(WIN, 512), (WIN + 512, 512), (WIN + 1024, 512), (WIN + 1536, 512)]
TT5 = 512
NTT5 = T_OWN // TT5
NTB5 = TT5 // 128
DT5 = 512
NDT5 = C // DT5


def build_nc():
    import os as _os
    _norace = _os.environ.get("ATHENA_SIM_NORACE", "") == "1"
    nc = bass.Bass("TRN2", target_bir_lowering=False,
                   detect_race_conditions=not _norace)

    x_d = nc.dram_tensor("x", [T_KV, C], F32, kind="ExternalInput").ap()
    wrvk_d = nc.dram_tensor("wrvk", [CCN, 128, 3 * CCN * 128], F32, kind="ExternalInput").ap()
    w12_d = nc.dram_tensor("w12", [HHN, 2, 128, C], F32, kind="ExternalInput").ap()
    w3t_d = nc.dram_tensor("w3t", [NDT5, HHN, 128, DT5], F32, kind="ExternalInput").ap()
    wout_d = nc.dram_tensor("wout", [NDT5, 128, CCN * DT5], F32, kind="ExternalInput").ap()
    td_d = nc.dram_tensor("td", [128, CCN], F32, kind="ExternalInput").ap()
    tf_d = nc.dram_tensor("tf", [128, CCN], F32, kind="ExternalInput").ap()
    mask_d = nc.dram_tensor("mask", [128, 1], F32, kind="ExternalInput").ap()
    eye_d = nc.dram_tensor("eye", [128, 128], BF16, kind="ExternalInput").ap()
    out_d = nc.dram_tensor("out", [T_OWN, C], F32, kind="ExternalOutput").ap()
    ysp_d = nc.dram_tensor("ysp", [128, CCN * T_OWN], BF16, kind="Internal").ap()
    hidsp_d = nc.dram_tensor("hidsp", [128, HHN * T_OWN], BF16, kind="Internal").ap()
    q3sp_d = nc.dram_tensor("q3sp", [128, NDT5 * HHN * DT5], BF16, kind="Internal").ap()
    woutsp_d = nc.dram_tensor("woutsp", [128, NDT5 * CCN * DT5], BF16, kind="Internal").ap()

    es = contextlib.ExitStack()

    SB_BASE = (nc._sbuf_addr_for_side("left") + 63) // 64 * 64
    SB_TOP = nc._sbuf_addr_for_side("right")
    off = [SB_BASE]

    def sb(name, shape, dtype, at=None):
        nbytes = int(np.prod(shape[1:])) * (2 if dtype == BF16 else 4)
        if at is None:
            at = off[0]
            off[0] = at + ((nbytes + 63) // 64) * 64
        return nc.alloc_sbuf_tensor_at(name, shape, dtype, offset=at), at

    def sba(name, shape, dtype, at=None):
        return sb(name, shape, dtype, at)[0]

    consts = sba("consts", [128, 16], F32)
    dch = sba("dch", [128, CCN], F32)
    euch = sba("euch", [128, CCN], F32)
    tdt = sba("tdt", [128, CCN], F32)
    etd = sba("etd", [128, CCN], F32)
    tft = sba("tft", [128, CCN], F32)
    eye_t = sba("eye", [128, 128], BF16)
    sum_s = sba("sum_s", [128, NBLK], F32)
    sq_s = sba("sq_s", [128, NBLK], F32)
    mean_s = sba("mean_s", [128, NBLK], F32)
    m2_s = sba("m2_s", [128, NBLK], F32)
    var_s = sba("var_s", [128, NBLK], F32)
    std_s = sba("std_s", [128, NBLK], F32)
    rstd_s = sba("rstd_s", [128, NBLK], F32)
    gacc = sba("gacc", [128, 144], F32)
    gred = sba("gred", [128, 8], F32)
    g1 = sba("g1", [1, 8], F32)
    canary = sba("canary", [128, 8], F32)
    ones_t = sba("ones_t", [128, 128], F32)
    sgn3, sgn3_at = sb("sgn3", [128, DT5], BF16)
    q3st = sba("q3st", [128, 2 * DT5], BF16)
    w3qb = sba("w3qb", [128, 2 * DT5], F32)
    tanh4 = sba("tanh4", [128, 2 * 512], F32)
    hidum, hid_at = sb("hidum", [128, C], F32)
    hidb = sba("hidb", [128, 2 * C], BF16, at=hid_at)
    gdum = hidum
    xsh = sba("xsh", [128, 2 * C], F32)
    big0, big0_at = sb("big0", [128, CCN * T_KV], BF16)
    union_at = off[0]
    wkb = sba("wkb", [128, 2 * 3 * CCN * 128], BF16)
    ekt, ekt_at = sb("ekt", [128, 4 * SCW], F32)
    vt = sba("vt", [128, 2 * SCW], F32)
    pbuf = sba("pbuf", [128, SCW + 1], F32)
    qbuf = sba("qbuf", [128, SCW + 1], F32)
    dte = sba("dte", [128, SCW], F32)
    ybf = sba("ybf", [128, 2 * SCW], BF16)
    trt = sba("trt", [128, 2 * T_OWN], F32)
    sgn = sba("sgn", [128, C], BF16)
    u_t = sba("u_t", [128, C], BF16)
    thr = sba("thr", [128, 8], F32)
    gapt = sba("gapt", [128, 128], F32)
    w12x = sba("w12x", [128, C], F32)
    end_at = off[0]
    qb = sba("qb", [128, 4 * C], BF16, at=ekt_at)
    xntok = sba("xntok", [128, 2 * C], BF16, at=ekt_at)
    QDTW = (HHN + CCN) * DT5
    qdt2 = sba("qdt2", [128, 2 * QDTW], BF16, at=big0_at + HHN * TT5 * 2)
    assert big0_at + HHN * TT5 * 2 + 2 * QDTW * 2 <= SB_TOP, "qdt2 overflow"
    y5 = sba("y5", [128, CCN * TT5], BF16, at=sgn3_at)
    xt5 = sba("xt5", [128, 4 * 512], F32, at=sgn3_at + CCN * TT5 * 2)
    ob5 = sba("ob5", [128, 4 * 512], F32, at=sgn3_at + CCN * TT5 * 2 + 4 * 512 * 4)
    assert end_at <= SB_TOP, f"sbuf overflow: {end_at} > {SB_TOP}"

    NMB = 6
    mmb = [es.enter_context(nc.psum_tensor(f"mmb{i}", [128, 512], F32)) for i in range(NMB)]
    ps_tr = [es.enter_context(nc.psum_tensor("ps_tr0", [128, 1024], BF16)),
             es.enter_context(nc.psum_tensor("ps_tr1", [128, 1024], BF16))]

    sem_names = ["sx", "sxA", "sxB", "swgA", "swgB", "sw5", "sws", "sscA", "sscB",
                 "smm", "str", "sevv", "sevs", "sv", "ss",
                 "sgy0", "sgy1", "sgh0", "sgh1", "sgq0", "sgq1",
                 "spx0", "spx1", "spx2", "spx3",
                 "sst0", "sst1", "sst2", "sst3", "sp5", "sth", "sws0", "sws1", "sws2", "sp5b", "sw5b", "swc"]
    sems = {n: es.enter_context(nc.semaphore(n)) for n in sem_names}
    cnt = {n: 0 for n in sem_names}
    OPS = {e: [] for e in ["sync", "scalar", "vector", "tensor", "gpsimd"]}

    def emit(eng, fn):
        OPS[eng].append(fn)

    def op_inc(eng, name, by, fn):
        cnt[name] += by
        v = cnt[name]
        sem = sems[name]
        emit(eng, lambda e, fn=fn, sem=sem, by=by: fn(e).then_inc(sem, by))
        return v

    def wait(eng, name, v):
        if v <= 0:
            return
        sem = sems[name]
        emit(eng, lambda e, sem=sem, v=v: e.wait_ge(sem, v))

    def vgap():
        emit("vector", lambda v: v.memset(gapt[:, 0:64], 0.0))

    def sgap():
        emit("scalar", lambda s: s.memzero(gapt[:, 64:128]))

    bank_free = [("sevv", 0)] * NMB
    bank_i = [0]

    def next_bank():
        b = bank_i[0] % NMB
        bank_i[0] += 1
        return b

    ST = {}

    def p0():
        op_inc("sync", "sx", 16, lambda e: e.dma_start(out=tdt[:], in_=td_d[:]))
        op_inc("sync", "sx", 16, lambda e: e.dma_start(out=tft[:], in_=tf_d[:]))
        op_inc("sync", "sx", 16, lambda e: e.dma_start(out=consts[:, 4:5], in_=mask_d[:]))
        op_inc("sync", "sx", 16, lambda e: e.dma_start(out=eye_t[:], in_=eye_d[:]))
        for dt in range(NDT5):
            op_inc("gpsimd", "swc", 16,
                   lambda e, dt=dt: e.dma_start(
                       out=woutsp_d[:, dt * CCN * DT5:(dt + 1) * CCN * DT5],
                       in_=wout_d[dt, :, :]))
        emit("vector", lambda v: v.memset(consts[:, 0:1], EPS_LN))
        emit("vector", lambda v: v.memset(consts[:, 1:2], 0.5))
        emit("vector", lambda v: v.memset(consts[:, 2:3], -1.0))
        emit("vector", lambda v: v.memset(consts[:, 3:4], 0.0))
        emit("vector", lambda v: v.memset(consts[:, 5:6], 0.25))
        emit("vector", lambda v: v.memset(pbuf[:, 0:1], 0.0))
        emit("vector", lambda v: v.memset(qbuf[:, 0:1], 0.0))
        emit("vector", lambda v: v.memset(gacc[:], 0.0))
        vv = op_inc("vector", "sv", 1, lambda v: v.memset(ones_t[:], 1.0))
        wait("scalar", "sx", 64)
        wait("scalar", "sv", vv)
        emit("scalar", lambda s: s.activation(out=etd[:], in_=tdt[:], func=ACT.Exp))
        op_inc("scalar", "ss", 1, lambda s: s.activation(out=euch[:], in_=tft[:], func=ACT.Exp))
        sgap()
        op_inc("scalar", "ss", 1, lambda s: s.activation(out=dch[:], in_=etd[:], func=ACT.Exp,
                                                         scale=consts[:, 2:3]))

    xn_done = {}
    sq_done = {}
    tr_last = {}
    tr_evac = {}
    trc = [0]

    def emit_xn(b):
        par = b % 2
        if b >= 2:
            wait("vector", "str", tr_last[b - 2])
        xv = op_inc("vector", "sevv", 1,
                    lambda v, b=b, par=par: v.tensor_scalar(
                        out=xntok[:, par * C:(par + 1) * C], in0=xsh[:, par * C:(par + 1) * C],
                        scalar1=mean_s[:, b:b + 1], scalar2=rstd_s[:, b:b + 1],
                        op0=ALU.subtract, op1=ALU.mult))
        xn_done[b] = xv
        wait("tensor", "sevv", xv)
        for g4 in range(CCN // 4):
            g = trc[0]
            bank = g % 2
            if g >= 2:
                wait("tensor", "sevv", tr_evac[g - 2])
            tv = 0
            for i in range(4):
                cc = g4 * 4 + i
                inst = [None]
                tv = op_inc("tensor", "str", 1,
                            lambda t, par=par, cc=cc, i=i, bank=bank: t.transpose(
                                ps_tr[bank][:, i * 256:(i + 1) * 256][:, 0:128]
                                if False else
                                ps_tr[bank][:, i * 128 + 0: i * 128 + 128],
                                xntok[:, par * C + cc * 128: par * C + (cc + 1) * 128],
                                eye_t[:]))
            wait("vector", "str", tv)
            ev = 0
            for i in range(4):
                cc = g4 * 4 + i
                ev = op_inc("vector", "sevv", 1,
                            lambda v, b=b, cc=cc, i=i, bank=bank: v.tensor_copy(
                                out=big0[:, cc * T_KV + b * 128: cc * T_KV + (b + 1) * 128],
                                in_=ps_tr[bank][:, i * 128: i * 128 + 128]))
            tr_evac[g] = ev
            trc[0] += 1
        tr_last[b] = cnt["str"]

    def p1():
        for b in range(NBLK):
            par = b % 2
            if b >= 2:
                wait("sync", "sevv", xn_done[b - 2])
                wait("sync", "sevs", sq_done[b - 2])
            sxp = "sxA" if par == 0 else "sxB"
            lv = op_inc("sync", sxp, 16,
                        lambda e, b=b, par=par: e.dma_start(
                            out=xsh[:, par * C:(par + 1) * C], in_=x_d[b * 128:(b + 1) * 128, :]))
            wait("vector", sxp, lv)
            rv = op_inc("vector", "sevv", 1,
                        lambda v, b=b, par=par: v.reduce_sum(
                            out=sum_s[:, b:b + 1], in_=xsh[:, par * C:(par + 1) * C], axis=AX.X))
            wait("scalar", sxp, lv)
            sqv = op_inc("scalar", "sevs", 1,
                         lambda s, b=b, par=par: s.activation(
                             out=gdum[:], in_=xsh[:, par * C:(par + 1) * C],
                             func=ACT.Square, accum_out=sq_s[:, b:b + 1]))
            sq_done[b] = sqv
            wait("scalar", "sevv", rv)
            emit("scalar", lambda s, b=b: s.mul(out=mean_s[:, b:b + 1], in_=sum_s[:, b:b + 1],
                                                mul=1.0 / C))
            sgap()
            mv = op_inc("scalar", "sevs", 1,
                        lambda s, b=b: s.activation(out=m2_s[:, b:b + 1], in_=mean_s[:, b:b + 1],
                                                    func=ACT.Square))
            wait("vector", "sevs", mv)
            vv = op_inc("vector", "sevv", 1,
                        lambda v, b=b: v.scalar_tensor_tensor(
                            out=var_s[:, b:b + 1], in0=sq_s[:, b:b + 1], scalar=1.0 / C,
                            in1=m2_s[:, b:b + 1], op0=ALU.mult, op1=ALU.subtract))
            wait("scalar", "sevv", vv)
            sv_ = op_inc("scalar", "sevs", 1,
                         lambda s, b=b: s.activation(out=std_s[:, b:b + 1], in_=var_s[:, b:b + 1],
                                                     func=ACT.Sqrt, bias=consts[:, 0:1]))
            wait("vector", "sevs", sv_)
            emit("vector", lambda v, b=b: v.reciprocal(out=rstd_s[:, b:b + 1],
                                                       in_=std_s[:, b:b + 1]))
            if b > 0:
                emit_xn(b - 1)
        emit("vector", lambda v: v.memset(consts[:, 9:10], 0.0))
        emit_xn(NBLK - 1)

    gunits = []

    def make_gamma_units():
        w3flat = w3t_d.rearrange("a b p f -> (a b) p f")
        for blk in range(NDT5 * HHN // 4):
            srcs = [w3flat[4 * blk + i, :, :] for i in range(4)]
            gunits.append((srcs, DT5, 96 + blk))
        for hh in range(HHN):
            for m in range(2):
                gunits.append(([w12_d[hh, m, :, :]], C, m * 48 + hh))

    gstate = {"load": 0, "use": 0, "loadv": {}}

    def pump_gamma(n):
        for _ in range(n):
            for _ld in range(2):
                if gstate["load"] < len(gunits) and gstate["load"] - gstate["use"] < 2:
                    i = gstate["load"]
                    srcs, ncols, col = gunits[i]
                    par = i % 2
                    if i == 0:
                        wait("scalar", "sevv", xn_done[NBLK - 1])
                    lv = 0
                    for si, s_ap in enumerate(srcs):
                        lv = op_inc("scalar", "sscA" if par == 0 else "sscB", 16,
                                    lambda e, s_ap=s_ap, ncols=ncols, par=par, si=si:
                                    e.dma_start(out=xsh[:, par * C + si * ncols:
                                                        par * C + (si + 1) * ncols], in_=s_ap))
                    gstate["loadv"][i] = lv
                    gstate["load"] += 1
            if gstate["use"] < gstate["load"]:
                i = gstate["use"]
                srcs, ncols, col = gunits[i]
                tot = ncols * len(srcs)
                par = i % 2
                wait("scalar", "sscA" if par == 0 else "sscB", gstate["loadv"][i])
                op_inc("scalar", "ss", 1,
                       lambda s, tot=tot, col=col, par=par: s.activation(
                           out=gdum[:, 0:tot], in_=xsh[:, par * C: par * C + tot],
                           func=ACT.Abs, accum_out=gacc[:, col:col + 1]))
                gstate["use"] += 1

    def gamma_finish():
        pump_gamma(len(gunits))
        ST["gamma_ss"] = cnt["ss"]
        wait("vector", "ss", cnt["ss"])
        rv = 0
        for m in range(3):
            rv = op_inc("vector", "sevv", 1,
                        lambda v, m=m: v.reduce_sum(out=gred[:, m:m + 1],
                                                    in_=gacc[:, m * 48:(m + 1) * 48], axis=AX.X))
        wait("tensor", "sevv", rv)
        b1 = next_bank()
        wait("tensor", *bank_free[b1])
        mv = op_inc("tensor", "smm", 1,
                    lambda t, b1=b1: t.matmul(mmb[b1][0:1, 0:3], ones_t[:, 0:1], gred[:, 0:3],
                                              start=True, stop=True))
        wait("vector", "smm", mv)
        cv = op_inc("vector", "sevv", 1, lambda v, b1=b1: v.tensor_copy(
            out=g1[0:1, 0:3], in_=mmb[b1][0:1, 0:3]))
        bank_free[b1] = ("sevv", cv)
        wait("tensor", "sevv", cv)
        b2 = next_bank()
        wait("tensor", *bank_free[b2])
        mv2 = op_inc("tensor", "smm", 1,
                     lambda t, b2=b2: t.matmul(mmb[b2][:, 4:7], ones_t[0:1, :], g1[0:1, 0:3],
                                               start=True, stop=True))
        wait("vector", "smm", mv2)
        tv_ = op_inc("vector", "sth", 1,
                     lambda v, b2=b2: v.tensor_scalar(out=thr[:, 0:3], in0=mmb[b2][:, 4:7],
                                                      scalar1=0.5 / NW, scalar2=0.5 * EPS_Q,
                                                      op0=ALU.mult, op1=ALU.add))
        bank_free[b2] = ("sth", tv_)
        vgap()
        vgap()
        ST["thr_done"] = op_inc("vector", "sth", 1,
                                lambda v: v.tensor_scalar(out=thr[:, 3:6], in0=thr[:, 0:3],
                                                          scalar1=-1.0, scalar2=None,
                                                          op0=ALU.mult))


    last_q = {"v": 0}

    def quantize_sgn(src_ap, dst_ap, m, sg_ap):
        wait("scalar", "sevv", last_q["v"])
        op_inc("scalar", "sevs", 1,
               lambda s, src_ap=src_ap, sg_ap=sg_ap, m=m: s.activation(
                   out=sg_ap, in_=src_ap, func=ACT.Sign, bias=thr[:, 3 + m:4 + m]))
        a2 = op_inc("scalar", "sevs", 1,
                    lambda s, src_ap=src_ap, dst_ap=dst_ap, m=m: s.activation(
                        out=dst_ap, in_=src_ap, func=ACT.Sign, bias=thr[:, m:m + 1]))
        wait("vector", "sevs", a2)
        qv = op_inc("vector", "sevv", 1,
                    lambda v, dst_ap=dst_ap, sg_ap=sg_ap: v.tensor_add(
                        out=dst_ap, in0=dst_ap, in1=sg_ap))
        last_q["v"] = qv
        return qv

    def quantize_vec(src_ap, dst_ap, m):
        op_inc("vector", "sevv", 1,
               lambda v, src_ap=src_ap, m=m: v.tensor_scalar(
                   out=u_t[:, 0:src_ap.shape[1]], in0=src_ap, scalar1=thr[:, 3 + m:4 + m],
                   scalar2=None, op0=ALU.is_le))
        vgap()
        return op_inc("vector", "sevv", 1,
                      lambda v, src_ap=src_ap, dst_ap=dst_ap, m=m: v.scalar_tensor_tensor(
                          out=dst_ap, in0=src_ap, scalar=thr[:, m:m + 1],
                          in1=u_t[:, 0:src_ap.shape[1]], op0=ALU.is_ge, op1=ALU.subtract))

    q3units = []

    def make_q3_units():
        for dt in range(NDT5):
            for hh in range(HHN):
                q3units.append((dt, hh))

    qstate = {"load": 0, "use": 0, "loadv": {}, "spw": {}}

    def pump_q3(n):
        for _ in range(n):
            for _ld in range(2):
                if qstate["load"] < len(q3units) and qstate["load"] - qstate["use"] < 2:
                    i = qstate["load"]
                    dt, hh = q3units[i]
                    par = i % 2
                    if i == 0:
                        wait("scalar", "sth", ST["thr_done"])
                    if i >= 2:
                        wait("scalar", "sgq0" if i % 2 == 0 else "sgq1", qstate["spw"][i - 2])
                    qstate["loadv"][i] = op_inc("scalar", "sscA" if par == 0 else "sscB", 16,
                                                lambda e, dt=dt, hh=hh, par=par: e.dma_start(
                                                    out=w3qb[:, par * DT5:(par + 1) * DT5],
                                                    in_=w3t_d[dt, hh, :, :]))
                    qstate["load"] += 1
            if qstate["use"] < qstate["load"]:
                i = qstate["use"]
                dt, hh = q3units[i]
                par = i % 2
                wait("scalar", "sscA" if par == 0 else "sscB", qstate["loadv"][i])
                qv = quantize_sgn(w3qb[:, par * DT5:(par + 1) * DT5],
                                  q3st[:, par * DT5:(par + 1) * DT5], 2, sgn3[:])
                wait("gpsimd", "sevv", qv)
                qstate["spw"][i] = op_inc("gpsimd", "sgq0" if i % 2 == 0 else "sgq1", 16,
                                          lambda e, dt=dt, hh=hh, par=par: e.dma_start(
                                              out=q3sp_d[:, (dt * HHN + hh) * DT5:
                                                         (dt * HHN + hh + 1) * DT5],
                                              in_=q3st[:, par * DT5:(par + 1) * DT5]))
                qstate["use"] += 1

    cons = {}
    ysp_done = {}

    def p3():
        import os as _os
        NCC = int(_os.environ.get("ATHENA_DBG_CC", str(CCN)))
        SKIP_SCAN = _os.environ.get("ATHENA_DBG_NOSCAN", "") == "1"
        NWKB = 3 * CCN * 128
        wk_used = {}
        for cc in range(NCC):
            par = cc % 2
            if cc >= 2:
                wait("gpsimd", "smm", wk_used[cc - 2])
            swp = "swgA" if par == 0 else "swgB"
            lw = op_inc("gpsimd", swp, 16,
                        lambda e, cc=cc, par=par: e.dma_start(
                            out=wkb[:, par * NWKB:(par + 1) * NWKB], in_=wrvk_d[cc, :, :]))
            wait("tensor", swp, lw)

            for j, tts in ((0, KV_TT), (1, KV_TT), (2, R_TT)):
                for (t0, tl) in tts:
                    b = next_bank()
                    wait("tensor", *bank_free[b])
                    for c in range(CCN - 1):
                        lhs = wkb[:, par * NWKB + (j * CCN + c) * 128:
                                  par * NWKB + (j * CCN + c) * 128 + 128]
                        rhs = big0[:, c * T_KV + t0: c * T_KV + t0 + tl]
                        emit("tensor", lambda t, lhs=lhs, rhs=rhs, b=b, c=c, tl=tl:
                             t.matmul(mmb[b][:, 0:tl], lhs, rhs,
                                      start=(c == 0), stop=False))
                    c = CCN - 1
                    lhs = wkb[:, par * NWKB + (j * CCN + c) * 128:
                              par * NWKB + (j * CCN + c) * 128 + 128]
                    rhs = big0[:, c * T_KV + t0: c * T_KV + t0 + tl]
                    gv = op_inc("tensor", "smm", 1,
                                lambda t, lhs=lhs, rhs=rhs, b=b, tl=tl:
                                t.matmul(mmb[b][:, 0:tl], lhs, rhs, start=False, stop=True))
                    wk_used[cc] = gv
                    pump_gamma(1)
                    ch = 0 if t0 < SC[1][0] else 1
                    c0 = SC[ch][0]
                    if j == 0:
                        ep = (cc % 2) * 2 * SCW
                        dst = ekt[:, ep + ch * SCW + (t0 - c0): ep + ch * SCW + (t0 - c0) + tl]
                        wait("scalar", "smm", gv)
                        if cc >= 2:
                            wait("scalar", "sevv", cons[cc - 2])
                        ev = op_inc("scalar", "sevs", 1,
                                    lambda s, dst=dst, b=b, tl=tl: s.activation(
                                        out=dst, in_=mmb[b][:, 0:tl], func=ACT.Exp))
                        bank_free[b] = ("sevs", ev)
                    elif j == 1:
                        dst = vt[:, ch * SCW + (t0 - c0): ch * SCW + (t0 - c0) + tl]
                        wait("vector", "smm", gv)
                        ev = op_inc("vector", "sevv", 1,
                                    lambda v, dst=dst, b=b, tl=tl: v.tensor_copy(
                                        out=dst, in_=mmb[b][:, 0:tl]))
                        bank_free[b] = ("sevv", ev)
                    else:
                        o0 = (cc % 2) * T_OWN + t0 - WIN
                        wait("scalar", "smm", gv)
                        if cc >= 2:
                            wait("scalar", "sevv", cons[cc - 2])
                        ev = op_inc("scalar", "sevs", 1,
                                    lambda s, o0=o0, b=b, tl=tl: s.activation(
                                        out=trt[:, o0:o0 + tl], in_=mmb[b][:, 0:tl],
                                        func=ACT.Tanh, scale=consts[:, 1:2]))
                        bank_free[b] = ("sevs", ev)

            wait("vector", "sevs", cnt["sevs"])
            if SKIP_SCAN:
                cons[cc] = op_inc("vector", "sevv", 1, lambda v: v.memset(consts[:, 8:9], 0.0))
                for _ch in range(2):
                    k_ = cc * 2 + _ch
                    ysp_done[k_] = cnt["sgy0"] if k_ % 2 == 0 else cnt["sgy1"]
                continue
            ep = (cc % 2) * 2 * SCW
            tp = (cc % 2) * T_OWN
            emit("vector", lambda v, ep=ep: v.tensor_scalar(
                out=ekt[:, ep: ep + WIN], in0=ekt[:, ep: ep + WIN],
                scalar1=consts[:, 4:5], scalar2=None, op0=ALU.mult))
            vgap()
            for ch, (c0, cl) in enumerate(SC):
                ek_c = ekt[:, ep + ch * SCW: ep + ch * SCW + cl]
                v_c = vt[:, ch * SCW: ch * SCW + cl]
                pb = pbuf[:]
                qbv = qbuf[:]
                d_bc = dch[:, cc:cc + 1].broadcast_to([128, cl])
                oo, ol = (WIN, 1024 - WIN) if ch == 0 else (0, cl)
                own_off = tp + (0 if ch == 0 else 1024 - WIN)
                init_q = consts[:, 3:4] if ch == 0 else qbv[:, 0:1]
                init_p = consts[:, 3:4] if ch == 0 else pb[:, 0:1]
                emit("vector", lambda v, qbv=qbv, d_bc=d_bc, ek_c=ek_c, cl=cl, init_q=init_q:
                     v.tensor_tensor_scan(out=qbv[:, 1:1 + cl], data0=d_bc, data1=ek_c,
                                          initial=init_q, op0=ALU.mult, op1=ALU.add))
                emit("vector", lambda v, own_off=own_off, ol=ol: v.tensor_scalar(
                    out=trt[:, own_off:own_off + ol], in0=trt[:, own_off:own_off + ol],
                    scalar1=0.5, scalar2=0.5, op0=ALU.mult, op1=ALU.add))
                emit("vector", lambda v, ek_c=ek_c, qbv=qbv, cc=cc, oo=oo, ol=ol:
                     v.scalar_tensor_tensor(out=dte[:, 0:ol], in0=ek_c[:, oo:oo + ol],
                                            scalar=euch[:, cc:cc + 1],
                                            in1=qbv[:, oo:oo + ol] if ol != 1024 - WIN
                                            else qbv[:, 1 + oo - 1: 1 + oo - 1 + ol],
                                            op0=ALU.mult, op1=ALU.add))
                if ch + 1 < len(SC):
                    emit("vector", lambda v, qbv=qbv, cl=cl: v.tensor_copy(
                        out=qbuf[:, 0:1], in_=qbv[:, cl:cl + 1]))
                emit("vector", lambda v, ek_c=ek_c, v_c=v_c: v.tensor_mul(
                    out=ek_c, in0=ek_c, in1=v_c))
                vgap()
                emit("vector", lambda v, pb=pb, d_bc=d_bc, ek_c=ek_c, cl=cl, init_p=init_p:
                     v.tensor_tensor_scan(out=pb[:, 1:1 + cl], data0=d_bc, data1=ek_c,
                                          initial=init_p, op0=ALU.mult, op1=ALU.add))
                emit("vector", lambda v, ol=ol: v.reciprocal(
                    out=dte[:, 0:ol], in_=dte[:, 0:ol]))
                emit("vector", lambda v, v_c=v_c, ek_c=ek_c, pb=pb, cc=cc, oo=oo, ol=ol:
                     v.scalar_tensor_tensor(out=v_c[:, 0:ol], in0=ek_c[:, oo:oo + ol],
                                            scalar=euch[:, cc:cc + 1],
                                            in1=pb[:, oo:oo + ol] if ol != 1024 - WIN
                                            else pb[:, 1 + oo - 1: 1 + oo - 1 + ol],
                                            op0=ALU.mult, op1=ALU.add))
                if ch + 1 < len(SC):
                    emit("vector", lambda v, pb=pb, cl=cl: v.tensor_copy(
                        out=pbuf[:, 0:1], in_=pb[:, cl:cl + 1]))
                vgap()
                emit("vector", lambda v, qbv=qbv, v_c=v_c, ol=ol: v.tensor_mul(
                    out=qbv[:, 1:1 + ol], in0=v_c[:, 0:ol], in1=dte[:, 0:ol]))
                k = cc * 2 + ch
                ypar = k % 2
                if k >= 2:
                    wait("vector", "sgy0" if ypar == 0 else "sgy1", ysp_done[k - 2])
                vgap()
                yv = op_inc("vector", "sevv", 1,
                            lambda v, own_off=own_off, ol=ol, qbv=qbv, ypar=ypar:
                            v.tensor_mul(out=ybf[:, ypar * SCW: ypar * SCW + ol],
                                         in0=trt[:, own_off:own_off + ol],
                                         in1=qbv[:, 1:1 + ol]))
                wait("sync", "sevv", yv)
                o_lo = own_off - tp
                sv_ = 0
                for tt_ in range(o_lo // 512, (o_lo + ol + 511) // 512):
                    lo = max(o_lo, tt_ * 512)
                    hi = min(o_lo + ol, (tt_ + 1) * 512)
                    sv_ = op_inc("sync", "sgy0" if ypar == 0 else "sgy1", 16,
                                 lambda e, cc=cc, tt_=tt_, lo=lo, hi=hi, o_lo=o_lo, ypar=ypar:
                                 e.dma_start(
                                     out=ysp_d[:, tt_ * CCN * 512 + cc * 512 + lo - tt_ * 512:
                                               tt_ * CCN * 512 + cc * 512 + hi - tt_ * 512],
                                     in_=ybf[:, ypar * SCW + lo - o_lo: ypar * SCW + hi - o_lo]))
                ysp_done[k] = sv_
            cons[cc] = op_inc("vector", "sevv", 1, lambda v: v.memset(consts[:, 8:9], 0.0))


    hid_ev = {}
    hid_spill = {}
    mm_after_hh = {}

    def p4():
        wait("sync", "ss", ST["gamma_ss"])
        q_ready = {}
        qdone = {}
        wslot = {}
        slots = [xsh[:, 0:C], xsh[:, C:2 * C], w12x[:]]

        def load_w12(hh):
            for m in (0, 1):
                i = 2 * hh + m
                s_ = i % 3
                if i >= 3:
                    wait("sync", "sevv", qdone[i - 3])
                wslot[i] = op_inc("sync", f"sws{s_}", 16,
                                  lambda e, hh=hh, m=m, s_=s_: e.dma_start(
                                      out=slots[s_], in_=w12_d[hh, m, :, :]))

        def quant_w12(hh):
            par = hh % 2
            if hh >= 2:
                wait("scalar", "smm", mm_after_hh[hh - 2] if hh - 2 in mm_after_hh
                     else 0)
                wait("vector", "smm", mm_after_hh[hh - 2] if hh - 2 in mm_after_hh
                     else 0)
            s1_, s2_ = (2 * hh) % 3, (2 * hh + 1) % 3
            wait("scalar", f"sws{s1_}", wslot[2 * hh])
            if hh == 0:
                wait("scalar", "sth", ST["thr_done"])
            qdone[2 * hh] = quantize_sgn(slots[s1_], qb[:, (par * 2) * C:(par * 2 + 1) * C],
                                         0, sgn[:])
            wait("vector", f"sws{s2_}", wslot[2 * hh + 1])
            qv = quantize_vec(slots[s2_], qb[:, (par * 2 + 1) * C:(par * 2 + 2) * C], 1)
            q_ready[hh] = qv
            qdone[2 * hh + 1] = qv

        load_w12(0)
        quant_w12(0)
        for hh in range(HHN):
            par = hh % 2
            if hh + 1 < HHN:
                load_w12(hh + 1)
                quant_w12(hh + 1)
            wait("tensor", "sevv", q_ready[hh])
            for ti, (t0, tl) in enumerate(R_TT):
                b1 = next_bank()
                wait("tensor", *bank_free[b1])
                for c in range(CCN - 1):
                    lhs = qb[:, (par * 2) * C + c * 128:(par * 2) * C + (c + 1) * 128]
                    rhs = big0[:, c * T_KV + t0: c * T_KV + t0 + tl]
                    emit("tensor", lambda t, lhs=lhs, rhs=rhs, b1=b1, c=c, tl=tl:
                         t.matmul(mmb[b1][:, 0:tl], lhs, rhs, start=(c == 0), stop=False))
                c = CCN - 1
                lhs = qb[:, (par * 2) * C + c * 128:(par * 2) * C + (c + 1) * 128]
                rhs = big0[:, c * T_KV + t0: c * T_KV + t0 + tl]
                g1v = op_inc("tensor", "smm", 1,
                             lambda t, lhs=lhs, rhs=rhs, b1=b1, tl=tl:
                             t.matmul(mmb[b1][:, 0:tl], lhs, rhs, start=False, stop=True))
                b2 = next_bank()
                wait("tensor", *bank_free[b2])
                for c in range(CCN - 1):
                    lhs = qb[:, (par * 2 + 1) * C + c * 128:(par * 2 + 1) * C + (c + 1) * 128]
                    rhs = big0[:, c * T_KV + t0: c * T_KV + t0 + tl]
                    emit("tensor", lambda t, lhs=lhs, rhs=rhs, b2=b2, c=c, tl=tl:
                         t.matmul(mmb[b2][:, 0:tl], lhs, rhs, start=(c == 0), stop=False))
                c = CCN - 1
                lhs = qb[:, (par * 2 + 1) * C + c * 128:(par * 2 + 1) * C + (c + 1) * 128]
                rhs = big0[:, c * T_KV + t0: c * T_KV + t0 + tl]
                g2v = op_inc("tensor", "smm", 1,
                             lambda t, lhs=lhs, rhs=rhs, b2=b2, tl=tl:
                             t.matmul(mmb[b2][:, 0:tl], lhs, rhs, start=False, stop=True))
                mm_after_hh[hh] = g2v
                k = hh * 4 + ti
                tpar = ti % 2
                wait("scalar", "smm", g1v)
                if k >= 2:
                    wait("scalar", "sevv", hid_ev[k - 2])
                e1 = op_inc("scalar", "sevs", 1,
                            lambda s, b1=b1, tpar=tpar, tl=tl: s.activation(
                                out=tanh4[:, tpar * 512: tpar * 512 + tl],
                                in_=mmb[b1][:, 0:tl], func=ACT.Tanh, scale=consts[:, 5:6]))
                bank_free[b1] = ("sevs", e1)
                o0 = t0 - WIN
                hpar = ti % 2
                wait("vector", "sevs", e1)
                wait("vector", "smm", g2v)
                if k >= 2:
                    wait("vector", "sgh0" if k % 2 == 0 else "sgh1", hid_spill[k - 2])
                emit("vector", lambda v, tpar=tpar, hpar=hpar, tl=tl: v.tensor_scalar(
                    out=hidb[:, hpar * C: hpar * C + tl],
                    in0=tanh4[:, tpar * 512: tpar * 512 + tl],
                    scalar1=0.25, scalar2=0.25, op0=ALU.mult, op1=ALU.add))
                vgap()
                e3 = op_inc("vector", "sevv", 1,
                            lambda v, b2=b2, hpar=hpar, tl=tl: v.tensor_mul(
                                out=hidb[:, hpar * C: hpar * C + tl],
                                in0=hidb[:, hpar * C: hpar * C + tl], in1=mmb[b2][:, 0:tl]))
                hid_ev[k] = e3
                bank_free[b2] = ("sevv", e3)
                wait("gpsimd", "sevv", e3)
                hid_spill[k] = op_inc("gpsimd", "sgh0" if k % 2 == 0 else "sgh1", 16,
                                      lambda e, hh=hh, ti=ti, tl=tl, hpar=hpar: e.dma_start(
                                          out=hidsp_d[:, ti * HHN * 512 + hh * 512:
                                                      ti * HHN * 512 + hh * 512 + tl],
                                          in_=hidb[:, hpar * C: hpar * C + tl]))
                pump_q3(1)
        pump_q3(len(q3units))
        ST["hid_sg"] = (cnt["sgh0"], cnt["sgh1"])
        ST["q3_sg"] = (cnt["sgq0"], cnt["sgq1"])
        ST["y_sg"] = (cnt["sgy0"], cnt["sgy1"])
        ST["p4_mm"] = cnt["smm"]

    def p5():
        hoff = 0
        mm_tti = {}
        mm_g = {}
        xused = {}
        slot_free = {}
        eidx = [0]
        for tti in range(NTT5):
            wait("sync", "sgh0", ST["hid_sg"][0])
            wait("sync", "sgh1", ST["hid_sg"][1])
            wait("sync", "sgq0", ST["q3_sg"][0])
            wait("sync", "sgq1", ST["q3_sg"][1])
            wait("sync", "smm", mm_tti.get(tti - 1, ST["p4_mm"]))
            wait("scalar", "sgh0", ST["hid_sg"][0])
            wait("scalar", "sgh1", ST["hid_sg"][1])
            wait("scalar", "sgy0", ST["y_sg"][0])
            wait("scalar", "sgy1", ST["y_sg"][1])
            wait("scalar", "sgq0", ST["q3_sg"][0])
            wait("scalar", "sgq1", ST["q3_sg"][1])
            wait("scalar", "smm", mm_tti.get(tti - 1, ST["p4_mm"]))
            HB = HHN * 512
            op_inc("sync", "sp5", 16,
                   lambda e, tti=tti: e.dma_start(
                       out=big0[:, hoff: hoff + HB // 2],
                       in_=hidsp_d[:, tti * HB: tti * HB + HB // 2]))
            op_inc("scalar", "sscA", 16,
                   lambda e, tti=tti: e.dma_start(
                       out=big0[:, hoff + HB // 2: hoff + HB],
                       in_=hidsp_d[:, tti * HB + HB // 2: (tti + 1) * HB]))
            yl = op_inc("scalar", "sscB", 16,
                        lambda e, tti=tti: e.dma_start(
                            out=y5[:], in_=ysp_d[:, tti * CCN * 512:(tti + 1) * CCN * 512]))
            for dt in range(NDT5):
                g = tti * NDT5 + dt
                slot = g % 2
                qsem = "sp5" if slot == 0 else "sp5b"
                wsem = "sw5" if slot == 0 else "sw5b"
                prev = mm_g.get(g - 2, ST["p4_mm"])
                wait("scalar", "smm", prev)
                ql = op_inc("scalar", qsem, 16,
                            lambda e, dt=dt, slot=slot: e.dma_start(
                                out=qdt2[:, slot * QDTW: slot * QDTW + HHN * DT5],
                                in_=q3sp_d[:, dt * HHN * DT5:(dt + 1) * HHN * DT5]))
                wait("scalar", "swc", 16 * NDT5)
                wl = op_inc("scalar", wsem, 16,
                            lambda e, dt=dt, slot=slot: e.dma_start(
                                out=qdt2[:, slot * QDTW + HHN * DT5: (slot + 1) * QDTW],
                                in_=woutsp_d[:, dt * CCN * DT5:(dt + 1) * CCN * DT5]))
                for tb in range(NTB5):
                    tglob = tti * NTB5 + tb
                    xslot = eidx[0] % 4
                    if eidx[0] >= 4:
                        wait("sync", "sevv", xused[eidx[0] - 4])
                    xlv = op_inc("sync", f"spx{xslot}", 16,
                                 lambda e, tglob=tglob, dt=dt, xslot=xslot: e.dma_start(
                                     out=xt5[:, xslot * 512:(xslot + 1) * 512],
                                     in_=x_d[WIN + tglob * 128: WIN + (tglob + 1) * 128,
                                             dt * DT5:(dt + 1) * DT5]))
                    ST[("xl", eidx[0])] = xlv
                    b = next_bank()
                    wait("tensor", *bank_free[b])
                    wait("tensor", qsem, ql)
                    wait("tensor", "sp5", cnt["sp5"])
                    wait("tensor", "sscA", cnt["sscA"])
                    wait("tensor", "sscB", cnt["sscB"])
                    wait("tensor", wsem, wl)
                    for hh in range(HHN):
                        lhs = big0[:, hoff + hh * TT5 + tb * 128:
                                   hoff + hh * TT5 + (tb + 1) * 128]
                        rhs = qdt2[:, slot * QDTW + hh * DT5: slot * QDTW + (hh + 1) * DT5]
                        emit("tensor", lambda t, lhs=lhs, rhs=rhs, b=b, first=(hh == 0):
                             t.matmul(mmb[b][:], lhs, rhs, start=first, stop=False))
                    for cc in range(CCN - 1):
                        lhs = y5[:, cc * TT5 + tb * 128: cc * TT5 + (tb + 1) * 128]
                        rhs = qdt2[:, slot * QDTW + (HHN + cc) * DT5:
                                   slot * QDTW + (HHN + cc + 1) * DT5]
                        emit("tensor", lambda t, lhs=lhs, rhs=rhs, b=b:
                             t.matmul(mmb[b][:], lhs, rhs, start=False, stop=False))
                    cc = CCN - 1
                    lhs = y5[:, cc * TT5 + tb * 128: cc * TT5 + (tb + 1) * 128]
                    rhs = qdt2[:, slot * QDTW + (HHN + cc) * DT5:
                               slot * QDTW + (HHN + cc + 1) * DT5]
                    gv = op_inc("tensor", "smm", 1,
                                lambda t, lhs=lhs, rhs=rhs, b=b:
                                t.matmul(mmb[b][:], lhs, rhs, start=False, stop=True))
                    mm_g[g] = gv
                    mm_tti[tti] = gv
                    oslot = eidx[0] % 4
                    wait("vector", "smm", gv)
                    wait("vector", f"spx{xslot}", ST[("xl", eidx[0])])
                    if eidx[0] >= 4:
                        wait("vector", f"sst{oslot}", slot_free[eidx[0] - 4])
                    ev = op_inc("vector", "sevv", 1,
                                lambda v, b=b, oslot=oslot, xslot=xslot: v.tensor_add(
                                    out=ob5[:, oslot * 512:(oslot + 1) * 512],
                                    in0=mmb[b][:],
                                    in1=xt5[:, xslot * 512:(xslot + 1) * 512]))
                    bank_free[b] = ("sevv", ev)
                    xused[eidx[0]] = ev
                    wait("gpsimd", "sevv", ev)
                    sv = op_inc("gpsimd", f"sst{oslot}", 16,
                                lambda e, tglob=tglob, dt=dt, oslot=oslot: e.dma_start(
                                    out=out_d[tglob * 128:(tglob + 1) * 128,
                                              dt * DT5:(dt + 1) * DT5],
                                    in_=ob5[:, oslot * 512:(oslot + 1) * 512]))
                    slot_free[eidx[0]] = sv
                    eidx[0] += 1
        for s_ in range(4):
            wait("gpsimd", f"sst{s_}", cnt[f"sst{s_}"])

    import os
    DBG = int(os.environ.get("ATHENA_DBG_PHASES", "9"))
    make_gamma_units()
    make_q3_units()
    p0()
    if DBG >= 1:
        p1()
    if DBG >= 2:
        p3()
    if DBG >= 3:
        gamma_finish()
    if DBG >= 4:
        p4()
    if DBG >= 5:
        p5()
    if DBG < 5:
        for n in ("sevv", "sevs", "smm", "str", "ss", "sgy0", "sgy1",
                  "sgh0", "sgh1", "sgq0", "sgq1"):
            wait("gpsimd", n, cnt[n])

    with nc.Block() as block:
        @block.sync
        def _(e):
            for f in OPS["sync"]:
                f(e)

        @block.scalar
        def _(e):
            for f in OPS["scalar"]:
                f(e)

        @block.vector
        def _(e):
            for f in OPS["vector"]:
                f(e)

        @block.tensor
        def _(e):
            for f in OPS["tensor"]:
                f(e)

        @block.gpsimd
        def _(e):
            for f in OPS["gpsimd"]:
                f(e)

    es.close()
    return nc


def _prep_inputs(L0_attn_x, L0_attn_W_rvk, L0_attn_W_out, L0_attn_time_decay,
                 L0_attn_time_first, L0_ffn_w1, L0_ffn_w2, L0_ffn_w3):
    x = np.ascontiguousarray(np.asarray(L0_attn_x, dtype=np.float32))
    W_rvk = np.asarray(L0_attn_W_rvk, dtype=np.float32)
    W_out = np.asarray(L0_attn_W_out, dtype=np.float32)
    td = np.asarray(L0_attn_time_decay, dtype=np.float32)
    tf = np.asarray(L0_attn_time_first, dtype=np.float32)
    w1 = np.asarray(L0_ffn_w1, dtype=np.float32)
    w2 = np.asarray(L0_ffn_w2, dtype=np.float32)
    w3 = np.asarray(L0_ffn_w3, dtype=np.float32)

    WT = W_rvk.T
    wrvk = np.empty((CCN, 128, 3, CCN, 128), np.float32)
    for j, base in ((0, 2 * C), (1, C), (2, 0)):
        blk = WT[:, base:base + C].reshape(CCN, 128, CCN, 128)
        wrvk[:, :, j, :, :] = blk.transpose(2, 1, 0, 3)
    wrvk = np.ascontiguousarray(wrvk.reshape(CCN, 128, 3 * CCN * 128))

    w12 = np.empty((HHN, 2, 128, C), np.float32)
    for m, w in ((0, w1), (1, w2)):
        t = w.reshape(HHN, 128, CCN, 128)
        w12[:, m, :, :] = t.transpose(0, 3, 2, 1).reshape(HHN, 128, C)
    w12 = np.ascontiguousarray(w12)

    w3t = np.ascontiguousarray(w3.reshape(NDT5, DT5, HHN, 128).transpose(0, 2, 3, 1))

    wo = W_out.reshape(NDT5, DT5, CCN, 128).transpose(0, 3, 2, 1)
    wout = np.ascontiguousarray(wo.reshape(NDT5, 128, CCN * DT5))

    tdt = np.ascontiguousarray(td.reshape(CCN, 128).T)
    tft = np.ascontiguousarray(tf.reshape(CCN, 128).T)
    eye = np.eye(128, dtype=ml_dtypes.bfloat16)

    in_maps = []
    for core in range(8):
        b, h = core // 2, core % 2
        if h == 0:
            xs = np.concatenate([x[b, 0:WIN], x[b, 0:T_OWN]], axis=0)
            mask = np.zeros((128, 1), np.float32)
        else:
            xs = x[b, T_OWN - WIN: 2 * T_OWN]
            mask = np.ones((128, 1), np.float32)
        in_maps.append({
            "x": np.ascontiguousarray(xs),
            "wrvk": wrvk, "w12": w12, "w3t": w3t, "wout": wout,
            "td": tdt, "tf": tft, "mask": mask, "eye": eye,
        })
    return in_maps


def kernel(L0_attn_x, L0_attn_W_rvk, L0_attn_W_out, L0_attn_time_decay,
           L0_attn_time_first, L0_ffn_w1, L0_ffn_w2, L0_ffn_w3):
    in_maps = _prep_inputs(L0_attn_x, L0_attn_W_rvk, L0_attn_W_out,
                           L0_attn_time_decay, L0_attn_time_first,
                           L0_ffn_w1, L0_ffn_w2, L0_ffn_w3)
    nc = build_nc()
    res = run_bass_kernel_spmd(nc, in_maps, core_ids=list(range(8)))
    B, T = 4, 4096
    out = np.empty((B, T, C), np.float32)
    for core in range(8):
        b, h = core // 2, core % 2
        out[b, h * T_OWN:(h + 1) * T_OWN, :] = res.results[core]["out"]
    return out
